# version 05
# baseline (speedup 1.0000x reference)
import numpy as np
from contextlib import ExitStack
import concourse.bass as bass
import concourse.mybir as mybir
from concourse.bass_utils import run_bass_kernel_spmd

F32 = mybir.dt.float32
AF = mybir.ActivationFunctionType
ALU = mybir.AluOpType
AX = mybir.AxisListType.X

T = 2048
D = 1024
NT = 16
MASK_VAL = -1.0e8
NEG_BIG = -1.0e9

FP4_MID = [0.25, 0.75, 1.25, 1.75, 2.5, 3.5, 5.0]
FP4_WGT = [0.5, 0.5, 0.5, 0.5, 1.0, 1.0, 2.0]


class B:
    ENGS = ("tensor", "vector", "scalar", "sync", "gpsimd")

    def __init__(self, nc, ctx):
        self.nc = nc
        self.ctx = ctx
        self.q = {e: [] for e in self.ENGS}
        self.n = {"pe": 0, "ve": 0, "se": 0}
        self.dn = {}
        self.sems = {}
        self.pend = {e: {} for e in self.ENGS}

    def get_sem(self, name):
        if name not in self.sems:
            self.sems[name] = self.ctx.enter_context(self.nc.semaphore(name))
        return self.sems[name]

    def _flush_pend(self, eng, keep_one):
        pend = self.pend[eng]
        emb = None
        if pend:
            items = sorted(pend.items())
            if keep_one:
                emb = items.pop()
            for ch, v in items:
                self.q[eng].append(lambda e, ch=ch, v=v: e.wait_ge(self.sems[ch], v))
            self.pend[eng] = {}
        return emb

    def emit(self, eng, fn, inc):
        emb = self._flush_pend(eng, keep_one=(eng in ("tensor", "vector", "scalar")))
        if inc is not None:
            self.n[inc] += 1
            self.get_sem(inc)
        def thunk(e, fn=fn, inc=inc, emb=emb):
            ins = fn(e)
            if emb is not None:
                ins._wait_ge(self.sems[emb[0]], emb[1])
            if inc is not None:
                ins.then_inc(self.sems[inc], 1)
        self.q[eng].append(thunk)

    def wait(self, eng, ch, val=None):
        v = self.n[ch] if val is None else val
        self.get_sem(ch)
        p = self.pend[eng]
        if p.get(ch, -1) < v:
            p[ch] = v

    def dma(self, queue, out, in_, sem):
        self._flush_pend(queue, keep_one=False)
        self.dn[sem] = self.dn.get(sem, 0) + 16
        self.get_sem(sem)
        def thunk(e, out=out, in_=in_, sem=sem):
            e.dma_start(out=out, in_=in_).then_inc(self.sems[sem], 16)
        self.q[queue].append(thunk)

    def wait_dma(self, eng, sem, val=None):
        v = self.dn[sem] if val is None else val
        self.get_sem(sem)
        p = self.pend[eng]
        if p.get(sem, -1) < v:
            p[sem] = v

    def barrier(self):
        vals = dict(self.n)
        dvals = dict(self.dn)
        for eng in self.ENGS:
            for ch, v in vals.items():
                if v > 0:
                    self.wait(eng, ch, v)
            for sm, v in dvals.items():
                if v > 0:
                    self.wait_dma(eng, sm, v)

    def _drain(self, eng):
        ch = {"vector": "ve", "scalar": "se", "tensor": "pe"}[eng]
        if self.n[ch] > 0:
            self.wait(eng, ch)

    def mm(self, out, lhsT, rhs, start, stop):
        self.emit("tensor", lambda e: e.matmul(out, lhsT, rhs, start=start, stop=stop), "pe")

    def tp(self, out, in_, ident):
        self.emit("tensor", lambda e: e.transpose(out, in_, ident), "pe")

    def cp(self, out, in_, eng="vector", drain=True):
        if drain:
            self._drain(eng)
        ch = "ve" if eng == "vector" else "se"
        if eng == "vector":
            self.emit(eng, lambda e: e.tensor_copy(out, in_), ch)
        else:
            self.emit(eng, lambda e: e.copy(out, in_), ch)

    def tt(self, out, in0, in1, op, drain=True):
        if drain:
            self._drain("vector")
        self.emit("vector", lambda e: e.tensor_tensor(out, in0, in1, op), "ve")

    def ts(self, out, in0, s1, s2, op0, op1=None, accum=None, drain=True):
        if drain:
            self._drain("vector")
        if op1 is None:
            self.emit("vector", lambda e: e.tensor_scalar(out, in0, s1, None, op0, accum_out=accum), "ve")
        else:
            self.emit("vector", lambda e: e.tensor_scalar(out, in0, s1, s2, op0, op1, accum_out=accum), "ve")

    def stt(self, out, in0, scalar, in1, op0, op1, accum=None, drain=True):
        if drain:
            self._drain("vector")
        self.emit("vector", lambda e: e.scalar_tensor_tensor(
            out=out, in0=in0, scalar=scalar, in1=in1, op0=op0, op1=op1, accum_out=accum), "ve")

    def red(self, out, in_, op="max", negate=False, absval=False, drain=True):
        if drain:
            self._drain("vector")
        if op == "max":
            self.emit("vector", lambda e: e.reduce_max(out, in_, axis=AX, negate=negate,
                                                       apply_absolute_value=absval), "ve")
        else:
            self.emit("vector", lambda e: e.reduce_sum(out, in_, axis=AX, negate=negate,
                                                       apply_absolute_value=absval), "ve")

    def recip(self, out, in_, drain=True):
        if drain:
            self._drain("vector")
        self.emit("vector", lambda e: e.reciprocal(out, in_), "ve")

    def memset(self, ap, val, drain=False):
        if drain:
            self._drain("vector")
        self.emit("vector", lambda e: e.memset(ap, val), "ve")

    def act(self, out, in_, func, bias=0.0, scale=1.0, accum=None, drain=True):
        if drain:
            self._drain("scalar")
        self.emit("scalar", lambda e: e.activation(out, in_, func, bias=bias, scale=scale,
                                                   accum_out=accum), "se")

    def smul(self, out, in_, mul, drain=True):
        if drain:
            self._drain("scalar")
        self.emit("scalar", lambda e: e.mul(out, in_, mul), "se")



def vgaps(b, C):
    b.wait("vector", "ve")


def sgaps(b, C):
    b.wait("scalar", "se")


def tp_batch(b, C, ps, items, banks=(0, 1, 2, 3)):
    ident = C["ident"][:]
    nb = len(banks)
    ve_cp = {}
    b.wait("tensor", "ve")
    for k, (src, dst) in enumerate(items):
        slot = ps[banks[k % nb]][:, (k // nb % 4) * 128:(k // nb % 4) * 128 + 128]
        if k >= nb:
            b.wait("tensor", "ve", ve_cp[k - nb])
        b.tp(slot, src, ident)
        b.wait("vector", "pe")
        b.cp(dst, slot, drain=False)
        ve_cp[k] = b.n["ve"]


def bcast_col(b, C, ps, col_ap, dst_tile, bank=7):
    ident = C["ident"][:]
    b.wait("tensor", "ve")
    b.tp(ps[bank][0:1, 0:128], col_ap, ident)
    b.wait("vector", "pe")
    b.cp(C["rowb"][0:1, :], ps[bank][0:1, 0:128], drain=False)
    b.wait("tensor", "ve")
    b.mm(ps[bank][:, 128:256], C["ones_r"][:], C["rowb"][0:1, :], start=True, stop=True)
    b.wait("vector", "pe")
    b.cp(dst_tile, ps[bank][:, 128:256], drain=False)


def rstd_chain(b, C, v_col, out_col, n_feats, eps, extra_scale=None, is_sum=False):
    t1, t2 = C["st_t1"], C["st_t2"]
    w = v_col.shape[1]
    if is_sum:
        b.ts(v_col, v_col, 1.0, eps, ALU.mult, ALU.add)
    else:
        b.ts(v_col, v_col, 1.0 / n_feats, eps, ALU.mult, ALU.add)
    b.wait("scalar", "ve")
    b.act(t1[:, 0:w], v_col, AF.Sqrt)
    b.wait("vector", "se")
    b.recip(out_col, t1[:, 0:w])
    vgaps(b, C)
    for _ in range(2):
        b.tt(t1[:, 0:w], out_col, out_col, ALU.mult)
        vgaps(b, C)
        b.tt(t1[:, 0:w], t1[:, 0:w], v_col, ALU.mult)
        vgaps(b, C)
        b.ts(t1[:, 0:w], t1[:, 0:w], -0.5, 1.5, ALU.mult, ALU.add)
        vgaps(b, C)
        b.tt(out_col, out_col, t1[:, 0:w], ALU.mult)
        vgaps(b, C)
    if extra_scale is not None:
        b.ts(out_col, out_col, extra_scale, None, ALU.mult)
        vgaps(b, C)


def fp4_quant(b, C, x_ap, out_ap, width, wrk):
    G = width // 16
    absx, cmp = wrk["absx"], wrk["cmp"]
    mx, scale = wrk["mx"], wrk["scale"]
    b.wait("scalar", "ve")
    b.act(absx, x_ap, AF.Abs)
    b.wait("vector", "se")
    xv = absx.rearrange("p (g c) -> p g c", c=16)
    b.red(mx, xv, op="max")
    b.ts(scale, mx, 1.0 / 6.0, 1e-12, ALU.mult, ALU.max)
    rsc = wrk["thr"]
    b.recip(rsc, scale)
    rs_b = rsc.unsqueeze(2).broadcast_to([128, G, 16])
    sc_b = scale.unsqueeze(2).broadcast_to([128, G, 16])
    b.tt(xv, xv, rs_b, ALU.mult)
    b.memset(out_ap, 0.0)
    for j in range(7):
        b.ts(cmp, absx, FP4_MID[j], None, ALU.is_gt)
        b.stt(out_ap, cmp, FP4_WGT[j], out_ap, ALU.mult, ALU.add)
    b.wait("scalar", "ve")
    b.act(cmp, x_ap, AF.Sign)
    b.wait("vector", "se")
    b.tt(out_ap, out_ap, cmp, ALU.mult)
    ov = out_ap.rearrange("p (g c) -> p g c", c=16)
    b.tt(ov, ov, sc_b, ALU.mult)


def _build(upto="out", s1_limit=None, s2_limit=None, start="x0", s3_limit=None):
    nc = bass.Bass("TRN2", target_bir_lowering=False)
    b = B(None, None)

    dr = {}
    def din(name, shape):
        dr[name] = nc.dram_tensor(name, shape, F32, kind="ExternalInput").ap()
        return dr[name]
    def dint(name, shape):
        dr[name] = nc.dram_tensor(name, shape, F32, kind="Internal").ap()
        return dr[name]

    x0_d = din("x0", [T, D])
    xin_d = nc.dram_tensor("xin", [T, D], F32, kind="ExternalInput").ap() if start != "x0" else None
    din("l0_woT", [D, D])
    din("l0_wg", [2816, D]); din("l0_wu", [2816, D])
    din("l1_wq", [D, 1024]); din("l1_wk", [D, 1024]); din("l1_wv", [D, 2048])
    din("l1_wg", [D, 2048]); din("l1_wo", [2048, D]); din("l1_w2", [D, D])
    din("l1_wab", [D, 8])
    din("l1_convq", [128, 8, 4]); din("l1_convk", [128, 8, 4]); din("l1_convv", [128, 16, 4])
    din("l1_dtb", [128, 4]); din("l1_alog", [128, 4])
    din("bc_ln1g", [128, D]); din("bc_ln1b", [128, D]); din("bc_ln2g", [128, D])
    din("bc_ln2b", [128, D]); din("bc_rmsin", [128, D]); din("bc_b2", [128, D])
    din("bc_moerms", [128, D]); din("bc_onorm", [128, 512])
    din("moe_gwT", [128, 8, 8])
    din("moe_w1i", [8, 8, 128, 3072]); din("moe_w2i", [8, 8, 128, 3072])
    din("moe_w3i", [8, 8, 128, 3072])
    din("l0_pairs", [8, 128, 3072]); din("l0_wdi", [11, 128, 2048])
    din("ones_col", [128, 1]); din("iota_f", [128, 128]); din("iota_p", [128, 128])
    din("e127", [128, 1])
    din("ident", [128, 128]); din("mask_causal", [128, 128])
    din("mask_ls", [128, 128]); din("mask_us", [128, 128]); din("mask_ui", [128, 128])
    din("ones_ui", [128, 128]); din("ones_r", [1, 128])

    out_d = nc.dram_tensor("out", [T, D], F32, kind="ExternalOutput").ap()

    scr_x1qT = dint("scr_x1qT", [NT, 128, 1024])
    scr_x2 = dint("scr_x2", [T, D])
    scr_qT = dint("scr_qT", [NT, 128, 1024])
    scr_kT = dint("scr_kT", [NT, 128, 1024])
    scr_vhat = dint("scr_vhat", [T, 2048])
    scr_sgate = dint("scr_sgate", [T, 2048])
    scr_osc = dint("scr_osc", [T, 2048])
    scr_x3 = dint("scr_x3", [T, D])

    with ExitStack() as ctx:
        def sb(name, shape, dt=F32):
            return ctx.enter_context(nc.sbuf_tensor(name, shape, dt))

        A1 = sb("A1", [128, 16384])
        A2 = sb("A2", [128, 16384])
        WA = sb("WA", [128, 9216])
        WK = sb("WK", [128, 8704])
        C = {}
        C["ident"] = sb("c_ident", [128, 128])
        C["mask_causal"] = sb("c_mcz", [128, 128])
        C["mask_ls"] = sb("c_mls", [128, 128])
        C["mask_us"] = sb("c_mus", [128, 128])
        C["mask_ui"] = sb("c_mui", [128, 128])
        C["ones_ui"] = sb("c_oui", [128, 128])
        C["ones_r"] = sb("c_or", [1, 128])
        C["junk1"] = sb("c_j1", [128, 4])
        C["junk2"] = sb("c_j2", [128, 4])
        C["junkS1"] = sb("c_js1", [128, 4])
        C["junkS2"] = sb("c_js2", [128, 4])
        stats = sb("c_stats", [128, 96])
        C["stats"] = stats
        C["st_t1"] = stats[:, 64:80]
        C["st_t2"] = stats[:, 80:96]
        C["gb"] = sb("c_gb", [128, 16, 8])
        C["wfull"] = sb("c_wfull", [128, 16, 8])
        C["gwT"] = sb("c_gwT", [128, 8, 8])
        C["convq"] = sb("c_convq", [128, 8, 4])
        C["convk"] = sb("c_convk", [128, 8, 4])
        C["convv"] = sb("c_convv", [128, 16, 4])
        C["coef"] = sb("c_coef", [128, 4])
        C["dtb"] = sb("c_dtb", [128, 4])
        C["ones_col"] = sb("c_onescol", [128, 1])
        C["rowb"] = sb("c_rowb", [1, 128])
        C["iota_f"] = sb("c_iotaf", [128, 128])
        C["iota_p"] = sb("c_iotap", [128, 128])
        C["e127"] = sb("c_e127", [128, 1])
        C["rank"] = sb("c_rank", [128, 16, 8])
        C["offrow"] = sb("c_offrow", [1, 8])
        C["wab"] = sb("c_wab", [128, 8, 8])

        ps = [ctx.enter_context(nc.psum_tensor(f"ps{i}", [128, 512], F32)) for i in range(8)]

        b.nc, b.ctx = nc, ctx
        blk = ctx.enter_context(nc.Block())

        ident = C["ident"]

        for nm in ("ident", "mask_causal", "mask_ls", "mask_us", "mask_ui", "ones_ui"):
            b.dma("sync", C[nm][:], dr[nm], "c0")
        b.dma("sync", C["ones_r"][:], dr["ones_r"], "c0")
        b.dma("sync", C["gwT"][:], dr["moe_gwT"], "c0")
        b.dma("sync", C["convq"][:], dr["l1_convq"], "c0")
        b.dma("sync", C["convk"][:], dr["l1_convk"], "c0")
        b.dma("sync", C["convv"][:], dr["l1_convv"], "c0")
        b.dma("sync", C["dtb"][:], dr["l1_dtb"], "c0")
        b.dma("sync", C["coef"][:], dr["l1_alog"], "c0")
        b.dma("sync", C["ones_col"][:], dr["ones_col"], "c0")
        b.dma("sync", C["iota_f"][:], dr["iota_f"], "c0")
        b.dma("sync", C["iota_p"][:], dr["iota_p"], "c0")
        b.dma("sync", C["e127"][:], dr["e127"], "c0")
        b.dma("sync", C["wab"][:], dr["l1_wab"][:].rearrange("(c p) f -> p c f", p=128), "c0")
        b.wait_dma("vector", "c0")
        b.memset(C["junk1"][:], 0.0)
        b.memset(C["junkS1"][:], 0.0)
        b.wait_dma("scalar", "c0")
        b.act(C["coef"][:], C["coef"][:], AF.Exp, scale=1.0)
        b.wait("vector", "se")
        b.ts(C["coef"][:], C["coef"][:], -1.0, None, ALU.mult)
        b.barrier()

        x0T = A1[:].rearrange("p (c t) -> p c t", c=8)
        stream = A1[:].rearrange("p (c t) -> p c t", c=16)
        aoT = A2[:].rearrange("p (c t) -> p c t", c=8)
        A2_16 = A2[:].rearrange("p (c t) -> p c t", c=16)

        if start == "x0":
            emit_s1(nc, b, C, dr, ps, A1, A2, WA, WK, x0T, stream, aoT, x0_d, limit=s1_limit)
        else:
            xld = WK[:, 0:2048].rearrange("p (c f) -> p c f", c=2)
            for i in range(NT):
                b.dma("sync", xld[:, i % 2, :], xin_d[i * 128:(i + 1) * 128, :], f"xi{i % 2}")
                b.wait_dma("vector", f"xi{i % 2}")
                b.cp(stream[:, i, :], xld[:, i % 2, :], drain=False)
                b.dma("gpsimd", scr_x2[i * 128:(i + 1) * 128, :], xin_d[i * 128:(i + 1) * 128, :], "x2w")
                if i < NT - 2:
                    b.wait("sync", "ve")
            b.barrier()

        if upto == "x1":
            finish(b, stream, out_d)
            run_block(b, blk)
            return nc, b

        if start == "x0":
            emit_s2(nc, b, C, dr, ps, A1, A2, WA, WK, stream, A2_16, scr_x1qT, scr_x2, limit=s2_limit)

        if upto == "x2":
            finish(b, stream, out_d)
            run_block(b, blk)
            return nc, b

        if start in ("x0", "x2"):
            emit_s3(nc, b, C, dr, ps, A1, A2, WA, WK, stream, A2_16,
                    scr_x2, scr_qT, scr_kT, scr_vhat, scr_sgate, scr_osc, limit=s3_limit)

        if upto == "x3":
            finish(b, stream, out_d)
            run_block(b, blk)
            return nc, b

        emit_s4(nc, b, C, dr, ps, A1, A2, WA, WK, stream, A2_16, scr_x3)

        finish(b, stream, out_d)
        run_block(b, blk)
    return nc, b


def finish(b, stream, out_d):
    b.barrier()
    for i in range(NT):
        b.dma("gpsimd", out_d[i * 128:(i + 1) * 128, :], stream[:, i, :], "outw")
    b.wait_dma("gpsimd", "outw")


def run_block(b, blk):
    for eng in B.ENGS:
        b._flush_pend(eng, keep_one=False)
    for eng in B.ENGS:
        thunks = b.q[eng]
        getattr(blk, eng)(lambda e, thunks=thunks: [t(e) for t in thunks])


def emit_s1(nc, b, C, dr, ps, A1, A2, WA, WK, x0T, stream, aoT, x0_d, limit=None):
    ident = C["ident"][:]
    Vp = WK[:, 0:2048].rearrange("p (c t) -> p c t", c=16)
    srow = WK[:, 2048:4096]
    prow = WK[:, 4096:6144]
    ptslab = WK[:, 6144:8192].rearrange("p (c t) -> p c t", c=16)
    avo2 = WK[:, 8192:8320]
    xtile = WK[:, 0:2048].rearrange("p (c t) -> p c t", c=2)
    wqb = WA[:, 0:1024].rearrange("p (c t) -> p c t", c=8)
    wkb = WA[:, 1024:2048].rearrange("p (c t) -> p c t", c=8)
    wvb = WA[:, 2048:3072].rearrange("p (c t) -> p c t", c=8)
    QT = WA[:, 3072:5120]
    KT = WA[:, 5120:7168]
    woT = WA[:, 0:8192].rearrange("p (c t) -> p c t", c=8)

    ve_tile = {}
    for i in range(NT):
        if i >= 2:
            b.wait("sync", "ve", ve_tile[i - 2])
        b.dma("sync", xtile[:, i % 2, :], x0_d[i * 128:(i + 1) * 128, :], f"xt{i % 2}")
        b.wait_dma("tensor", f"xt{i % 2}")
        if i >= 2:
            b.wait("tensor", "ve", ve_tile[i - 2])
        bk = 2 * (i % 2)
        for c in range(8):
            slot = ps[bk + c // 4][:, (c % 4) * 128:(c % 4) * 128 + 128]
            b.tp(slot, xtile[:, i % 2, c * 128:(c + 1) * 128], ident)
        b.wait("vector", "pe")
        for c in range(8):
            slot = ps[bk + c // 4][:, (c % 4) * 128:(c % 4) * 128 + 128]
            b.cp(x0T[:, c, i * 128:(i + 1) * 128], slot, drain=False)
        ve_tile[i] = b.n["ve"]
    b.barrier()
    if limit == "p0":
        return

    for p in range(8):
        if p > 0:
            b.wait("sync", "pe")
        b.dma("sync", WA[:, 0:3072].rearrange("p (c f) -> p c f", c=24),
              dr["l0_pairs"][p].rearrange("p (c f) -> p c f", c=24), "wpair")
        b.wait_dma("tensor", "wpair")

        for dst, wb, psl in ((QT, wqb, (0, 1)), (KT, wkb, (2, 3))):
            for n in range(4):
                pp = ps[psl[n % 2]]
                if n >= 2:
                    b.wait("tensor", "ve")
                for c in range(8):
                    b.mm(pp[:, 0:512], wb[:, c, :], x0T[:, c, n * 512:(n + 1) * 512],
                         start=(c == 0), stop=(c == 7))
                b.wait("vector", "pe")
                b.cp(dst[:, n * 512:(n + 1) * 512], pp[:, 0:512])
        if limit == "pqk" and p == 0:
            b.barrier()
            return
        for i in range(NT):
            pp = ps[4 + (i % 2)]
            if i >= 2:
                b.wait("tensor", "ve")
            for c in range(8):
                b.mm(pp[:, 0:128], x0T[:, c, i * 128:(i + 1) * 128], wvb[:, c, :],
                     start=(c == 0), stop=(c == 7))
            b.wait("vector", "pe")
            b.cp(Vp[:, i, :], pp[:, 0:128])
        b.barrier()
        if limit == "pv" and p == 0:
            return

        for i in range(NT):
            L = (i + 1) * 128
            for hh in range(2):
                hsl = slice(hh * 64, hh * 64 + 64)
                b.wait("tensor", "se")
                b.wait("tensor", "ve")
                nkg = (L + 511) // 512
                for kg in range(nkg):
                    nk = min(512, L - kg * 512)
                    pp = ps[6 + (kg % 2)]
                    if kg >= 2:
                        b.wait("tensor", "ve")
                    b.mm(pp[:, 0:nk], QT[hsl, i * 128:(i + 1) * 128],
                         KT[hsl, kg * 512:kg * 512 + nk], start=True, stop=True)
                    b.wait("vector", "pe")
                    b.ts(srow[:, kg * 512:kg * 512 + nk], pp[:, 0:nk], 0.125, None, ALU.mult)
                b.tt(srow[:, i * 128:L], srow[:, i * 128:L], C["mask_causal"][:], ALU.add)
                nmx = C["stats"][:, 0:1]
                ssum = C["stats"][:, 1:2]
                rsum = C["stats"][:, 2:3]
                b.red(nmx, srow[:, 0:L], op="max", negate=True)
                b.wait("scalar", "ve")
                b.act(prow[:, 0:L], srow[:, 0:L], AF.Exp, bias=nmx, accum=ssum)
                b.wait("vector", "se")
                b.recip(rsum, ssum)
                if limit == f"pa{i}s" and p == 0 and hh == 0:
                    b.barrier()
                    return
                b.wait("tensor", "se")
                ve_copy = {}
                for j in range(i + 1):
                    slot = ps[j % 4][:, (j // 4) * 128:(j // 4) * 128 + 128]
                    if j >= 4:
                        b.wait("tensor", "ve", ve_copy[j - 4])
                    b.tp(slot, prow[:, j * 128:(j + 1) * 128], ident)
                    b.wait("vector", "pe")
                    b.cp(ptslab[:, j, :], slot, drain=False)
                    ve_copy[j] = b.n["ve"]
                b.wait("tensor", "ve")
                if limit == f"pa{i}t" and p == 0 and hh == 0:
                    b.barrier()
                    return
                for j in range(i + 1):
                    b.mm(ps[6][:, 0:64], ptslab[:, j, :], Vp[:, j, hsl],
                         start=(j == 0), stop=(j == i))
                b.wait("scalar", "pe")
                b.smul(avo2[:, hh * 64:hh * 64 + 64], ps[6][:, 0:64], rsum)
                if limit == f"pa{i}v" and p == 0 and hh == 0:
                    b.barrier()
                    return
            b.wait("tensor", "se")
            b.wait("tensor", "ve")
            b.tp(ps[7][:, 0:128], avo2[:], ident)
            b.wait("vector", "pe")
            b.cp(aoT[:, p, i * 128:(i + 1) * 128], ps[7][:, 0:128])
            if limit is not None and limit.startswith("pa") and limit[2:].isdigit() and p == 0 and i == int(limit[2:]):
                b.barrier()
                return
        if limit == "p1one" and p == 0:
            b.barrier()
            return
    b.barrier()
    if limit == "p1":
        return

    b.dma("sync", woT[:], dr["l0_woT"][:].rearrange("(c p2) f -> p2 c f", p2=128), "wbig")
    ztmp = WK[:, 0:1024]
    xt0 = WK[:, 1024:3072].rearrange("p (c t) -> p c t", c=2)
    sqd = WK[:, 3072:4096]
    b.wait_dma("tensor", "wbig")
    ve_chunk = {}
    for i in range(NT):
        if i >= 2:
            b.wait("sync", "ve", ve_chunk[i - 2])
        b.dma("sync", xt0[:, i % 2, :], x0_d[i * 128:(i + 1) * 128, :], f"xt{i % 2}")
        if i >= 1:
            b.wait("tensor", "ve")
        for mh in range(2):
            pp = ps[mh]
            for c in range(8):
                b.mm(pp[:, 0:512], aoT[:, c, i * 128:(i + 1) * 128],
                     woT[:, c, mh * 512:(mh + 1) * 512], start=(c == 0), stop=(c == 7))
        b.wait("vector", "pe")
        b.wait_dma("vector", f"xt{i % 2}")
        b.tt(stream[:, i, 0:512], ps[0][:, 0:512], xt0[:, i % 2, 0:512], ALU.add)
        b.tt(stream[:, i, 512:1024], ps[1][:, 0:512], xt0[:, i % 2, 512:1024], ALU.add)
        b.red(C["stats"][:, i:i + 1], stream[:, i, :], op="sum")
        ve_chunk[i] = b.n["ve"]
    postln_finish(b, C, stream, ztmp, sqd, eps=1e-5)


def postln_finish(b, C, stream, ztmp, sqd, eps):
    stats = C["stats"]
    zsum, ssq = stats[:, 0:16], stats[:, 16:32]
    nmean, rstd = stats[:, 32:48], stats[:, 48:64]
    vgaps(b, C)
    b.ts(nmean, zsum, -1.0 / 1024.0, None, ALU.mult)
    b.wait("scalar", "ve")
    for i in range(NT):
        b.act(ztmp, stream[:, i, :], AF.Identity, bias=nmean[:, i:i + 1])
        b.cp(stream[:, i, :], ztmp, eng="scalar")
        b.wait("vector", "se")
        b.stt(sqd, stream[:, i, :], 1.0, stream[:, i, :], ALU.mult, ALU.mult,
              accum=ssq[:, i:i + 1])
    vgaps(b, C)
    rstd_chain(b, C, ssq, rstd, 1024, eps)
    b.wait("scalar", "ve")
    for i in range(NT):
        b.act(ztmp, stream[:, i, :], AF.Identity, scale=rstd[:, i:i + 1])
        b.cp(stream[:, i, :], ztmp, eng="scalar")
    b.barrier()


def emit_s2(nc, b, C, dr, ps, A1, A2, WA, WK, stream, A2_16, scr_x1qT, scr_x2, limit=None):
    ident = C["ident"][:]
    NB, NW = 11, 256

    xq = WK[:, 0:1024]
    wrk = {"absx": WK[:, 1024:2048], "cmp": WK[:, 2048:3072],
           "mx": WK[:, 3072:3136], "scale": WK[:, 3136:3200], "thr": WK[:, 3200:3264]}
    xqTb = WK[:, 3264:5312].rearrange("p (s c f) -> p s c f", s=2, c=8)
    ve_done = {}
    for i in range(NT):
        b.wait("vector", "pe")
        fp4_quant(b, C, stream[:, i, :], xq, 1024, wrk)
        bk = 2 * (i % 2)
        if i >= 2:
            b.wait("tensor", "ve", ve_done[i - 2])
        b.wait("tensor", "ve")
        for c in range(8):
            slot = ps[bk + c // 4][:, (c % 4) * 128:(c % 4) * 128 + 128]
            b.tp(slot, xq[:, c * 128:(c + 1) * 128], ident)
        b.wait("vector", "pe")
        if i >= 2:
            b.wait_dma("vector", f"xqw{i % 2}")
        for c in range(8):
            slot = ps[bk + c // 4][:, (c % 4) * 128:(c % 4) * 128 + 128]
            b.cp(xqTb[:, i % 2, c, :], slot, drain=False)
        ve_done[i] = b.n["ve"]
        b.wait("gpsimd", "ve", ve_done[i])
        b.dma("gpsimd", scr_x1qT[i].rearrange("p (c f) -> p c f", c=8),
              xqTb[:, i % 2, :, :], f"xqw{i % 2}")
    b.barrier()
    if limit == "q0":
        return

    for i in range(NT):
        b.memset(A2_16[:, i, :], 0.0)
    b.barrier()

    wrows = WK[:, 0:2048]
    wqo = WK[:, 2048:4096]
    wrkw = {"absx": WK[:, 4096:6144], "cmp": WK[:, 6144:8192],
            "mx": WK[:, 8192:8320], "scale": WK[:, 8320:8448], "thr": WK[:, 8448:8576]}
    wgqT = WA[:, 0:2048].rearrange("p (c f) -> p c f", c=8)
    wuqT = WA[:, 2048:4096].rearrange("p (c f) -> p c f", c=8)
    wdqT = WA[:, 4096:6144].rearrange("p (c f) -> p c f", c=2)
    for nb in range(NB):
        n0 = nb * NW
        for wi, (src, dstT) in enumerate(
                ((dr["l0_wg"], wgqT), (dr["l0_wu"], wuqT), (None, wdqT))):
            b.wait("sync", "ve")
            b.wait("sync", "se")
            if wi < 2:
                b.dma("sync", wrows.rearrange("p (c f) -> p c f", c=2),
                      src[n0:n0 + NW, :].rearrange("(c p) f -> p c f", p=128), "wrow")
            else:
                b.dma("sync", wrows.rearrange("p (c f) -> p c f", c=8),
                      dr["l0_wdi"][nb].rearrange("p (c f) -> p c f", c=8), "wrow")
            b.wait_dma("vector", "wrow")
            b.wait_dma("scalar", "wrow")
            b.wait("vector", "pe")
            fp4_quant(b, C, wrows, wqo, 2048, wrkw)
            b.wait("tensor", "ve")
            if wi < 2:
                items = [(wqo[:, r * 1024 + c * 128: r * 1024 + (c + 1) * 128],
                          dstT[:, c, r * 128:(r + 1) * 128])
                         for r in range(2) for c in range(8)]
            else:
                items = [(wqo[:, c * 256 + nn * 128: c * 256 + (nn + 1) * 128],
                          dstT[:, nn, c * 128:(c + 1) * 128])
                         for c in range(8) for nn in range(2)]
            tp_batch(b, C, ps, items)
        b.barrier()
        if limit is not None and limit == f"w{nb}":
            return

        xqTs = WK[:, 0:2048].rearrange("p (s c f) -> p s c f", s=2, c=8)
        hb = WK[:, 2048:2304]
        tmps = WK[:, 2304:2560]
        hq = WK[:, 2560:2816]
        hqT = WK[:, 2816:3072].rearrange("p (c f) -> p c f", c=2)
        wrkh = {"absx": WK[:, 3072:3328], "cmp": WK[:, 3328:3584],
                "mx": WK[:, 3584:3600], "scale": WK[:, 3600:3616], "thr": WK[:, 3616:3632]}
        pe_mm, se_silu, ve_tt, ve_q, ve_add = {}, {}, {}, {}, {}

        def stage_a(i):
            gb, ub = (0, 1) if i % 2 == 0 else (5, 6)
            if i >= 2:
                b.wait("sync", "pe", pe_mm[i - 2])
            b.dma("sync", xqTs[:, i % 2, :, :],
                  scr_x1qT[i].rearrange("p (c f) -> p c f", c=8), f"xqr{i % 2}")
            b.wait_dma("tensor", f"xqr{i % 2}")
            if i >= 2:
                b.wait("tensor", "se", se_silu[i - 2])
                b.wait("tensor", "ve", ve_tt[i - 2])
            for c in range(8):
                b.mm(ps[gb][:, 0:NW], xqTs[:, i % 2, c, :], wgqT[:, c, :],
                     start=(c == 0), stop=(c == 7))
            for c in range(8):
                b.mm(ps[ub][:, 0:NW], xqTs[:, i % 2, c, :], wuqT[:, c, :],
                     start=(c == 0), stop=(c == 7))
            pe_mm[i] = b.n["pe"]

        def stage_b(i):
            gb, ub = (0, 1) if i % 2 == 0 else (5, 6)
            b.wait("scalar", "pe", pe_mm[i])
            b.wait("scalar", "ve")
            b.act(tmps, ps[gb][:, 0:NW], AF.Silu)
            se_silu[i] = b.n["se"]
            b.wait("vector", "se", se_silu[i])
            b.tt(hb, tmps, ps[ub][:, 0:NW], ALU.mult)
            ve_tt[i] = b.n["ve"]
            if i >= 1:
                b.wait("vector", "pe")
            fp4_quant(b, C, hb, hq, NW, wrkh)
            ve_q[i] = b.n["ve"]

        def stage_c(i):
            b.wait("tensor", "ve", ve_q[i])
            if i >= 1:
                b.wait("tensor", "ve", ve_add[i - 1])
            for c in range(2):
                b.tp(ps[2][:, c * 128:(c + 1) * 128], hq[:, c * 128:(c + 1) * 128], ident)
            b.wait("vector", "pe")
            b.cp(hqT[:, 0, :], ps[2][:, 0:128], drain=False)
            b.cp(hqT[:, 1, :], ps[2][:, 128:256], drain=False)
            b.wait("tensor", "ve")
            for mh in range(2):
                for c in range(2):
                    b.mm(ps[3 + mh][:, 0:512], hqT[:, c, :],
                         wdqT[:, c, mh * 512:(mh + 1) * 512], start=(c == 0), stop=(c == 1))
            b.wait("vector", "pe")
            b.tt(A2_16[:, i, 0:512], A2_16[:, i, 0:512], ps[3][:, 0:512], ALU.add)
            b.tt(A2_16[:, i, 512:1024], A2_16[:, i, 512:1024], ps[4][:, 0:512], ALU.add)
            ve_add[i] = b.n["ve"]

        stage_a(0)
        stage_b(0)
        for i in range(1, NT):
            stage_a(i)
            stage_c(i - 1)
            stage_b(i)
        stage_c(NT - 1)
        b.barrier()
        if limit is not None and limit == f"n{nb}":
            return

    for i in range(NT):
        b.tt(stream[:, i, :], A2_16[:, i, :], stream[:, i, :], ALU.add)
        b.red(C["stats"][:, i:i + 1], stream[:, i, :], op="sum")
    postln_finish(b, C, stream, WK[:, 0:1024], WK[:, 1024:2048], eps=1e-5)
    for i in range(NT):
        b.dma("gpsimd", scr_x2[i * 128:(i + 1) * 128, :], stream[:, i, :], "x2w")
    b.barrier()


def emit_s3(nc, b, C, dr, ps, A1, A2, WA, WK, stream, A2_16,
            scr_x2, scr_qT, scr_kT, scr_vhat, scr_sgate, scr_osc, limit=None):
    ident = C["ident"][:]
    stats = C["stats"]
    hT = A2[:].rearrange("p (c t) -> p c t", c=8)

    g1bc = WK[:, 0:1024]
    b1bc = WK[:, 1024:2048]
    rmbc = WK[:, 2048:3072]
    ztmp = WK[:, 3072:4096]
    sqd = WK[:, 4096:5120]
    b.dma("sync", g1bc, dr["bc_ln1g"], "bc0")
    b.dma("sync", b1bc, dr["bc_ln1b"], "bc1")
    b.dma("sync", rmbc, dr["bc_rmsin"], "bc2")
    zsum, ssq = stats[:, 0:16], stats[:, 16:32]
    nmean, rstd = stats[:, 32:48], stats[:, 48:64]
    for i in range(NT):
        b.red(zsum[:, i:i + 1], stream[:, i, :], op="sum")
    vgaps(b, C)
    b.ts(nmean, zsum, -1.0 / 1024.0, None, ALU.mult)
    b.wait("scalar", "ve")
    for i in range(NT):
        b.act(ztmp, stream[:, i, :], AF.Identity, bias=nmean[:, i:i + 1])
        b.cp(stream[:, i, :], ztmp, eng="scalar")
        b.wait("vector", "se")
        b.stt(sqd, stream[:, i, :], 1.0, stream[:, i, :], ALU.mult, ALU.mult,
              accum=ssq[:, i:i + 1])
    vgaps(b, C)
    rstd_chain(b, C, ssq, rstd, 1024, 1e-5)
    b.wait("scalar", "ve")
    b.wait_dma("vector", "bc0")
    b.wait_dma("vector", "bc1")
    b.wait_dma("vector", "bc2")
    ssq2, rstd2 = stats[:, 0:16], stats[:, 16:32]
    for i in range(NT):
        b.wait("scalar", "ve")
        b.act(ztmp, stream[:, i, :], AF.Identity, scale=rstd[:, i:i + 1])
        b.wait("vector", "se")
        b.tt(ztmp, ztmp, g1bc, ALU.mult)
        b.tt(ztmp, ztmp, b1bc, ALU.add)
        b.cp(stream[:, i, :], ztmp)
        b.stt(sqd, stream[:, i, :], 1.0, stream[:, i, :], ALU.mult, ALU.mult,
              accum=ssq2[:, i:i + 1])
    vgaps(b, C)
    rstd_chain(b, C, ssq2, rstd2, 1024, 1e-6)
    b.wait("scalar", "ve")
    for i in range(NT):
        b.wait("scalar", "ve")
        b.wait("scalar", "pe")
        b.act(ztmp, stream[:, i, :], AF.Identity, scale=rstd2[:, i:i + 1])
        b.wait("vector", "se")
        b.tt(ztmp, ztmp, rmbc, ALU.mult)
        b.wait("tensor", "ve")
        items = [(ztmp[:, c * 128:(c + 1) * 128], hT[:, c, i * 128:(i + 1) * 128])
                 for c in range(8)]
        tp_batch(b, C, ps, items)
    b.barrier()
    if limit == "g0":
        for i in range(NT):
            b.cp(stream[:, i, :], A2_16[:, i, :])
        b.barrier()
        return

    gb = C["gb"]
    sp1 = WK[:, 0:64].rearrange("p (a f) -> p a f", a=8)
    for i in range(NT):
        b.wait("tensor", "ve")
        for c in range(8):
            b.mm(ps[i % 2][:, 0:8], hT[:, c, i * 128:(i + 1) * 128], C["wab"][:, c, :],
                 start=(c == 0), stop=(c == 7))
        b.wait("vector", "pe")
        b.wait("vector", "se")
        b.tt(sp1[:, 0, 0:4], ps[i % 2][:, 0:4], C["dtb"][:], ALU.add)
        b.wait("scalar", "ve")
        b.act(sp1[:, 1, 0:4], sp1[:, 0, 0:4], AF.Abs)
        b.act(sp1[:, 2, 0:4], sp1[:, 1, 0:4], AF.Exp, scale=-1.0)
        b.act(sp1[:, 3, 0:4], sp1[:, 2, 0:4], AF.Ln, bias=1.0)
        b.act(sp1[:, 4, 0:4], sp1[:, 0, 0:4], AF.Relu)
        b.act(gb[:, i, 4:8], ps[i % 2][:, 4:8], AF.Sigmoid)
        b.wait("vector", "se")
        b.tt(sp1[:, 5, 0:4], sp1[:, 3, 0:4], sp1[:, 4, 0:4], ALU.add)
        b.tt(gb[:, i, 0:4], sp1[:, 5, 0:4], C["coef"][:], ALU.mult)
    b.barrier()

    vpre = WK[:, 0:1048].rearrange("p (c f) -> p c f", c=8)
    cvo = WK[:, 1056:2080].rearrange("p (c f) -> p c f", c=8)
    cvs = WK[:, 2080:3104].rearrange("p (c f) -> p c f", c=8)
    slab = WK[:, 3104:4128]
    wv_ = WA[:, 0:8192].rearrange("p (c f) -> p c f", c=8)
    for (wname, cw, scr, do_conv) in (("l1_wv", "convv", scr_vhat, True),
                                      ("l1_wg", None, scr_sgate, False)):
        for half in range(2):
            b.barrier()
            b.dma("sync", wv_[:], dr[wname][:, half * 1024:(half + 1) * 1024]
                  .rearrange("(c p) f -> p c f", p=128), "wbig")
            b.wait_dma("tensor", "wbig")
            for i in range(NT):
                if i >= 1:
                    if do_conv:
                        b.wait_dma("vector", "sw", None)
                    else:
                        b.wait_dma("scalar", "sw", None)
                if do_conv:
                    b.wait("tensor", "ve")
                    for m in range(8):
                        pp = ps[m % 2]
                        if m >= 2:
                            b.wait("tensor", "ve")
                        if i == 0:
                            for c in range(8):
                                b.mm(pp[:, 0:128], wv_[:, c, m * 128:(m + 1) * 128],
                                     hT[:, c, 0:128], start=(c == 0), stop=(c == 7))
                        else:
                            for c in range(8):
                                b.mm(pp[:, 0:131], wv_[:, c, m * 128:(m + 1) * 128],
                                     hT[:, c, i * 128 - 3:(i + 1) * 128],
                                     start=(c == 0), stop=(c == 7))
                        b.wait("vector", "pe")
                        if i == 0:
                            b.memset(vpre[:, m, 0:3], 0.0)
                            b.cp(vpre[:, m, 3:131], pp[:, 0:128], drain=False)
                        else:
                            b.cp(vpre[:, m, :], pp[:, 0:131], drain=False)
                    cwt = C[cw]
                    hc0 = half * 8
                    for m in range(8):
                        b.ts(cvo[:, m, :], vpre[:, m, 0:128], cwt[:, hc0 + m, 0:1], None,
                             ALU.mult)
                        for j in range(1, 4):
                            b.stt(cvo[:, m, :], vpre[:, m, j:j + 128],
                                  cwt[:, hc0 + m, j:j + 1], cvo[:, m, :], ALU.mult, ALU.add)
                    b.wait("scalar", "ve")
                    b.wait("scalar", "pe")
                    b.act(cvs[:, :, :], cvo[:, :, :], AF.Silu)
                    b.wait("tensor", "se")
                    items = [(cvs[:, m, :], slab[:, m * 128:(m + 1) * 128]) for m in range(8)]
                    tp_batch(b, C, ps, items, banks=(2, 3, 4, 5))
                else:
                    b.wait("tensor", "se")
                    for nh in range(2):
                        pp = ps[nh]
                        for c in range(8):
                            b.mm(pp[:, 0:512], hT[:, c, i * 128:(i + 1) * 128],
                                 wv_[:, c, nh * 512:(nh + 1) * 512],
                                 start=(c == 0), stop=(c == 7))
                    b.wait("scalar", "pe")
                    b.act(slab[:, 0:512], ps[0][:, 0:512], AF.Silu)
                    b.act(slab[:, 512:1024], ps[1][:, 0:512], AF.Silu)
                ch = "ve" if do_conv else "se"
                b.wait("gpsimd", ch)
                b.dma("gpsimd", scr[i * 128:(i + 1) * 128, half * 1024:(half + 1) * 1024],
                      slab, "sw")
        b.barrier()
    if limit == "g2ab":
        return

    qpre = WK[:, 0:1048].rearrange("p (c f) -> p c f", c=8)
    qc = WK[:, 1056:2080].rearrange("p (c f) -> p c f", c=8)
    qs = WK[:, 2080:3104].rearrange("p (c f) -> p c f", c=8)
    qq = WK[:, 3104:4128].rearrange("p (c f) -> p c f", c=8)
    rstdT = WK[:, 4128:4256]
    bcr = WK[:, 4256:4384]
    for (wname, cw, scr, extra) in (("l1_wq", "convq", scr_qT, 0.0625),
                                    ("l1_wk", "convk", scr_kT, None)):
        b.barrier()
        b.dma("sync", wv_[:], dr[wname][:].rearrange("(c p) f -> p c f", p=128), "wbig")
        b.wait_dma("tensor", "wbig")
        for i in range(NT):
            b.wait("tensor", "ve")
            for m in range(8):
                pp = ps[m % 2]
                if m >= 2:
                    b.wait("tensor", "ve")
                if i == 0:
                    for c in range(8):
                        b.mm(pp[:, 0:128], wv_[:, c, m * 128:(m + 1) * 128],
                             hT[:, c, 0:128], start=(c == 0), stop=(c == 7))
                else:
                    for c in range(8):
                        b.mm(pp[:, 0:131], wv_[:, c, m * 128:(m + 1) * 128],
                             hT[:, c, i * 128 - 3:(i + 1) * 128],
                             start=(c == 0), stop=(c == 7))
                b.wait("vector", "pe")
                if i == 0:
                    b.memset(qpre[:, m, 0:3], 0.0)
                    b.cp(qpre[:, m, 3:131], pp[:, 0:128], drain=False)
                else:
                    b.cp(qpre[:, m, :], pp[:, 0:131], drain=False)
            cwt = C[cw]
            for m in range(8):
                b.ts(qc[:, m, :], qpre[:, m, 0:128], cwt[:, m, 0:1], None, ALU.mult)
                for j in range(1, 4):
                    b.stt(qc[:, m, :], qpre[:, m, j:j + 128], cwt[:, m, j:j + 1],
                          qc[:, m, :], ALU.mult, ALU.add)
            b.wait("scalar", "ve")
            b.wait("scalar", "pe")
            if i >= 1:
                b.wait_dma("scalar", f"qw{(i - 1) % 2}", None)
                b.wait_dma("vector", f"qw{(i - 1) % 2}", None)
            b.act(qs[:, :, :], qc[:, :, :], AF.Silu)
            b.act(qq[:, :, :], qs[:, :, :], AF.Square)
            b.wait("tensor", "se")
            for h in range(4):
                b.mm(ps[2][:, h:h + 1], qq[:, 2 * h, :], C["ones_col"][:],
                     start=True, stop=False)
                b.mm(ps[2][:, h:h + 1], qq[:, 2 * h + 1, :], C["ones_col"][:],
                     start=False, stop=True)
            b.wait("vector", "pe")
            b.cp(stats[:, 0:4], ps[2][:, 0:4])
            vgaps(b, C)
            rstd_chain(b, C, stats[:, 0:4], stats[:, 4:8], 1, 1e-6,
                       extra_scale=extra, is_sum=True)
            for h in range(4):
                bcast_col(b, C, ps, stats[:, 4 + h:5 + h], bcr, bank=3)
                b.tt(qs[:, 2 * h, :], qs[:, 2 * h, :], bcr, ALU.mult)
                b.tt(qs[:, 2 * h + 1, :], qs[:, 2 * h + 1, :], bcr, ALU.mult)
            b.wait("gpsimd", "ve")
            if i >= 2:
                b.wait_dma("gpsimd", f"qw{i % 2}", None)
            b.dma("gpsimd", scr[i].rearrange("p (c f) -> p c f", c=8),
                  qs[:, :, :], f"qw{i % 2}")
        b.barrier()
    if limit == "g2pre":
        return

    SS = A1[:, 0:4096].rearrange("p (c f) -> p c f", c=8)
    TIL = A1[:, 4096:5888].rearrange("p (c f) -> p c f", c=14)
    bcG, nbcG, DqT, Dk_, Aqk, Xs, XTs, X2s, X2Ts, TTa, TTb, Ns, NTs, bcb = (
        TIL[:, k, :] for k in range(14))
    GBT = A1[:, 5888:6016]
    gb2 = A1[:, 6016:6024]
    sm = A1[:, 6024:6056]
    negG4 = sm[:, 0:4]
    eG4 = sm[:, 4:8]
    ebg4 = sm[:, 8:12]
    gcm = sm[:, 12:16]
    egd = sm[:, 16:20]
    eGC = sm[:, 20:21]
    ktok = A1[:, 6080:6336].rearrange("p (c f) -> p c f", c=2)
    kdec = A1[:, 6336:6592].rearrange("p (c f) -> p c f", c=2)
    kd2 = A1[:, 6592:6848].rearrange("p (c f) -> p c f", c=2)
    qg = A1[:, 6848:7104].rearrange("p (c f) -> p c f", c=2)
    Ub = A1[:, 7104:7616]
    bVh = A1[:, 7616:8128]
    ob4 = A1[:, 8128:10176].rearrange("p (c f) -> p c f", c=4)
    qTc = A1[:, 10176:11200].rearrange("p (c f) -> p c f", c=8)
    kTc = A1[:, 11200:12224].rearrange("p (c f) -> p c f", c=8)
    vc = A1[:, 12224:14272]
    oscb = A1[:, 14336:16384]
    b.memset(SS[:, :, :], 0.0)
    onb = WK[:, 0:512]
    sq2 = WK[:, 512:1024]
    b.dma("sync", onb, dr["bc_onorm"], "onw")
    b.barrier()
    for c in range(NT):
        b.wait("tensor", "ve")
        b.wait("vector", "pe")
        b.wait("scalar", "pe")
        b.wait("sync", "pe")
        b.wait("sync", "ve")
        b.dma("sync", qTc[:, :, :], scr_qT[c].rearrange("p (a f) -> p a f", a=8), "qrd")
        b.dma("sync", kTc[:, :, :], scr_kT[c].rearrange("p (a f) -> p a f", a=8), "krd")
        b.dma("sync", vc, scr_vhat[c * 128:(c + 1) * 128, :], "vrd")
        b.dma("sync", oscb, scr_sgate[c * 128:(c + 1) * 128, :], "grd")
        b.wait_dma("tensor", "qrd")
        b.wait_dma("tensor", "krd")
        b.wait_dma("vector", "vrd")
        b.wait_dma("vector", "grd")
        b.mm(ps[7][:, 0:4], C["ones_ui"][:], gb[:, c, 0:4], start=True, stop=True)
        b.wait("vector", "pe")
        b.cp(gb2[:, 0:4], ps[7][:, 0:4])
        b.cp(gb2[:, 4:8], gb[:, c, 4:8], drain=False)
        vgaps(b, C)
        b.ts(negG4, gb2[:, 0:4], -1.0, None, ALU.mult)
        b.wait("scalar", "ve")
        b.act(eG4, gb2[:, 0:4], AF.Exp)
        b.wait("vector", "se")
        b.tt(ebg4, eG4, gb2[:, 4:8], ALU.mult)
        for h in range(4):
            b.wait("tensor", "ve")
            b.mm(ps[0][:, 0:128], kTc[:, 2 * h, :], qTc[:, 2 * h, :], start=True, stop=False)
            b.mm(ps[0][:, 0:128], kTc[:, 2 * h + 1, :], qTc[:, 2 * h + 1, :],
                 start=False, stop=True)
            b.mm(ps[1][:, 0:128], kTc[:, 2 * h, :], kTc[:, 2 * h, :], start=True, stop=False)
            b.mm(ps[1][:, 0:128], kTc[:, 2 * h + 1, :], kTc[:, 2 * h + 1, :],
                 start=False, stop=True)
            bcast_col(b, C, ps, gb2[:, h:h + 1], bcG, bank=7)
            b.ts(nbcG, bcG, -1.0, None, ALU.mult)
            bcast_col(b, C, ps, gb2[:, 4 + h:5 + h], bcb, bank=7)
            b.ts(bcb, bcb, -1.0, None, ALU.mult)
            b.stt(DqT, bcG, negG4[:, h:h + 1], C["mask_ui"][:], ALU.add, ALU.add)
            b.stt(Dk_, bcG, negG4[:, h:h + 1], C["mask_us"][:], ALU.add, ALU.add)
            b.wait("scalar", "ve")
            b.act(DqT, DqT, AF.Exp)
            b.act(Dk_, Dk_, AF.Exp)
            b.wait("vector", "se")
            b.tt(Aqk, ps[0][:, 0:128], DqT, ALU.mult)
            b.tt(NTs, ps[1][:, 0:128], Dk_, ALU.mult)
            b.tt(NTs, NTs, bcb, ALU.mult)
            b.stt(Dk_, nbcG, gb2[:, h:h + 1], C["mask_ls"][:], ALU.add, ALU.add)
            b.wait("scalar", "ve")
            b.act(Dk_, Dk_, AF.Exp)
            b.wait("vector", "se")
            b.tt(Ns, ps[1][:, 0:128], Dk_, ALU.mult)
            b.ts(Ns, Ns, gb2[:, 4 + h:5 + h], -1.0, ALU.mult, ALU.mult)
            b.tt(TTa, NTs, ident, ALU.add)
            b.cp(Xs, Ns)
            b.cp(XTs, NTs)
            ta, tb = TTa, TTb
            for lev in range(6):
                b.wait("tensor", "ve")
                b.mm(ps[1][:, 0:128], XTs, Xs, start=True, stop=True)
                b.mm(ps[2][:, 0:128], Xs, XTs, start=True, stop=True)
                b.wait("vector", "pe")
                b.cp(X2s, ps[1][:, 0:128])
                b.cp(X2Ts, ps[2][:, 0:128])
                b.wait("tensor", "ve")
                b.mm(ps[3][:, 0:128], X2s, ta, start=True, stop=True)
                b.wait("vector", "pe")
                b.tt(tb, ta, ps[3][:, 0:128], ALU.add)
                b.cp(Xs, X2s)
                b.cp(XTs, X2Ts)
                ta, tb = tb, ta
            b.ts(bVh, vc[:, h * 512:(h + 1) * 512], gb2[:, 4 + h:5 + h], None, ALU.mult)
            b.wait("tensor", "ve")
            b.mm(ps[4][:, 0:512], ta, bVh, start=True, stop=True)
            b.wait("vector", "pe")
            b.cp(Ub, ps[4][:, 0:512])
            b.wait("tensor", "ve")
            b.tp(ps[5][:, 0:128], kTc[:, 2 * h, :], ident)
            b.tp(ps[5][:, 128:256], kTc[:, 2 * h + 1, :], ident)
            b.wait("vector", "pe")
            b.cp(ktok[:, 0, :], ps[5][:, 0:128])
            b.cp(ktok[:, 1, :], ps[5][:, 128:256])
            b.ts(kdec[:, 0, :], ktok[:, 0, :], ebg4[:, h:h + 1], None, ALU.mult)
            b.ts(kdec[:, 1, :], ktok[:, 1, :], ebg4[:, h:h + 1], None, ALU.mult)
            b.wait("tensor", "ve")
            b.mm(ps[5][:, 0:128], kdec[:, 0, :], ta, start=True, stop=True)
            b.mm(ps[5][:, 128:256], kdec[:, 1, :], ta, start=True, stop=True)
            b.wait("vector", "pe")
            b.cp(qg[:, 0, :], ps[5][:, 0:128])
            b.cp(qg[:, 1, :], ps[5][:, 128:256])
            if c > 0:
                b.wait("tensor", "ve")
                b.mm(ps[4][:, 0:512], qg[:, 0, :], SS[:, 2 * h, :], start=True, stop=False)
                b.mm(ps[4][:, 0:512], qg[:, 1, :], SS[:, 2 * h + 1, :], start=False, stop=True)
                b.wait("vector", "pe")
                b.tt(Ub, Ub, ps[4][:, 0:512], ALU.subtract)
            b.wait("tensor", "ve")
            b.mm(ps[4][:, 0:512], Aqk, Ub, start=True, stop=(c == 0))
            if c > 0:
                b.wait("scalar", "ve")
                b.act(bcG, bcG, AF.Exp)
                b.wait("vector", "se")
                b.tt(qg[:, 0, :], qTc[:, 2 * h, :], bcG, ALU.mult)
                b.tt(qg[:, 1, :], qTc[:, 2 * h + 1, :], bcG, ALU.mult)
                b.wait("tensor", "ve")
                b.mm(ps[4][:, 0:512], qg[:, 0, :], SS[:, 2 * h, :], start=False, stop=False)
                b.mm(ps[4][:, 0:512], qg[:, 1, :], SS[:, 2 * h + 1, :], start=False, stop=True)
            b.wait("vector", "pe")
            b.cp(ob4[:, h, :], ps[4][:, 0:512])
            bcast_col(b, C, ps, gb2[:, h:h + 1], bcG, bank=7)
            b.tt(gcm[:, 0:1], bcG[:, 127:128], gb2[:, h:h + 1], ALU.subtract)
            b.wait("scalar", "ve")
            b.act(egd[:, 0:1], gcm[:, 0:1], AF.Exp)
            b.act(eGC, bcG[:, 127:128], AF.Exp)
            b.wait("vector", "se")
            b.ts(kd2[:, 0, :], ktok[:, 0, :], egd[:, 0:1], None, ALU.mult)
            b.ts(kd2[:, 1, :], ktok[:, 1, :], egd[:, 0:1], None, ALU.mult)
            b.wait("tensor", "ve")
            b.mm(ps[5][:, 0:512], kd2[:, 0, :], Ub, start=True, stop=True)
            b.mm(ps[6][:, 0:512], kd2[:, 1, :], Ub, start=True, stop=True)
            b.wait("vector", "pe")
            if c == 0:
                b.cp(SS[:, 2 * h, :], ps[5][:, 0:512])
                b.cp(SS[:, 2 * h + 1, :], ps[6][:, 0:512])
            else:
                b.stt(SS[:, 2 * h, :], SS[:, 2 * h, :], eGC[:, 0:1], ps[5][:, 0:512],
                      ALU.mult, ALU.add)
                b.stt(SS[:, 2 * h + 1, :], SS[:, 2 * h + 1, :], eGC[:, 0:1], ps[6][:, 0:512],
                      ALU.mult, ALU.add)
        for h in range(4):
            b.stt(sq2, ob4[:, h, :], 1.0, ob4[:, h, :], ALU.mult, ALU.mult,
                  accum=stats[:, h:h + 1])
        vgaps(b, C)
        rstd_chain(b, C, stats[:, 0:4], stats[:, 4:8], 512, 1e-6)
        b.wait("scalar", "ve")
        b.wait_dma("vector", "onw")
        for h in range(4):
            b.wait("scalar", "ve")
            b.act(sq2, ob4[:, h, :], AF.Identity, scale=stats[:, 4 + h:5 + h])
            b.wait("vector", "se")
            b.tt(sq2, sq2, onb, ALU.mult)
            b.tt(oscb[:, h * 512:(h + 1) * 512], oscb[:, h * 512:(h + 1) * 512],
                 sq2, ALU.mult)
        b.wait("gpsimd", "ve")
        if c >= 1:
            b.wait_dma("gpsimd", "ow", None)
        b.dma("gpsimd", scr_osc[c * 128:(c + 1) * 128, :], oscb, "ow")
        b.wait_dma("sync", "ow")
    b.barrier()
    if limit == "g2c":
        return

    oT = WK[:, 0:2048].rearrange("p (c f) -> p c f", c=16)
    xtk = WK[:, 2048:4096].rearrange("p (c f) -> p c f", c=2)
    osb = WK[:, 4096:6144]
    xa16 = A2_16
    for half in range(2):
        b.barrier()
        b.dma("sync", wv_[:], dr["l1_wo"][half * 1024:(half + 1) * 1024, :]
              .rearrange("(c p) f -> p c f", p=128), "wbig")
        b.wait_dma("tensor", "wbig")
        for i in range(NT):
            if i >= 1:
                b.wait("sync", "pe")
                b.wait("sync", "ve")
            b.dma("sync", osb, scr_osc[i * 128:(i + 1) * 128, :], "osr")
            b.dma("sync", xtk[:, i % 2, :], scr_x2[i * 128:(i + 1) * 128, :], f"x2r{i % 2}")
            b.wait_dma("tensor", "osr")
            b.wait("tensor", "ve")
            items = [(osb[:, m * 128:(m + 1) * 128], oT[:, m, :]) for m in range(16)]
            tp_batch(b, C, ps, items, banks=(2, 3, 4, 5))
            b.wait("tensor", "ve")
            for mh in range(2):
                pp = ps[mh]
                for cch in range(8):
                    b.mm(pp[:, 0:512], oT[:, half * 8 + cch, :],
                         wv_[:, cch, mh * 512:(mh + 1) * 512],
                         start=(cch == 0), stop=(cch == 7))
            b.wait("vector", "pe")
            b.wait_dma("vector", f"x2r{i % 2}")
            if half == 0:
                b.tt(xa16[:, i, 0:512], ps[0][:, 0:512], xtk[:, i % 2, 0:512], ALU.add)
                b.tt(xa16[:, i, 512:1024], ps[1][:, 0:512], xtk[:, i % 2, 512:1024], ALU.add)
            else:
                b.tt(xa16[:, i, 0:512], ps[0][:, 0:512], xa16[:, i, 0:512], ALU.add)
                b.tt(xa16[:, i, 512:1024], ps[1][:, 0:512], xa16[:, i, 512:1024], ALU.add)
    b.barrier()
    g2bc = WK[:, 0:1024]
    b2bc2 = WK[:, 1024:2048]
    bb2 = WK[:, 2048:3072]
    h2t = WK[:, 3072:4096]
    sqd3 = WK[:, 4096:5120]
    h2T = WK[:, 5120:6144].rearrange("p (c f) -> p c f", c=8)
    xtk2 = WK[:, 6144:8192].rearrange("p (c f) -> p c f", c=2)
    b.dma("sync", g2bc, dr["bc_ln2g"], "bc0")
    b.dma("sync", b2bc2, dr["bc_ln2b"], "bc1")
    b.dma("sync", bb2, dr["bc_b2"], "bc2")
    b.dma("sync", WA[:, 0:8192].rearrange("p (c f) -> p c f", c=8)[:],
          dr["l1_w2"][:].rearrange("(c p) f -> p c f", p=128), "wbig")
    w2_ = WA[:, 0:8192].rearrange("p (c f) -> p c f", c=8)
    for i in range(NT):
        b.red(zsum[:, i:i + 1], xa16[:, i, :], op="sum")
    vgaps(b, C)
    b.ts(nmean, zsum, -1.0 / 1024.0, None, ALU.mult)
    b.wait("scalar", "ve")
    for i in range(NT):
        b.wait("scalar", "ve")
        b.act(h2t, xa16[:, i, :], AF.Identity, bias=nmean[:, i:i + 1])
        b.wait("vector", "se")
        b.stt(sqd3, h2t, 1.0, h2t, ALU.mult, ALU.mult, accum=ssq[:, i:i + 1])
        b.cp(stream[:, i, :], h2t)
    vgaps(b, C)
    rstd_chain(b, C, ssq, rstd, 1024, 1e-5)
    b.wait("scalar", "ve")
    b.wait_dma("vector", "bc0")
    b.wait_dma("vector", "bc1")
    b.wait_dma("tensor", "wbig")
    for i in range(NT):
        b.wait("scalar", "ve")
        b.wait("scalar", "pe")
        b.act(h2t, stream[:, i, :], AF.Identity, scale=rstd[:, i:i + 1])
        b.wait("vector", "se")
        b.tt(h2t, h2t, g2bc, ALU.mult)
        b.tt(h2t, h2t, b2bc2, ALU.add)
        b.wait("tensor", "ve")
        items = [(h2t[:, cch * 128:(cch + 1) * 128], h2T[:, cch, :]) for cch in range(8)]
        tp_batch(b, C, ps, items, banks=(2, 3, 4, 5))
        b.wait("tensor", "ve")
        for mh in range(2):
            pp = ps[mh]
            for cch in range(8):
                b.mm(pp[:, 0:512], h2T[:, cch, :], w2_[:, cch, mh * 512:(mh + 1) * 512],
                     start=(cch == 0), stop=(cch == 7))
        if i >= 2:
            b.wait("sync", "ve")
        b.dma("sync", xtk2[:, i % 2, :], scr_x2[i * 128:(i + 1) * 128, :], f"x2s{i % 2}")
        b.wait("scalar", "pe")
        b.wait("vector", "pe")
        b.wait_dma("vector", "bc2")
        b.tt(h2t[:, 0:512], ps[0][:, 0:512], bb2[:, 0:512], ALU.add)
        b.tt(h2t[:, 512:1024], ps[1][:, 0:512], bb2[:, 512:1024], ALU.add)
        b.wait("scalar", "ve")
        b.act(h2t, h2t, AF.Gelu)
        b.wait("vector", "se")
        b.wait_dma("vector", f"x2s{i % 2}")
        b.tt(h2t, h2t, xa16[:, i, :], ALU.add)
        b.tt(stream[:, i, :], h2t, xtk2[:, i % 2, :], ALU.add)
        b.red(C["stats"][:, i:i + 1], stream[:, i, :], op="sum")
    postln_finish(b, C, stream, WK[:, 0:1024], WK[:, 1024:2048], eps=1e-5)


SPARSE_MOE = True


def emit_s4(nc, b, C, dr, ps, A1, A2, WA, WK, stream, A2_16, scr_x3):
    if SPARSE_MOE:
        return emit_s4_sparse(nc, b, C, dr, ps, A1, A2, WA, WK, stream, A2_16, scr_x3)
    return emit_s4_dense(nc, b, C, dr, ps, A1, A2, WA, WK, stream, A2_16, scr_x3)


def emit_s4_sparse(nc, b, C, dr, ps, A1, A2, WA, WK, stream, A2_16, scr_x3):
    ident = C["ident"][:]
    stats = C["stats"]
    ytok = A2_16
    comb = stream

    mrbc = WK[:, 0:1024]
    ytmp = WK[:, 1024:2048]
    sqd = WK[:, 2048:3072]
    lgb = WK[:, 3072:3080]
    lg2 = WK[:, 3080:3088]
    t1 = WK[:, 3088:3096]
    msk1 = WK[:, 3096:3104]
    msk2 = WK[:, 3104:3112]
    m8 = WK[:, 3112:3120]
    asb = WK[:, 3120:3128]
    offbc = WK[:, 3128:3136]
    ytT = WK[:, 4096:5120].rearrange("p (c f) -> p c f", c=8)
    b.dma("sync", mrbc, dr["bc_moerms"], "bc0")
    ssqm, rstdm = stats[:, 0:16], stats[:, 16:32]
    m1c, m2c, d12 = stats[:, 32:33], stats[:, 33:34], stats[:, 34:35]
    w1c, w2c = stats[:, 35:36], stats[:, 36:37]
    for i in range(NT):
        b.stt(sqd, stream[:, i, :], 1.0, stream[:, i, :], ALU.mult, ALU.mult,
              accum=ssqm[:, i:i + 1])
    vgaps(b, C)
    rstd_chain(b, C, ssqm, rstdm, 1024, 1e-6)
    b.wait("scalar", "ve")
    b.wait_dma("vector", "bc0")
    b.memset(C["offrow"][:], 0.0)
    for i in range(NT):
        b.wait("scalar", "ve")
        b.wait("scalar", "pe")
        b.act(ytmp, stream[:, i, :], AF.Identity, scale=rstdm[:, i:i + 1])
        b.wait("vector", "se")
        b.tt(ytmp, ytmp, mrbc, ALU.mult)
        b.cp(ytok[:, i, :], ytmp)
        b.wait("tensor", "ve")
        items = [(ytmp[:, c * 128:(c + 1) * 128], ytT[:, c, :]) for c in range(8)]
        tp_batch(b, C, ps, items, banks=(0, 1, 2, 3))
        b.wait("tensor", "ve")
        for c in range(8):
            b.mm(ps[6][:, 0:8], ytT[:, c, :], C["gwT"][:, c, :],
                 start=(c == 0), stop=(c == 7))
        b.wait("vector", "pe")
        b.cp(lgb, ps[6][:, 0:8])
        b.red(m1c, lgb, op="max")
        b.tt(msk1, lgb, m1c.broadcast_to([128, 8]), ALU.is_equal)
        b.stt(lg2, msk1, -1e30, lgb, ALU.mult, ALU.add)
        b.red(m2c, lg2, op="max")
        b.tt(msk2, lg2, m2c.broadcast_to([128, 8]), ALU.is_equal)
        b.tt(d12, m1c, m2c, ALU.subtract)
        b.wait("scalar", "ve")
        b.act(w1c, d12, AF.Sigmoid)
        b.act(w2c, d12, AF.Sigmoid, scale=-1.0)
        b.wait("vector", "se")
        b.ts(t1, msk2, w2c, None, ALU.mult)
        b.stt(C["wfull"][:, i, :], msk1, w1c, t1, ALU.mult, ALU.add)
        b.tt(m8, msk1, msk2, ALU.max)
        b.wait("tensor", "ve")
        b.mm(ps[6][:, 8:16], C["ones_ui"][:], m8, start=True, stop=True)
        b.mm(ps[7][:, 0:8], C["ones_r"][:], C["offrow"][:], start=True, stop=True)
        b.wait("vector", "pe")
        b.cp(offbc, ps[7][:, 0:8])
        b.tt(asb, ps[6][:, 8:16], offbc, ALU.add)
        b.tt(t1, asb, m8, ALU.mult)
        b.ts(C["rank"][:, i, :], t1, 1.0, -1.0, ALU.mult, ALU.add)
        b.wait("tensor", "ve")
        b.mm(ps[7][0:1, 8:16], C["e127"][:], asb, start=True, stop=True)
        b.wait("vector", "pe")
        b.cp(C["offrow"][:], ps[7][0:1, 8:16])
    b.barrier()

    for i in range(NT):
        b.dma("gpsimd", scr_x3[i * 128:(i + 1) * 128, :], stream[:, i, :], "x3w")
    b.wait_dma("vector", "x3w")
    for i in range(NT):
        b.memset(comb[:, i, :], 0.0)
    b.barrier()

    NBW = [384] * 7 + [128]
    w1s = WA[:, 0:3072].rearrange("p (c f) -> p c f", c=8)
    w2s = WA[:, 3072:6144].rearrange("p (c f) -> p c f", c=8)
    w3s = WA[:, 6144:9216].rearrange("p (c f) -> p c f", c=3)
    yeT = WK[:, 0:3072].rearrange("p (c f) -> p c f", c=8)
    oute = WK[:, 3072:6144].rearrange("p (c f) -> p c f", c=3)
    ygt = WK[:, 3072:4096]
    Pslab = WK[:, 6144:7168].rearrange("p (c f) -> p c f", c=8)
    tmps = WK[:, 7168:7552]
    hb = WK[:, 7552:7936]
    hTe = WK[:, 7936:8320].rearrange("p (c f) -> p c f", c=3)
    rk8 = WK[:, 8320:8328]
    rkbc = WK[:, 8328:8456]
    wbc = WK[:, 8456:8584]
    atile = Pslab

    for e in range(8):
        for half in range(2):
            for rcl in range(3):
                rc = 3 * half + rcl
                for g in range(2):
                    b.wait("vector", "pe")
                    for j in range(8):
                        tc = g * 8 + j
                        b.ts(rk8[:, j:j + 1], C["rank"][:, tc, e:e + 1], 1.0,
                             -128.0 * rc, ALU.mult, ALU.add)
                    vgaps(b, C)
                    for j in range(8):
                        b.ts(Pslab[:, j, :], C["iota_f"][:], rk8[:, j:j + 1], None,
                             ALU.is_equal)
                    b.wait("tensor", "ve")
                    for j in range(8):
                        tc = g * 8 + j
                        for mh in range(2):
                            b.mm(ps[mh][:, 0:512], Pslab[:, j, :],
                                 ytok[:, tc, mh * 512:(mh + 1) * 512],
                                 start=(g == 0 and j == 0), stop=(g == 1 and j == 7))
                b.wait("vector", "pe")
                b.cp(ygt[:, 0:512], ps[0][:, 0:512])
                b.cp(ygt[:, 512:1024], ps[1][:, 0:512])
                b.wait("tensor", "ve")
                items = [(ygt[:, c * 128:(c + 1) * 128], yeT[:, c, rcl * 128:(rcl + 1) * 128])
                         for c in range(8)]
                tp_batch(b, C, ps, items, banks=(2, 3, 4, 5))
            b.barrier()
            wslot = [WA[:, 0:3072].rearrange("p (c f) -> p c f", c=8),
                     WA[:, 3072:6144].rearrange("p (c f) -> p c f", c=8)]
            wslot3 = [WA[:, 0:3072].rearrange("p (c f) -> p c f", c=3),
                      WA[:, 3072:6144].rearrange("p (c f) -> p c f", c=3)]
            upb = WA[:, 6144:7296].rearrange("p (c f) -> p c f", c=3)
            hb3 = WA[:, 7296:8448].rearrange("p (c f) -> p c f", c=3)
            loads = []
            for nb in range(8):
                loads += [(1, nb), (2, nb), (3, nb)]
            pe_done = {}

            def issue_load(li):
                kind, nb = loads[li]
                slot = li % 2
                n0 = sum(NBW[:nb])
                NW = NBW[nb]
                nc3 = NW // 128
                q = "sync" if li % 2 == 0 else "gpsimd"
                if li >= 2:
                    b.wait(q, "pe", pe_done.get(li - 2, 0))
                src = dr[("moe_w1i", "moe_w2i", "moe_w3i")[kind - 1]]
                if kind < 3:
                    b.dma(q, wslot[slot][:, :, 0:NW],
                          src[e, nb, :, 0:8 * NW].rearrange("p (c f) -> p c f", c=8),
                          f"wl{slot}")
                else:
                    b.dma(q, wslot3[slot][:, 0:nc3, :],
                          src[e, nb, :, 0:nc3 * 1024].rearrange("p (c f) -> p c f", c=nc3),
                          f"wl{slot}")

            issue_load(0)
            issue_load(1)
            for nb in range(8):
                n0 = sum(NBW[:nb])
                NW = NBW[nb]
                nc3 = NW // 128
                li = nb * 3
                s1_, s2_, s3_ = li % 2, (li + 1) % 2, (li + 2) % 2
                b.wait_dma("tensor", f"wl{s1_}", None)
                for r in range(3):
                    pp = ps[(0, 1, 5)[r]]
                    if nb > 0:
                        b.wait("tensor", "ve")
                    for c in range(8):
                        b.mm(pp[:, 0:NW], yeT[:, c, r * 128:(r + 1) * 128],
                             wslot[s1_][:, c, 0:NW], start=(c == 0), stop=(c == 7))
                    b.wait("vector", "pe")
                    b.cp(upb[:, r, 0:NW], pp[:, 0:NW], drain=False)
                pe_done[li] = b.n["pe"]
                if li + 2 < len(loads):
                    issue_load(li + 2)
                b.wait_dma("tensor", f"wl{s2_}", None)
                for r in range(3):
                    pp = ps[(0, 1, 5)[r]]
                    b.wait("tensor", "ve")
                    for c in range(8):
                        b.mm(pp[:, 0:NW], yeT[:, c, r * 128:(r + 1) * 128],
                             wslot[s2_][:, c, 0:NW], start=(c == 0), stop=(c == 7))
                    b.wait("scalar", "pe")
                    b.wait("scalar", "ve")
                    b.act(tmps[:, 0:NW], upb[:, r, 0:NW], AF.Silu)
                    b.wait("vector", "se")
                    b.tt(hb3[:, r, 0:NW], tmps[:, 0:NW], pp[:, 0:NW], ALU.mult)
                pe_done[li + 1] = b.n["pe"]
                if li + 3 < len(loads):
                    issue_load(li + 3)
                b.wait_dma("tensor", f"wl{s3_}", None)
                b.wait("tensor", "ve")
                for r in range(3):
                    items = [(hb3[:, r, c * 128:(c + 1) * 128],
                              hTe[:, c, :]) for c in range(nc3)]
                    tp_batch(b, C, ps, items, banks=(2, 6))
                    b.wait("tensor", "ve")
                    for mh in range(2):
                        for c in range(nc3):
                            b.mm(ps[3 + mh][:, 0:512], hTe[:, c, :],
                                 wslot3[s3_][:, c, mh * 512:(mh + 1) * 512],
                                 start=(c == 0), stop=(c == nc3 - 1))
                    b.wait("vector", "pe")
                    if nb == 0:
                        b.cp(oute[:, r, 0:512], ps[3][:, 0:512], drain=False)
                        b.cp(oute[:, r, 512:1024], ps[4][:, 0:512], drain=False)
                    else:
                        b.tt(oute[:, r, 0:512], oute[:, r, 0:512], ps[3][:, 0:512], ALU.add)
                        b.tt(oute[:, r, 512:1024], oute[:, r, 512:1024], ps[4][:, 0:512],
                             ALU.add)
                pe_done[li + 2] = b.n["pe"]
            b.barrier()
            for tc in range(NT):
                bcast_col(b, C, ps, C["rank"][:, tc, e:e + 1], rkbc, bank=7)
                bcast_col(b, C, ps, C["wfull"][:, tc, e:e + 1], wbc, bank=7)
                b.wait("vector", "pe")
                for rcl in range(3):
                    rc = 3 * half + rcl
                    b.ts(atile[:, 0, :], rkbc, 1.0, -128.0 * rc, ALU.mult, ALU.add)
                    b.tt(atile[:, 1, :], C["iota_p"][:], atile[:, 0, :], ALU.is_equal)
                    b.tt(Pslab[:, 4 + rcl, :], atile[:, 1, :], wbc, ALU.mult)
                b.wait("tensor", "ve")
                for mh in range(2):
                    for rcl in range(3):
                        b.mm(ps[mh][:, 0:512], Pslab[:, 4 + rcl, :],
                             oute[:, rcl, mh * 512:(mh + 1) * 512],
                             start=(rcl == 0), stop=(rcl == 2))
                b.wait("vector", "pe")
                b.tt(comb[:, tc, 0:512], comb[:, tc, 0:512], ps[0][:, 0:512], ALU.add)
                b.tt(comb[:, tc, 512:1024], comb[:, tc, 512:1024], ps[1][:, 0:512], ALU.add)
            b.barrier()

    xrd = WK[:, 0:2048].rearrange("p (c f) -> p c f", c=2)
    for i in range(NT):
        if i >= 2:
            b.wait("sync", "ve")
        b.dma("sync", xrd[:, i % 2, :], scr_x3[i * 128:(i + 1) * 128, :], f"x3r{i % 2}")
        b.wait_dma("vector", f"x3r{i % 2}")
        b.tt(stream[:, i, :], stream[:, i, :], xrd[:, i % 2, :], ALU.add)
        b.red(C["stats"][:, i:i + 1], stream[:, i, :], op="sum")
    postln_finish(b, C, stream, WK[:, 2048:3072], WK[:, 3072:4096], eps=1e-5)


def emit_s4_dense(nc, b, C, dr, ps, A1, A2, WA, WK, stream, A2_16, scr_x3):
    ident = C["ident"][:]
    stats = C["stats"]
    yT = A2[:].rearrange("p (c t) -> p c t", c=8)

    mrbc = WK[:, 0:1024]
    ytmp = WK[:, 1024:2048]
    sqd = WK[:, 2048:3072]
    lgb = WK[:, 3072:3080]
    lg2 = WK[:, 3080:3088]
    t1 = WK[:, 3088:3096]
    msk1 = WK[:, 3096:3104]
    msk2 = WK[:, 3104:3112]
    b.dma("sync", mrbc, dr["bc_moerms"], "bc0")
    ssqm, rstdm = stats[:, 0:16], stats[:, 16:32]
    m1c, m2c, d12 = stats[:, 32:33], stats[:, 33:34], stats[:, 34:35]
    w1c, w2c = stats[:, 35:36], stats[:, 36:37]
    for i in range(NT):
        b.stt(sqd, stream[:, i, :], 1.0, stream[:, i, :], ALU.mult, ALU.mult,
              accum=ssqm[:, i:i + 1])
    vgaps(b, C)
    rstd_chain(b, C, ssqm, rstdm, 1024, 1e-6)
    b.wait("scalar", "ve")
    b.wait_dma("vector", "bc0")
    for i in range(NT):
        b.wait("scalar", "ve")
        b.wait("scalar", "pe")
        b.act(ytmp, stream[:, i, :], AF.Identity, scale=rstdm[:, i:i + 1])
        b.wait("vector", "se")
        b.tt(ytmp, ytmp, mrbc, ALU.mult)
        b.wait("tensor", "ve")
        items = [(ytmp[:, c * 128:(c + 1) * 128], yT[:, c, i * 128:(i + 1) * 128])
                 for c in range(8)]
        tp_batch(b, C, ps, items, banks=(0, 1, 2, 3))
        b.wait("tensor", "ve")
        for c in range(8):
            b.mm(ps[6][:, 0:8], yT[:, c, i * 128:(i + 1) * 128], C["gwT"][:, c, :],
                 start=(c == 0), stop=(c == 7))
        b.wait("vector", "pe")
        b.cp(lgb, ps[6][:, 0:8])
        b.red(m1c, lgb, op="max")
        b.tt(msk1, lgb, m1c.broadcast_to([128, 8]), ALU.is_equal)
        b.stt(lg2, msk1, -1e30, lgb, ALU.mult, ALU.add)
        b.red(m2c, lg2, op="max")
        b.tt(msk2, lg2, m2c.broadcast_to([128, 8]), ALU.is_equal)
        b.tt(d12, m1c, m2c, ALU.subtract)
        b.wait("scalar", "ve")
        b.act(w1c, d12, AF.Sigmoid)
        b.act(w2c, d12, AF.Sigmoid, scale=-1.0)
        b.wait("vector", "se")
        b.ts(t1, msk2, w2c, None, ALU.mult)
        b.stt(C["wfull"][:, i, :], msk1, w1c, t1, ALU.mult, ALU.add)
    b.barrier()

    for i in range(NT):
        b.dma("gpsimd", scr_x3[i * 128:(i + 1) * 128, :], stream[:, i, :], "x3w")
    b.wait_dma("vector", "x3w")
    comb = stream
    for i in range(NT):
        b.memset(comb[:, i, :], 0.0)
    b.barrier()

    NBW = [384] * 7 + [128]
    w1s = WA[:, 0:3072].rearrange("p (c f) -> p c f", c=8)
    w2s = WA[:, 3072:6144].rearrange("p (c f) -> p c f", c=8)
    w3s = WA[:, 6144:9216].rearrange("p (c f) -> p c f", c=3)
    tmps = WK[:, 0:384]
    hb = WK[:, 384:768]
    hTe = WK[:, 768:1152].rearrange("p (c f) -> p c f", c=3)
    for e in range(8):
        for nb in range(8):
            n0 = sum(NBW[:nb])
            NW = NBW[nb]
            nc3 = NW // 128
            b.wait("sync", "pe")
            b.dma("sync", w1s[:, :, 0:NW],
                  dr["moe_w1T"][e, :, n0:n0 + NW].rearrange("(c p) f -> p c f", p=128), "we1")
            b.dma("sync", w2s[:, :, 0:NW],
                  dr["moe_w2T"][e, :, n0:n0 + NW].rearrange("(c p) f -> p c f", p=128), "we2")
            b.dma("sync", w3s[:, 0:nc3, :],
                  dr["moe_w3T"][e, n0:n0 + NW, :].rearrange("(c p) f -> p c f", p=128), "we3")
            b.wait_dma("tensor", "we1")
            b.wait_dma("tensor", "we2")
            b.wait_dma("tensor", "we3")
            pe_mm, se_silu, ve_tt, ve_add = {}, {}, {}, {}

            def stage_a(i):
                gb_, ub_ = (0, 1) if i % 2 == 0 else (5, 6)
                if i >= 2:
                    b.wait("tensor", "se", se_silu[i - 2])
                    b.wait("tensor", "ve", ve_tt[i - 2])
                for c in range(8):
                    b.mm(ps[ub_][:, 0:NW], yT[:, c, i * 128:(i + 1) * 128],
                         w1s[:, c, 0:NW], start=(c == 0), stop=(c == 7))
                for c in range(8):
                    b.mm(ps[gb_][:, 0:NW], yT[:, c, i * 128:(i + 1) * 128],
                         w2s[:, c, 0:NW], start=(c == 0), stop=(c == 7))
                pe_mm[i] = b.n["pe"]

            def stage_b(i):
                gb_, ub_ = (0, 1) if i % 2 == 0 else (5, 6)
                b.wait("scalar", "pe", pe_mm[i])
                b.wait("scalar", "ve")
                b.act(tmps[:, 0:NW], ps[ub_][:, 0:NW], AF.Silu)
                se_silu[i] = b.n["se"]
                b.wait("vector", "se", se_silu[i])
                b.wait("vector", "pe")
                b.tt(hb[:, 0:NW], tmps[:, 0:NW], ps[gb_][:, 0:NW], ALU.mult)
                ve_tt[i] = b.n["ve"]

            def stage_c(i):
                b.wait("tensor", "ve", ve_tt[i])
                if i >= 1:
                    b.wait("tensor", "ve", ve_add[i - 1])
                for c in range(nc3):
                    b.tp(ps[2][:, c * 128:(c + 1) * 128], hb[:, c * 128:(c + 1) * 128], ident)
                b.wait("vector", "pe")
                for c in range(nc3):
                    b.cp(hTe[:, c, :], ps[2][:, c * 128:(c + 1) * 128], drain=False)
                b.wait("tensor", "ve")
                for mh in range(2):
                    for c in range(nc3):
                        b.mm(ps[3 + mh][:, 0:512], hTe[:, c, :],
                             w3s[:, c, mh * 512:(mh + 1) * 512],
                             start=(c == 0), stop=(c == nc3 - 1))
                b.wait("vector", "pe")
                b.stt(comb[:, i, 0:512], ps[3][:, 0:512], C["wfull"][:, i, e:e + 1],
                      comb[:, i, 0:512], ALU.mult, ALU.add)
                b.stt(comb[:, i, 512:1024], ps[4][:, 0:512], C["wfull"][:, i, e:e + 1],
                      comb[:, i, 512:1024], ALU.mult, ALU.add)
                ve_add[i] = b.n["ve"]

            stage_a(0)
            stage_b(0)
            for i in range(1, NT):
                stage_a(i)
                stage_c(i - 1)
                stage_b(i)
            stage_c(NT - 1)
            b.barrier()

    xrd = WK[:, 0:2048].rearrange("p (c f) -> p c f", c=2)
    for i in range(NT):
        if i >= 2:
            b.wait("sync", "ve")
        b.dma("sync", xrd[:, i % 2, :], scr_x3[i * 128:(i + 1) * 128, :], f"x3r{i % 2}")
        b.wait_dma("vector", f"x3r{i % 2}")
        b.tt(stream[:, i, :], stream[:, i, :], xrd[:, i % 2, :], ALU.add)
        b.red(C["stats"][:, i:i + 1], stream[:, i, :], op="sum")
    postln_finish(b, C, stream, WK[:, 2048:3072], WK[:, 3072:4096], eps=1e-5)


def _consts():
    r = np.arange(128)
    cc = {}
    cc["ident"] = np.eye(128, dtype=np.float32)
    cc["mask_causal"] = np.where(r[None, :] > r[:, None], np.float32(MASK_VAL), np.float32(0.0)).astype(np.float32)
    cc["mask_ls"] = np.where(r[None, :] < r[:, None], np.float32(0.0), np.float32(NEG_BIG)).astype(np.float32)
    cc["mask_us"] = np.where(r[None, :] > r[:, None], np.float32(0.0), np.float32(NEG_BIG)).astype(np.float32)
    cc["mask_ui"] = np.where(r[None, :] >= r[:, None], np.float32(0.0), np.float32(NEG_BIG)).astype(np.float32)
    cc["ones_ui"] = np.triu(np.ones((128, 128), dtype=np.float32))
    cc["ones_r"] = np.ones((1, 128), dtype=np.float32)
    cc["ones_col"] = np.ones((128, 1), dtype=np.float32)
    cc["iota_f"] = np.broadcast_to(np.arange(128, dtype=np.float32)[None, :], (128, 128)).copy()
    cc["iota_p"] = np.broadcast_to(np.arange(128, dtype=np.float32)[:, None], (128, 128)).copy()
    cc["e127"] = np.zeros((128, 1), dtype=np.float32); cc["e127"][127, 0] = 1.0
    return cc


def _prep_inputs(inp):
    f = np.float32
    def t(a):
        return np.ascontiguousarray(np.asarray(a, dtype=f).T)
    def c(a):
        return np.ascontiguousarray(np.asarray(a, dtype=f))
    def bc(a, n=D):
        return np.ascontiguousarray(np.broadcast_to(np.asarray(a, dtype=f), (128, n)))

    m = {}
    m["l0_wqT"] = t(inp["L0_attn_q_proj_weight"])
    m["l0_wkT"] = t(inp["L0_attn_k_proj_weight"])
    m["l0_wvT"] = t(inp["L0_attn_v_proj_weight"])
    m["l0_woT"] = t(inp["L0_attn_o_proj_weight"])
    m["l0_wg"] = c(inp["L0_ffn_w_gate"])
    m["l0_wu"] = c(inp["L0_ffn_w_up"])
    m["l0_wd"] = c(inp["L0_ffn_w_down"])
    m["l1_wq"] = c(inp["L1_attn_Wq"]); m["l1_wk"] = c(inp["L1_attn_Wk"])
    m["l1_wv"] = c(inp["L1_attn_Wv"]); m["l1_wg"] = c(inp["L1_attn_Wg"])
    m["l1_wo"] = c(inp["L1_attn_Wo"]); m["l1_w2"] = c(inp["L1_attn_W2"])
    m["l1_wab"] = np.ascontiguousarray(
        np.concatenate([np.asarray(inp["L1_attn_Wa"], f), np.asarray(inp["L1_attn_Wb"], f)], axis=1))
    m["l1_convq"] = np.ascontiguousarray(np.asarray(inp["L1_attn_conv_q_w"], f).reshape(8, 128, 4).transpose(1, 0, 2))
    m["l1_convk"] = np.ascontiguousarray(np.asarray(inp["L1_attn_conv_k_w"], f).reshape(8, 128, 4).transpose(1, 0, 2))
    m["l1_convv"] = np.ascontiguousarray(np.asarray(inp["L1_attn_conv_v_w"], f).reshape(16, 128, 4).transpose(1, 0, 2))
    m["l1_dtb"] = bc(inp["L1_attn_dt_bias"], 4)
    m["l1_alog"] = bc(inp["L1_attn_A_log"], 4)
    m["bc_ln1g"] = bc(inp["L1_attn_ln1_g"]); m["bc_ln1b"] = bc(inp["L1_attn_ln1_b"])
    m["bc_ln2g"] = bc(inp["L1_attn_ln2_g"]); m["bc_ln2b"] = bc(inp["L1_attn_ln2_b"])
    m["bc_rmsin"] = bc(inp["L1_attn_rms_in_w"]); m["bc_b2"] = bc(inp["L1_attn_b2"])
    m["bc_moerms"] = bc(inp["L1_ffn_rms_w"]); m["bc_onorm"] = bc(inp["L1_attn_o_norm_w"], 512)
    gwT = np.asarray(inp["L1_ffn_gate_w"], f).T
    m["moe_gwT"] = np.ascontiguousarray(gwT.reshape(8, 128, 8).transpose(1, 0, 2))
    def img_ldT(wT, n0, nw):
        blk = wT[:, n0:n0 + nw]
        return blk.reshape(8, 128, nw).transpose(1, 0, 2).reshape(128, 8 * nw)
    NBW = [384] * 7 + [128]
    w1T = np.asarray(inp["L1_ffn_W1"], f).transpose(0, 2, 1)
    w2T = np.asarray(inp["L1_ffn_W2"], f).transpose(0, 2, 1)
    w3T = np.asarray(inp["L1_ffn_W3"], f).transpose(0, 2, 1)
    im1, im2, im3 = [], [], []
    for e in range(8):
        r1, r2, r3 = [], [], []
        for nb in range(8):
            n0, nw = sum(NBW[:nb]), NBW[nb]
            pad = 384 - nw
            a = img_ldT(w1T[e], n0, nw)
            bb = img_ldT(w2T[e], n0, nw)
            cc = w3T[e][n0:n0 + nw, :].reshape(nw // 128, 128, 1024).transpose(1, 0, 2).reshape(128, -1)
            if pad:
                a = np.concatenate([a, np.zeros((128, 8 * pad), f)], 1)
                bb = np.concatenate([bb, np.zeros((128, 8 * pad), f)], 1)
                cc = np.concatenate([cc, np.zeros((128, (pad // 128) * 1024), f)], 1)
            r1.append(a); r2.append(bb); r3.append(cc)
        im1.append(np.stack(r1)); im2.append(np.stack(r2)); im3.append(np.stack(r3))
    m["moe_w1i"] = np.ascontiguousarray(np.stack(im1))
    m["moe_w2i"] = np.ascontiguousarray(np.stack(im2))
    m["moe_w3i"] = np.ascontiguousarray(np.stack(im3))
    l0p = []
    for p in range(8):
        cols = slice(p * 128, (p + 1) * 128)
        l0p.append(np.concatenate([
            m["l0_wqT"][:, cols].reshape(8, 128, 128).transpose(1, 0, 2).reshape(128, 1024),
            m["l0_wkT"][:, cols].reshape(8, 128, 128).transpose(1, 0, 2).reshape(128, 1024),
            m["l0_wvT"][:, cols].reshape(8, 128, 128).transpose(1, 0, 2).reshape(128, 1024)], 1))
    m["l0_pairs"] = np.ascontiguousarray(np.stack(l0p))
    wd = m["l0_wd"]
    m["l0_wdi"] = np.ascontiguousarray(np.stack(
        [wd[:, nb * 256:(nb + 1) * 256].reshape(8, 128, 256).transpose(1, 0, 2).reshape(128, 2048)
         for nb in range(11)]))
    m.update(_consts())
    return m


def kernel(**inputs):
    return _run(inputs, upto="out")


def _run(inputs, upto="out", s1_limit=None, s2_limit=None, start="x0", s3_limit=None, xin=None):
    x0 = np.asarray(inputs["L0_attn_in_features"], dtype=np.float32)
    shared = _prep_inputs(inputs)
    in_maps = []
    for i in range(8):
        mi = dict(shared)
        mi["x0"] = np.ascontiguousarray(x0[i])
        in_maps.append(mi)
    nc, _ = _build(upto=upto, s1_limit=s1_limit, s2_limit=s2_limit, start=start, s3_limit=s3_limit)
    if xin is not None:
        for i in range(8):
            in_maps[i]["xin"] = np.ascontiguousarray(np.asarray(xin[i % len(xin)], np.float32))
    res = run_bass_kernel_spmd(nc, in_maps, core_ids=list(range(8)))
    return np.stack([np.asarray(res.results[i]["out"]) for i in range(8)], axis=0)


# version 06
# speedup vs baseline: 1.0729x; 1.0729x over previous
import numpy as np
from contextlib import ExitStack
import concourse.bass as bass
import concourse.mybir as mybir
from concourse.bass_utils import run_bass_kernel_spmd

F32 = mybir.dt.float32
AF = mybir.ActivationFunctionType
ALU = mybir.AluOpType
AX = mybir.AxisListType.X

T = 2048
D = 1024
NT = 16
MASK_VAL = -1.0e8
NEG_BIG = -1.0e9

FP4_MID = [0.25, 0.75, 1.25, 1.75, 2.5, 3.5, 5.0]
FP4_WGT = [0.5, 0.5, 0.5, 0.5, 1.0, 1.0, 2.0]


class B:
    ENGS = ("tensor", "vector", "scalar", "sync", "gpsimd")

    def __init__(self, nc, ctx):
        self.nc = nc
        self.ctx = ctx
        self.q = {e: [] for e in self.ENGS}
        self.n = {"pe": 0, "ve": 0, "se": 0}
        self.dn = {}
        self.sems = {}
        self.pend = {e: {} for e in self.ENGS}

    def get_sem(self, name):
        if name not in self.sems:
            self.sems[name] = self.ctx.enter_context(self.nc.semaphore(name))
        return self.sems[name]

    def _flush_pend(self, eng, keep_one):
        pend = self.pend[eng]
        emb = None
        if pend:
            items = sorted(pend.items())
            if keep_one:
                emb = items.pop()
            for ch, v in items:
                self.q[eng].append(lambda e, ch=ch, v=v: e.wait_ge(self.sems[ch], v))
            self.pend[eng] = {}
        return emb

    def emit(self, eng, fn, inc):
        emb = self._flush_pend(eng, keep_one=(eng in ("tensor", "vector", "scalar")))
        if inc is not None:
            self.n[inc] += 1
            self.get_sem(inc)
        def thunk(e, fn=fn, inc=inc, emb=emb):
            ins = fn(e)
            if emb is not None:
                ins._wait_ge(self.sems[emb[0]], emb[1])
            if inc is not None:
                ins.then_inc(self.sems[inc], 1)
        self.q[eng].append(thunk)

    def wait(self, eng, ch, val=None):
        v = self.n[ch] if val is None else val
        self.get_sem(ch)
        p = self.pend[eng]
        if p.get(ch, -1) < v:
            p[ch] = v

    def dma(self, queue, out, in_, sem):
        self._flush_pend(queue, keep_one=False)
        self.dn[sem] = self.dn.get(sem, 0) + 16
        self.get_sem(sem)
        def thunk(e, out=out, in_=in_, sem=sem):
            e.dma_start(out=out, in_=in_).then_inc(self.sems[sem], 16)
        self.q[queue].append(thunk)

    def wait_dma(self, eng, sem, val=None):
        v = self.dn[sem] if val is None else val
        self.get_sem(sem)
        p = self.pend[eng]
        if p.get(sem, -1) < v:
            p[sem] = v

    def barrier(self):
        vals = dict(self.n)
        dvals = dict(self.dn)
        for eng in self.ENGS:
            for ch, v in vals.items():
                if v > 0:
                    self.wait(eng, ch, v)
            for sm, v in dvals.items():
                if v > 0:
                    self.wait_dma(eng, sm, v)

    def _drain(self, eng):
        ch = {"vector": "ve", "scalar": "se", "tensor": "pe"}[eng]
        if self.n[ch] > 0:
            self.wait(eng, ch)

    def mm(self, out, lhsT, rhs, start, stop):
        self.emit("tensor", lambda e: e.matmul(out, lhsT, rhs, start=start, stop=stop), "pe")

    def tp(self, out, in_, ident):
        self.emit("tensor", lambda e: e.transpose(out, in_, ident), "pe")

    def cp(self, out, in_, eng="vector", drain=True):
        if drain:
            self._drain(eng)
        ch = "ve" if eng == "vector" else "se"
        if eng == "vector":
            self.emit(eng, lambda e: e.tensor_copy(out, in_), ch)
        else:
            self.emit(eng, lambda e: e.copy(out, in_), ch)

    def tt(self, out, in0, in1, op, drain=True):
        if drain:
            self._drain("vector")
        self.emit("vector", lambda e: e.tensor_tensor(out, in0, in1, op), "ve")

    def ts(self, out, in0, s1, s2, op0, op1=None, accum=None, drain=True):
        if drain:
            self._drain("vector")
        if op1 is None:
            self.emit("vector", lambda e: e.tensor_scalar(out, in0, s1, None, op0, accum_out=accum), "ve")
        else:
            self.emit("vector", lambda e: e.tensor_scalar(out, in0, s1, s2, op0, op1, accum_out=accum), "ve")

    def stt(self, out, in0, scalar, in1, op0, op1, accum=None, drain=True):
        if drain:
            self._drain("vector")
        self.emit("vector", lambda e: e.scalar_tensor_tensor(
            out=out, in0=in0, scalar=scalar, in1=in1, op0=op0, op1=op1, accum_out=accum), "ve")

    def red(self, out, in_, op="max", negate=False, absval=False, drain=True):
        if drain:
            self._drain("vector")
        if op == "max":
            self.emit("vector", lambda e: e.reduce_max(out, in_, axis=AX, negate=negate,
                                                       apply_absolute_value=absval), "ve")
        else:
            self.emit("vector", lambda e: e.reduce_sum(out, in_, axis=AX, negate=negate,
                                                       apply_absolute_value=absval), "ve")

    def recip(self, out, in_, drain=True):
        if drain:
            self._drain("vector")
        self.emit("vector", lambda e: e.reciprocal(out, in_), "ve")

    def memset(self, ap, val, drain=False):
        if drain:
            self._drain("vector")
        self.emit("vector", lambda e: e.memset(ap, val), "ve")

    def act(self, out, in_, func, bias=0.0, scale=1.0, accum=None, drain=True):
        if drain:
            self._drain("scalar")
        self.emit("scalar", lambda e: e.activation(out, in_, func, bias=bias, scale=scale,
                                                   accum_out=accum), "se")

    def smul(self, out, in_, mul, drain=True):
        if drain:
            self._drain("scalar")
        self.emit("scalar", lambda e: e.mul(out, in_, mul), "se")



def vgaps(b, C):
    b.wait("vector", "ve")


def sgaps(b, C):
    b.wait("scalar", "se")


def tp_batch(b, C, ps, items, banks=(0, 1, 2, 3)):
    ident = C["ident"][:]
    nb = len(banks)
    ve_cp = {}
    b.wait("tensor", "ve")
    for k, (src, dst) in enumerate(items):
        slot = ps[banks[k % nb]][:, (k // nb % 4) * 128:(k // nb % 4) * 128 + 128]
        if k >= nb:
            b.wait("tensor", "ve", ve_cp[k - nb])
        b.tp(slot, src, ident)
        b.wait("vector", "pe")
        b.cp(dst, slot, drain=False)
        ve_cp[k] = b.n["ve"]


def bcast_col(b, C, ps, col_ap, dst_tile, bank=7):
    ident = C["ident"][:]
    b.wait("tensor", "ve")
    b.tp(ps[bank][0:1, 0:128], col_ap, ident)
    b.wait("vector", "pe")
    b.cp(C["rowb"][0:1, :], ps[bank][0:1, 0:128], drain=False)
    b.wait("tensor", "ve")
    b.mm(ps[bank][:, 128:256], C["ones_r"][:], C["rowb"][0:1, :], start=True, stop=True)
    b.wait("vector", "pe")
    b.cp(dst_tile, ps[bank][:, 128:256], drain=False)


def rstd_chain(b, C, v_col, out_col, n_feats, eps, extra_scale=None, is_sum=False):
    t1, t2 = C["st_t1"], C["st_t2"]
    w = v_col.shape[1]
    if is_sum:
        b.ts(v_col, v_col, 1.0, eps, ALU.mult, ALU.add)
    else:
        b.ts(v_col, v_col, 1.0 / n_feats, eps, ALU.mult, ALU.add)
    b.wait("scalar", "ve")
    b.act(t1[:, 0:w], v_col, AF.Sqrt)
    b.wait("vector", "se")
    b.recip(out_col, t1[:, 0:w])
    vgaps(b, C)
    for _ in range(2):
        b.tt(t1[:, 0:w], out_col, out_col, ALU.mult)
        vgaps(b, C)
        b.tt(t1[:, 0:w], t1[:, 0:w], v_col, ALU.mult)
        vgaps(b, C)
        b.ts(t1[:, 0:w], t1[:, 0:w], -0.5, 1.5, ALU.mult, ALU.add)
        vgaps(b, C)
        b.tt(out_col, out_col, t1[:, 0:w], ALU.mult)
        vgaps(b, C)
    if extra_scale is not None:
        b.ts(out_col, out_col, extra_scale, None, ALU.mult)
        vgaps(b, C)


def fp4_quant(b, C, x_ap, out_ap, width, wrk):
    G = width // 16
    absx, cmp = wrk["absx"], wrk["cmp"]
    mx, scale = wrk["mx"], wrk["scale"]
    b.wait("scalar", "ve")
    b.act(absx, x_ap, AF.Abs)
    b.wait("vector", "se")
    xv = absx.rearrange("p (g c) -> p g c", c=16)
    b.red(mx, xv, op="max")
    b.ts(scale, mx, 1.0 / 6.0, 1e-12, ALU.mult, ALU.max)
    rsc = wrk["thr"]
    b.recip(rsc, scale)
    rs_b = rsc.unsqueeze(2).broadcast_to([128, G, 16])
    sc_b = scale.unsqueeze(2).broadcast_to([128, G, 16])
    b.tt(xv, xv, rs_b, ALU.mult)
    b.memset(out_ap, 0.0)
    for j in range(7):
        b.ts(cmp, absx, FP4_MID[j], None, ALU.is_gt)
        b.stt(out_ap, cmp, FP4_WGT[j], out_ap, ALU.mult, ALU.add)
    b.wait("scalar", "ve")
    b.act(cmp, x_ap, AF.Sign)
    b.wait("vector", "se")
    b.tt(out_ap, out_ap, cmp, ALU.mult)
    ov = out_ap.rearrange("p (g c) -> p g c", c=16)
    b.tt(ov, ov, sc_b, ALU.mult)


def _build(upto="out", s1_limit=None, s2_limit=None, start="x0", s3_limit=None):
    nc = bass.Bass("TRN2", target_bir_lowering=False)
    b = B(None, None)

    dr = {}
    def din(name, shape):
        dr[name] = nc.dram_tensor(name, shape, F32, kind="ExternalInput").ap()
        return dr[name]
    def dint(name, shape):
        dr[name] = nc.dram_tensor(name, shape, F32, kind="Internal").ap()
        return dr[name]

    x0_d = din("x0", [T, D])
    xin_d = nc.dram_tensor("xin", [T, D], F32, kind="ExternalInput").ap() if start != "x0" else None
    din("l0_woT", [D, D])
    din("l0_wg", [2816, D]); din("l0_wu", [2816, D])
    din("l1_wq", [D, 1024]); din("l1_wk", [D, 1024]); din("l1_wv", [D, 2048])
    din("l1_wg", [D, 2048]); din("l1_wo", [2048, D]); din("l1_w2", [D, D])
    din("l1_wab", [D, 8])
    din("l1_convq", [128, 8, 4]); din("l1_convk", [128, 8, 4]); din("l1_convv", [128, 16, 4])
    din("l1_dtb", [128, 4]); din("l1_alog", [128, 4])
    din("bc_ln1g", [128, D]); din("bc_ln1b", [128, D]); din("bc_ln2g", [128, D])
    din("bc_ln2b", [128, D]); din("bc_rmsin", [128, D]); din("bc_b2", [128, D])
    din("bc_moerms", [128, D]); din("bc_onorm", [128, 512])
    din("moe_gwT", [128, 8, 8])
    din("moe_w1i", [8, 8, 128, 3072]); din("moe_w2i", [8, 8, 128, 3072])
    din("moe_w3i", [8, 8, 128, 3072])
    din("l0_pairs", [8, 128, 3072]); din("l0_wdi", [11, 128, 2048])
    din("ones_col", [128, 1]); din("iota_f", [128, 128]); din("iota_p", [128, 128])
    din("e127", [128, 1])
    din("ident", [128, 128]); din("mask_causal", [128, 128])
    din("mask_ls", [128, 128]); din("mask_us", [128, 128]); din("mask_ui", [128, 128])
    din("ones_ui", [128, 128]); din("ones_r", [1, 128])

    out_d = nc.dram_tensor("out", [T, D], F32, kind="ExternalOutput").ap()

    scr_x1qT = dint("scr_x1qT", [NT, 128, 1024])
    scr_x2 = dint("scr_x2", [T, D])
    scr_qT = dint("scr_qT", [NT, 128, 1024])
    scr_kT = dint("scr_kT", [NT, 128, 1024])
    scr_vhat = dint("scr_vhat", [T, 2048])
    scr_sgate = dint("scr_sgate", [T, 2048])
    scr_osc = dint("scr_osc", [T, 2048])
    scr_x3 = dint("scr_x3", [T, D])

    with ExitStack() as ctx:
        def sb(name, shape, dt=F32):
            return ctx.enter_context(nc.sbuf_tensor(name, shape, dt))

        A1 = sb("A1", [128, 16384])
        A2 = sb("A2", [128, 16384])
        WA = sb("WA", [128, 9216])
        WK = sb("WK", [128, 8704])
        C = {}
        C["ident"] = sb("c_ident", [128, 128])
        C["mask_causal"] = sb("c_mcz", [128, 128])
        C["mask_ls"] = sb("c_mls", [128, 128])
        C["mask_us"] = sb("c_mus", [128, 128])
        C["mask_ui"] = sb("c_mui", [128, 128])
        C["ones_ui"] = sb("c_oui", [128, 128])
        C["ones_r"] = sb("c_or", [1, 128])
        C["junk1"] = sb("c_j1", [128, 4])
        C["junk2"] = sb("c_j2", [128, 4])
        C["junkS1"] = sb("c_js1", [128, 4])
        C["junkS2"] = sb("c_js2", [128, 4])
        stats = sb("c_stats", [128, 96])
        C["stats"] = stats
        C["st_t1"] = stats[:, 64:80]
        C["st_t2"] = stats[:, 80:96]
        C["gb"] = sb("c_gb", [128, 16, 8])
        C["wfull"] = sb("c_wfull", [128, 16, 8])
        C["gwT"] = sb("c_gwT", [128, 8, 8])
        C["convq"] = sb("c_convq", [128, 8, 4])
        C["convk"] = sb("c_convk", [128, 8, 4])
        C["convv"] = sb("c_convv", [128, 16, 4])
        C["coef"] = sb("c_coef", [128, 4])
        C["dtb"] = sb("c_dtb", [128, 4])
        C["ones_col"] = sb("c_onescol", [128, 1])
        C["rowb"] = sb("c_rowb", [1, 128])
        C["iota_f"] = sb("c_iotaf", [128, 128])
        C["iota_p"] = sb("c_iotap", [128, 128])
        C["e127"] = sb("c_e127", [128, 1])
        C["rank"] = sb("c_rank", [128, 16, 8])
        C["offrow"] = sb("c_offrow", [1, 8])
        C["wab"] = sb("c_wab", [128, 8, 8])

        ps = [ctx.enter_context(nc.psum_tensor(f"ps{i}", [128, 512], F32)) for i in range(8)]

        b.nc, b.ctx = nc, ctx
        blk = ctx.enter_context(nc.Block())

        ident = C["ident"]

        for nm in ("ident", "mask_causal", "mask_ls", "mask_us", "mask_ui", "ones_ui"):
            b.dma("sync", C[nm][:], dr[nm], "c0")
        b.dma("sync", C["ones_r"][:], dr["ones_r"], "c0")
        b.dma("sync", C["gwT"][:], dr["moe_gwT"], "c0")
        b.dma("sync", C["convq"][:], dr["l1_convq"], "c0")
        b.dma("sync", C["convk"][:], dr["l1_convk"], "c0")
        b.dma("sync", C["convv"][:], dr["l1_convv"], "c0")
        b.dma("sync", C["dtb"][:], dr["l1_dtb"], "c0")
        b.dma("sync", C["coef"][:], dr["l1_alog"], "c0")
        b.dma("sync", C["ones_col"][:], dr["ones_col"], "c0")
        b.dma("sync", C["iota_f"][:], dr["iota_f"], "c0")
        b.dma("sync", C["iota_p"][:], dr["iota_p"], "c0")
        b.dma("sync", C["e127"][:], dr["e127"], "c0")
        b.dma("sync", C["wab"][:], dr["l1_wab"][:].rearrange("(c p) f -> p c f", p=128), "c0")
        b.wait_dma("vector", "c0")
        b.memset(C["junk1"][:], 0.0)
        b.memset(C["junkS1"][:], 0.0)
        b.wait_dma("scalar", "c0")
        b.act(C["coef"][:], C["coef"][:], AF.Exp, scale=1.0)
        b.wait("vector", "se")
        b.ts(C["coef"][:], C["coef"][:], -1.0, None, ALU.mult)
        b.barrier()

        x0T = A1[:].rearrange("p (c t) -> p c t", c=8)
        stream = A1[:].rearrange("p (c t) -> p c t", c=16)
        aoT = A2[:].rearrange("p (c t) -> p c t", c=8)
        A2_16 = A2[:].rearrange("p (c t) -> p c t", c=16)

        if start == "x0":
            emit_s1(nc, b, C, dr, ps, A1, A2, WA, WK, x0T, stream, aoT, x0_d, limit=s1_limit)
        else:
            xld = WK[:, 0:2048].rearrange("p (c f) -> p c f", c=2)
            for i in range(NT):
                b.dma("sync", xld[:, i % 2, :], xin_d[i * 128:(i + 1) * 128, :], f"xi{i % 2}")
                b.wait_dma("vector", f"xi{i % 2}")
                b.cp(stream[:, i, :], xld[:, i % 2, :], drain=False)
                b.dma("gpsimd", scr_x2[i * 128:(i + 1) * 128, :], xin_d[i * 128:(i + 1) * 128, :], "x2w")
                if i < NT - 2:
                    b.wait("sync", "ve")
            b.barrier()

        if upto == "x1":
            finish(b, stream, out_d)
            run_block(b, blk)
            return nc, b

        if start == "x0":
            emit_s2(nc, b, C, dr, ps, A1, A2, WA, WK, stream, A2_16, scr_x1qT, scr_x2, limit=s2_limit)

        if upto == "x2":
            finish(b, stream, out_d)
            run_block(b, blk)
            return nc, b

        if start in ("x0", "x2"):
            emit_s3(nc, b, C, dr, ps, A1, A2, WA, WK, stream, A2_16,
                    scr_x2, scr_qT, scr_kT, scr_vhat, scr_sgate, scr_osc, limit=s3_limit)

        if upto == "x3":
            finish(b, stream, out_d)
            run_block(b, blk)
            return nc, b

        emit_s4(nc, b, C, dr, ps, A1, A2, WA, WK, stream, A2_16, scr_x3)

        finish(b, stream, out_d)
        run_block(b, blk)
    return nc, b


def finish(b, stream, out_d):
    b.barrier()
    for i in range(NT):
        b.dma("gpsimd", out_d[i * 128:(i + 1) * 128, :], stream[:, i, :], "outw")
    b.wait_dma("gpsimd", "outw")


def run_block(b, blk):
    for eng in B.ENGS:
        b._flush_pend(eng, keep_one=False)
    for eng in B.ENGS:
        thunks = b.q[eng]
        getattr(blk, eng)(lambda e, thunks=thunks: [t(e) for t in thunks])


def emit_s1(nc, b, C, dr, ps, A1, A2, WA, WK, x0T, stream, aoT, x0_d, limit=None):
    ident = C["ident"][:]
    Vp = WK[:, 0:2048].rearrange("p (c t) -> p c t", c=16)
    srow = WK[:, 2048:4096]
    prow = WK[:, 4096:6144]
    ptslab = WK[:, 6144:8192].rearrange("p (c t) -> p c t", c=16)
    avo2 = WK[:, 8192:8320]
    xtile = WK[:, 0:2048].rearrange("p (c t) -> p c t", c=2)
    wqb = WA[:, 0:1024].rearrange("p (c t) -> p c t", c=8)
    wkb = WA[:, 1024:2048].rearrange("p (c t) -> p c t", c=8)
    wvb = WA[:, 2048:3072].rearrange("p (c t) -> p c t", c=8)
    QT = WA[:, 3072:5120]
    KT = WA[:, 5120:7168]
    woT = WA[:, 0:8192].rearrange("p (c t) -> p c t", c=8)

    ve_tile = {}
    for i in range(NT):
        if i >= 2:
            b.wait("sync", "ve", ve_tile[i - 2])
        b.dma("sync", xtile[:, i % 2, :], x0_d[i * 128:(i + 1) * 128, :], f"xt{i % 2}")
        b.wait_dma("tensor", f"xt{i % 2}")
        if i >= 2:
            b.wait("tensor", "ve", ve_tile[i - 2])
        bk = 2 * (i % 2)
        for c in range(8):
            slot = ps[bk + c // 4][:, (c % 4) * 128:(c % 4) * 128 + 128]
            b.tp(slot, xtile[:, i % 2, c * 128:(c + 1) * 128], ident)
        b.wait("vector", "pe")
        for c in range(8):
            slot = ps[bk + c // 4][:, (c % 4) * 128:(c % 4) * 128 + 128]
            b.cp(x0T[:, c, i * 128:(i + 1) * 128], slot, drain=False)
        ve_tile[i] = b.n["ve"]
    b.barrier()
    if limit == "p0":
        return

    for p in range(8):
        if p > 0:
            b.wait("sync", "pe")
        b.dma("sync", WA[:, 0:3072].rearrange("p (c f) -> p c f", c=24),
              dr["l0_pairs"][p].rearrange("p (c f) -> p c f", c=24), "wpair")
        b.wait_dma("tensor", "wpair")

        for dst, wb, psl in ((QT, wqb, (0, 1)), (KT, wkb, (2, 3))):
            for n in range(4):
                pp = ps[psl[n % 2]]
                if n >= 2:
                    b.wait("tensor", "ve")
                for c in range(8):
                    b.mm(pp[:, 0:512], wb[:, c, :], x0T[:, c, n * 512:(n + 1) * 512],
                         start=(c == 0), stop=(c == 7))
                b.wait("vector", "pe")
                b.cp(dst[:, n * 512:(n + 1) * 512], pp[:, 0:512])
        if limit == "pqk" and p == 0:
            b.barrier()
            return
        for i in range(NT):
            pp = ps[4 + (i % 2)]
            if i >= 2:
                b.wait("tensor", "ve")
            for c in range(8):
                b.mm(pp[:, 0:128], x0T[:, c, i * 128:(i + 1) * 128], wvb[:, c, :],
                     start=(c == 0), stop=(c == 7))
            b.wait("vector", "pe")
            b.cp(Vp[:, i, :], pp[:, 0:128])
        b.barrier()
        if limit == "pv" and p == 0:
            return

        for i in range(NT):
            L = (i + 1) * 128
            for hh in range(2):
                hsl = slice(hh * 64, hh * 64 + 64)
                b.wait("tensor", "se")
                b.wait("tensor", "ve")
                nkg = (L + 511) // 512
                for kg in range(nkg):
                    nk = min(512, L - kg * 512)
                    pp = ps[6 + (kg % 2)]
                    if kg >= 2:
                        b.wait("tensor", "ve")
                    b.mm(pp[:, 0:nk], QT[hsl, i * 128:(i + 1) * 128],
                         KT[hsl, kg * 512:kg * 512 + nk], start=True, stop=True)
                    b.wait("vector", "pe")
                    b.ts(srow[:, kg * 512:kg * 512 + nk], pp[:, 0:nk], 0.125, None, ALU.mult)
                b.tt(srow[:, i * 128:L], srow[:, i * 128:L], C["mask_causal"][:], ALU.add)
                nmx = C["stats"][:, 0:1]
                ssum = C["stats"][:, 1:2]
                rsum = C["stats"][:, 2:3]
                b.red(nmx, srow[:, 0:L], op="max", negate=True)
                b.wait("scalar", "ve")
                b.act(prow[:, 0:L], srow[:, 0:L], AF.Exp, bias=nmx, accum=ssum)
                b.wait("vector", "se")
                b.recip(rsum, ssum)
                if limit == f"pa{i}s" and p == 0 and hh == 0:
                    b.barrier()
                    return
                b.wait("tensor", "se")
                ve_copy = {}
                for j in range(i + 1):
                    slot = ps[j % 4][:, (j // 4) * 128:(j // 4) * 128 + 128]
                    if j >= 4:
                        b.wait("tensor", "ve", ve_copy[j - 4])
                    b.tp(slot, prow[:, j * 128:(j + 1) * 128], ident)
                    b.wait("vector", "pe")
                    b.cp(ptslab[:, j, :], slot, drain=False)
                    ve_copy[j] = b.n["ve"]
                b.wait("tensor", "ve")
                if limit == f"pa{i}t" and p == 0 and hh == 0:
                    b.barrier()
                    return
                for j in range(i + 1):
                    b.mm(ps[6][:, 0:64], ptslab[:, j, :], Vp[:, j, hsl],
                         start=(j == 0), stop=(j == i))
                b.wait("scalar", "pe")
                b.smul(avo2[:, hh * 64:hh * 64 + 64], ps[6][:, 0:64], rsum)
                if limit == f"pa{i}v" and p == 0 and hh == 0:
                    b.barrier()
                    return
            b.wait("tensor", "se")
            b.wait("tensor", "ve")
            b.tp(ps[7][:, 0:128], avo2[:], ident)
            b.wait("vector", "pe")
            b.cp(aoT[:, p, i * 128:(i + 1) * 128], ps[7][:, 0:128])
            if limit is not None and limit.startswith("pa") and limit[2:].isdigit() and p == 0 and i == int(limit[2:]):
                b.barrier()
                return
        if limit == "p1one" and p == 0:
            b.barrier()
            return
    b.barrier()
    if limit == "p1":
        return

    b.dma("sync", woT[:], dr["l0_woT"][:].rearrange("(c p2) f -> p2 c f", p2=128), "wbig")
    ztmp = WK[:, 0:1024]
    xt0 = WK[:, 1024:3072].rearrange("p (c t) -> p c t", c=2)
    sqd = WK[:, 3072:4096]
    b.wait_dma("tensor", "wbig")
    ve_chunk = {}
    for i in range(NT):
        if i >= 2:
            b.wait("sync", "ve", ve_chunk[i - 2])
        b.dma("sync", xt0[:, i % 2, :], x0_d[i * 128:(i + 1) * 128, :], f"xt{i % 2}")
        if i >= 1:
            b.wait("tensor", "ve")
        for mh in range(2):
            pp = ps[mh]
            for c in range(8):
                b.mm(pp[:, 0:512], aoT[:, c, i * 128:(i + 1) * 128],
                     woT[:, c, mh * 512:(mh + 1) * 512], start=(c == 0), stop=(c == 7))
        b.wait("vector", "pe")
        b.wait_dma("vector", f"xt{i % 2}")
        b.tt(stream[:, i, 0:512], ps[0][:, 0:512], xt0[:, i % 2, 0:512], ALU.add)
        b.tt(stream[:, i, 512:1024], ps[1][:, 0:512], xt0[:, i % 2, 512:1024], ALU.add)
        b.red(C["stats"][:, i:i + 1], stream[:, i, :], op="sum")
        ve_chunk[i] = b.n["ve"]
    postln_finish(b, C, stream, ztmp, sqd, eps=1e-5)


def postln_finish(b, C, stream, ztmp, sqd, eps):
    stats = C["stats"]
    zsum, ssq = stats[:, 0:16], stats[:, 16:32]
    nmean, rstd = stats[:, 32:48], stats[:, 48:64]
    vgaps(b, C)
    b.ts(nmean, zsum, -1.0 / 1024.0, None, ALU.mult)
    b.wait("scalar", "ve")
    for i in range(NT):
        b.act(ztmp, stream[:, i, :], AF.Identity, bias=nmean[:, i:i + 1])
        b.cp(stream[:, i, :], ztmp, eng="scalar")
        b.wait("vector", "se")
        b.stt(sqd, stream[:, i, :], 1.0, stream[:, i, :], ALU.mult, ALU.mult,
              accum=ssq[:, i:i + 1])
    vgaps(b, C)
    rstd_chain(b, C, ssq, rstd, 1024, eps)
    b.wait("scalar", "ve")
    for i in range(NT):
        b.act(ztmp, stream[:, i, :], AF.Identity, scale=rstd[:, i:i + 1])
        b.cp(stream[:, i, :], ztmp, eng="scalar")
    b.barrier()


def emit_s2(nc, b, C, dr, ps, A1, A2, WA, WK, stream, A2_16, scr_x1qT, scr_x2, limit=None):
    ident = C["ident"][:]
    NB, NW = 11, 256

    xq = WK[:, 0:1024]
    wrk = {"absx": WK[:, 1024:2048], "cmp": WK[:, 2048:3072],
           "mx": WK[:, 3072:3136], "scale": WK[:, 3136:3200], "thr": WK[:, 3200:3264]}
    xqTb = WK[:, 3264:5312].rearrange("p (s c f) -> p s c f", s=2, c=8)
    ve_done = {}
    for i in range(NT):
        b.wait("vector", "pe")
        fp4_quant(b, C, stream[:, i, :], xq, 1024, wrk)
        bk = 2 * (i % 2)
        if i >= 2:
            b.wait("tensor", "ve", ve_done[i - 2])
        b.wait("tensor", "ve")
        for c in range(8):
            slot = ps[bk + c // 4][:, (c % 4) * 128:(c % 4) * 128 + 128]
            b.tp(slot, xq[:, c * 128:(c + 1) * 128], ident)
        b.wait("vector", "pe")
        if i >= 2:
            b.wait_dma("vector", f"xqw{i % 2}")
        for c in range(8):
            slot = ps[bk + c // 4][:, (c % 4) * 128:(c % 4) * 128 + 128]
            b.cp(xqTb[:, i % 2, c, :], slot, drain=False)
        ve_done[i] = b.n["ve"]
        b.wait("gpsimd", "ve", ve_done[i])
        b.dma("gpsimd", scr_x1qT[i].rearrange("p (c f) -> p c f", c=8),
              xqTb[:, i % 2, :, :], f"xqw{i % 2}")
    b.barrier()
    if limit == "q0":
        return

    for i in range(NT):
        b.memset(A2_16[:, i, :], 0.0)
    b.barrier()

    wrows = WK[:, 0:2048]
    wqo = WK[:, 2048:4096]
    wrkw = {"absx": WK[:, 4096:6144], "cmp": WK[:, 6144:8192],
            "mx": WK[:, 8192:8320], "scale": WK[:, 8320:8448], "thr": WK[:, 8448:8576]}
    wgqT = WA[:, 0:2048].rearrange("p (c f) -> p c f", c=8)
    wuqT = WA[:, 2048:4096].rearrange("p (c f) -> p c f", c=8)
    wdqT = WA[:, 4096:6144].rearrange("p (c f) -> p c f", c=2)
    for nb in range(NB):
        n0 = nb * NW
        for wi, (src, dstT) in enumerate(
                ((dr["l0_wg"], wgqT), (dr["l0_wu"], wuqT), (None, wdqT))):
            b.wait("sync", "ve")
            b.wait("sync", "se")
            if wi < 2:
                b.dma("sync", wrows.rearrange("p (c f) -> p c f", c=2),
                      src[n0:n0 + NW, :].rearrange("(c p) f -> p c f", p=128), "wrow")
            else:
                b.dma("sync", wrows.rearrange("p (c f) -> p c f", c=8),
                      dr["l0_wdi"][nb].rearrange("p (c f) -> p c f", c=8), "wrow")
            b.wait_dma("vector", "wrow")
            b.wait_dma("scalar", "wrow")
            b.wait("vector", "pe")
            fp4_quant(b, C, wrows, wqo, 2048, wrkw)
            b.wait("tensor", "ve")
            if wi < 2:
                items = [(wqo[:, r * 1024 + c * 128: r * 1024 + (c + 1) * 128],
                          dstT[:, c, r * 128:(r + 1) * 128])
                         for r in range(2) for c in range(8)]
            else:
                items = [(wqo[:, c * 256 + nn * 128: c * 256 + (nn + 1) * 128],
                          dstT[:, nn, c * 128:(c + 1) * 128])
                         for c in range(8) for nn in range(2)]
            tp_batch(b, C, ps, items)
        b.barrier()
        if limit is not None and limit == f"w{nb}":
            return

        xqTs = WK[:, 0:2048].rearrange("p (s c f) -> p s c f", s=2, c=8)
        hb = WK[:, 2048:2304]
        tmps = WK[:, 2304:2560]
        hq = WK[:, 2560:2816]
        hqT = WK[:, 2816:3072].rearrange("p (c f) -> p c f", c=2)
        wrkh = {"absx": WK[:, 3072:3328], "cmp": WK[:, 3328:3584],
                "mx": WK[:, 3584:3600], "scale": WK[:, 3600:3616], "thr": WK[:, 3616:3632]}
        pe_mm, se_silu, ve_tt, ve_q, ve_add = {}, {}, {}, {}, {}

        def stage_a(i):
            gb, ub = (0, 1) if i % 2 == 0 else (5, 6)
            if i >= 2:
                b.wait("sync", "pe", pe_mm[i - 2])
            b.dma("sync", xqTs[:, i % 2, :, :],
                  scr_x1qT[i].rearrange("p (c f) -> p c f", c=8), f"xqr{i % 2}")
            b.wait_dma("tensor", f"xqr{i % 2}")
            if i >= 2:
                b.wait("tensor", "se", se_silu[i - 2])
                b.wait("tensor", "ve", ve_tt[i - 2])
            for c in range(8):
                b.mm(ps[gb][:, 0:NW], xqTs[:, i % 2, c, :], wgqT[:, c, :],
                     start=(c == 0), stop=(c == 7))
            for c in range(8):
                b.mm(ps[ub][:, 0:NW], xqTs[:, i % 2, c, :], wuqT[:, c, :],
                     start=(c == 0), stop=(c == 7))
            pe_mm[i] = b.n["pe"]

        def stage_b(i):
            gb, ub = (0, 1) if i % 2 == 0 else (5, 6)
            b.wait("scalar", "pe", pe_mm[i])
            b.wait("scalar", "ve")
            b.act(tmps, ps[gb][:, 0:NW], AF.Silu)
            se_silu[i] = b.n["se"]
            b.wait("vector", "se", se_silu[i])
            b.tt(hb, tmps, ps[ub][:, 0:NW], ALU.mult)
            ve_tt[i] = b.n["ve"]
            if i >= 1:
                b.wait("vector", "pe")
            fp4_quant(b, C, hb, hq, NW, wrkh)
            ve_q[i] = b.n["ve"]

        def stage_c(i):
            b.wait("tensor", "ve", ve_q[i])
            if i >= 1:
                b.wait("tensor", "ve", ve_add[i - 1])
            for c in range(2):
                b.tp(ps[2][:, c * 128:(c + 1) * 128], hq[:, c * 128:(c + 1) * 128], ident)
            b.wait("vector", "pe")
            b.cp(hqT[:, 0, :], ps[2][:, 0:128], drain=False)
            b.cp(hqT[:, 1, :], ps[2][:, 128:256], drain=False)
            b.wait("tensor", "ve")
            for mh in range(2):
                for c in range(2):
                    b.mm(ps[3 + mh][:, 0:512], hqT[:, c, :],
                         wdqT[:, c, mh * 512:(mh + 1) * 512], start=(c == 0), stop=(c == 1))
            b.wait("vector", "pe")
            b.tt(A2_16[:, i, 0:512], A2_16[:, i, 0:512], ps[3][:, 0:512], ALU.add)
            b.tt(A2_16[:, i, 512:1024], A2_16[:, i, 512:1024], ps[4][:, 0:512], ALU.add)
            ve_add[i] = b.n["ve"]

        stage_a(0)
        stage_b(0)
        for i in range(1, NT):
            stage_a(i)
            stage_c(i - 1)
            stage_b(i)
        stage_c(NT - 1)
        b.barrier()
        if limit is not None and limit == f"n{nb}":
            return

    for i in range(NT):
        b.tt(stream[:, i, :], A2_16[:, i, :], stream[:, i, :], ALU.add)
        b.red(C["stats"][:, i:i + 1], stream[:, i, :], op="sum")
    postln_finish(b, C, stream, WK[:, 0:1024], WK[:, 1024:2048], eps=1e-5)
    for i in range(NT):
        b.dma("gpsimd", scr_x2[i * 128:(i + 1) * 128, :], stream[:, i, :], "x2w")
    b.barrier()


def emit_s3(nc, b, C, dr, ps, A1, A2, WA, WK, stream, A2_16,
            scr_x2, scr_qT, scr_kT, scr_vhat, scr_sgate, scr_osc, limit=None):
    ident = C["ident"][:]
    stats = C["stats"]
    hT = A2[:].rearrange("p (c t) -> p c t", c=8)

    g1bc = WK[:, 0:1024]
    b1bc = WK[:, 1024:2048]
    rmbc = WK[:, 2048:3072]
    ztmp = WK[:, 3072:4096]
    sqd = WK[:, 4096:5120]
    b.dma("sync", g1bc, dr["bc_ln1g"], "bc0")
    b.dma("sync", b1bc, dr["bc_ln1b"], "bc1")
    b.dma("sync", rmbc, dr["bc_rmsin"], "bc2")
    zsum, ssq = stats[:, 0:16], stats[:, 16:32]
    nmean, rstd = stats[:, 32:48], stats[:, 48:64]
    for i in range(NT):
        b.red(zsum[:, i:i + 1], stream[:, i, :], op="sum")
    vgaps(b, C)
    b.ts(nmean, zsum, -1.0 / 1024.0, None, ALU.mult)
    b.wait("scalar", "ve")
    for i in range(NT):
        b.act(ztmp, stream[:, i, :], AF.Identity, bias=nmean[:, i:i + 1])
        b.cp(stream[:, i, :], ztmp, eng="scalar")
        b.wait("vector", "se")
        b.stt(sqd, stream[:, i, :], 1.0, stream[:, i, :], ALU.mult, ALU.mult,
              accum=ssq[:, i:i + 1])
    vgaps(b, C)
    rstd_chain(b, C, ssq, rstd, 1024, 1e-5)
    b.wait("scalar", "ve")
    b.wait_dma("vector", "bc0")
    b.wait_dma("vector", "bc1")
    b.wait_dma("vector", "bc2")
    ssq2, rstd2 = stats[:, 0:16], stats[:, 16:32]
    for i in range(NT):
        b.wait("scalar", "ve")
        b.act(ztmp, stream[:, i, :], AF.Identity, scale=rstd[:, i:i + 1])
        b.wait("vector", "se")
        b.tt(ztmp, ztmp, g1bc, ALU.mult)
        b.tt(ztmp, ztmp, b1bc, ALU.add)
        b.cp(stream[:, i, :], ztmp)
        b.stt(sqd, stream[:, i, :], 1.0, stream[:, i, :], ALU.mult, ALU.mult,
              accum=ssq2[:, i:i + 1])
    vgaps(b, C)
    rstd_chain(b, C, ssq2, rstd2, 1024, 1e-6)
    b.wait("scalar", "ve")
    for i in range(NT):
        b.wait("scalar", "ve")
        b.wait("scalar", "pe")
        b.act(ztmp, stream[:, i, :], AF.Identity, scale=rstd2[:, i:i + 1])
        b.wait("vector", "se")
        b.tt(ztmp, ztmp, rmbc, ALU.mult)
        b.wait("tensor", "ve")
        items = [(ztmp[:, c * 128:(c + 1) * 128], hT[:, c, i * 128:(i + 1) * 128])
                 for c in range(8)]
        tp_batch(b, C, ps, items)
    b.barrier()
    if limit == "g0":
        for i in range(NT):
            b.cp(stream[:, i, :], A2_16[:, i, :])
        b.barrier()
        return

    gb = C["gb"]
    sp1 = WK[:, 0:64].rearrange("p (a f) -> p a f", a=8)
    for i in range(NT):
        b.wait("tensor", "ve")
        for c in range(8):
            b.mm(ps[i % 2][:, 0:8], hT[:, c, i * 128:(i + 1) * 128], C["wab"][:, c, :],
                 start=(c == 0), stop=(c == 7))
        b.wait("vector", "pe")
        b.wait("vector", "se")
        b.tt(sp1[:, 0, 0:4], ps[i % 2][:, 0:4], C["dtb"][:], ALU.add)
        b.wait("scalar", "ve")
        b.act(sp1[:, 1, 0:4], sp1[:, 0, 0:4], AF.Abs)
        b.act(sp1[:, 2, 0:4], sp1[:, 1, 0:4], AF.Exp, scale=-1.0)
        b.act(sp1[:, 3, 0:4], sp1[:, 2, 0:4], AF.Ln, bias=1.0)
        b.act(sp1[:, 4, 0:4], sp1[:, 0, 0:4], AF.Relu)
        b.act(gb[:, i, 4:8], ps[i % 2][:, 4:8], AF.Sigmoid)
        b.wait("vector", "se")
        b.tt(sp1[:, 5, 0:4], sp1[:, 3, 0:4], sp1[:, 4, 0:4], ALU.add)
        b.tt(gb[:, i, 0:4], sp1[:, 5, 0:4], C["coef"][:], ALU.mult)
    b.barrier()

    vpre = WK[:, 0:1048].rearrange("p (c f) -> p c f", c=8)
    cvo = WK[:, 1056:2080].rearrange("p (c f) -> p c f", c=8)
    cvs = WK[:, 2080:3104].rearrange("p (c f) -> p c f", c=8)
    slab = WK[:, 3104:4128]
    wv_ = WA[:, 0:8192].rearrange("p (c f) -> p c f", c=8)
    for (wname, cw, scr, do_conv) in (("l1_wv", "convv", scr_vhat, True),
                                      ("l1_wg", None, scr_sgate, False)):
        for half in range(2):
            b.barrier()
            b.dma("sync", wv_[:], dr[wname][:, half * 1024:(half + 1) * 1024]
                  .rearrange("(c p) f -> p c f", p=128), "wbig")
            b.wait_dma("tensor", "wbig")
            for i in range(NT):
                if i >= 1:
                    if do_conv:
                        b.wait_dma("vector", "sw", None)
                    else:
                        b.wait_dma("scalar", "sw", None)
                if do_conv:
                    b.wait("tensor", "ve")
                    for m in range(8):
                        pp = ps[m % 2]
                        if m >= 2:
                            b.wait("tensor", "ve")
                        if i == 0:
                            for c in range(8):
                                b.mm(pp[:, 0:128], wv_[:, c, m * 128:(m + 1) * 128],
                                     hT[:, c, 0:128], start=(c == 0), stop=(c == 7))
                        else:
                            for c in range(8):
                                b.mm(pp[:, 0:131], wv_[:, c, m * 128:(m + 1) * 128],
                                     hT[:, c, i * 128 - 3:(i + 1) * 128],
                                     start=(c == 0), stop=(c == 7))
                        b.wait("vector", "pe")
                        if i == 0:
                            b.memset(vpre[:, m, 0:3], 0.0)
                            b.cp(vpre[:, m, 3:131], pp[:, 0:128], drain=False)
                        else:
                            b.cp(vpre[:, m, :], pp[:, 0:131], drain=False)
                    cwt = C[cw]
                    hc0 = half * 8
                    for m in range(8):
                        b.ts(cvo[:, m, :], vpre[:, m, 0:128], cwt[:, hc0 + m, 0:1], None,
                             ALU.mult)
                        for j in range(1, 4):
                            b.stt(cvo[:, m, :], vpre[:, m, j:j + 128],
                                  cwt[:, hc0 + m, j:j + 1], cvo[:, m, :], ALU.mult, ALU.add)
                    b.wait("scalar", "ve")
                    b.wait("scalar", "pe")
                    b.act(cvs[:, :, :], cvo[:, :, :], AF.Silu)
                    b.wait("tensor", "se")
                    items = [(cvs[:, m, :], slab[:, m * 128:(m + 1) * 128]) for m in range(8)]
                    tp_batch(b, C, ps, items, banks=(2, 3, 4, 5))
                else:
                    b.wait("tensor", "se")
                    for nh in range(2):
                        pp = ps[nh]
                        for c in range(8):
                            b.mm(pp[:, 0:512], hT[:, c, i * 128:(i + 1) * 128],
                                 wv_[:, c, nh * 512:(nh + 1) * 512],
                                 start=(c == 0), stop=(c == 7))
                    b.wait("scalar", "pe")
                    b.act(slab[:, 0:512], ps[0][:, 0:512], AF.Silu)
                    b.act(slab[:, 512:1024], ps[1][:, 0:512], AF.Silu)
                ch = "ve" if do_conv else "se"
                b.wait("gpsimd", ch)
                b.dma("gpsimd", scr[i * 128:(i + 1) * 128, half * 1024:(half + 1) * 1024],
                      slab, "sw")
        b.barrier()
    if limit == "g2ab":
        return

    qpre = WK[:, 0:1048].rearrange("p (c f) -> p c f", c=8)
    qc = WK[:, 1056:2080].rearrange("p (c f) -> p c f", c=8)
    qs = WK[:, 2080:3104].rearrange("p (c f) -> p c f", c=8)
    qq = WK[:, 3104:4128].rearrange("p (c f) -> p c f", c=8)
    rstdT = WK[:, 4128:4256]
    bcr = WK[:, 4256:4384]
    for (wname, cw, scr, extra) in (("l1_wq", "convq", scr_qT, 0.0625),
                                    ("l1_wk", "convk", scr_kT, None)):
        b.barrier()
        b.dma("sync", wv_[:], dr[wname][:].rearrange("(c p) f -> p c f", p=128), "wbig")
        b.wait_dma("tensor", "wbig")
        for i in range(NT):
            b.wait("tensor", "ve")
            for m in range(8):
                pp = ps[m % 2]
                if m >= 2:
                    b.wait("tensor", "ve")
                if i == 0:
                    for c in range(8):
                        b.mm(pp[:, 0:128], wv_[:, c, m * 128:(m + 1) * 128],
                             hT[:, c, 0:128], start=(c == 0), stop=(c == 7))
                else:
                    for c in range(8):
                        b.mm(pp[:, 0:131], wv_[:, c, m * 128:(m + 1) * 128],
                             hT[:, c, i * 128 - 3:(i + 1) * 128],
                             start=(c == 0), stop=(c == 7))
                b.wait("vector", "pe")
                if i == 0:
                    b.memset(qpre[:, m, 0:3], 0.0)
                    b.cp(qpre[:, m, 3:131], pp[:, 0:128], drain=False)
                else:
                    b.cp(qpre[:, m, :], pp[:, 0:131], drain=False)
            cwt = C[cw]
            for m in range(8):
                b.ts(qc[:, m, :], qpre[:, m, 0:128], cwt[:, m, 0:1], None, ALU.mult)
                for j in range(1, 4):
                    b.stt(qc[:, m, :], qpre[:, m, j:j + 128], cwt[:, m, j:j + 1],
                          qc[:, m, :], ALU.mult, ALU.add)
            b.wait("scalar", "ve")
            b.wait("scalar", "pe")
            if i >= 1:
                b.wait_dma("scalar", f"qw{(i - 1) % 2}", None)
                b.wait_dma("vector", f"qw{(i - 1) % 2}", None)
            b.act(qs[:, :, :], qc[:, :, :], AF.Silu)
            b.act(qq[:, :, :], qs[:, :, :], AF.Square)
            b.wait("tensor", "se")
            for h in range(4):
                b.mm(ps[2][:, h:h + 1], qq[:, 2 * h, :], C["ones_col"][:],
                     start=True, stop=False)
                b.mm(ps[2][:, h:h + 1], qq[:, 2 * h + 1, :], C["ones_col"][:],
                     start=False, stop=True)
            b.wait("vector", "pe")
            b.cp(stats[:, 0:4], ps[2][:, 0:4])
            vgaps(b, C)
            rstd_chain(b, C, stats[:, 0:4], stats[:, 4:8], 1, 1e-6,
                       extra_scale=extra, is_sum=True)
            for h in range(4):
                bcast_col(b, C, ps, stats[:, 4 + h:5 + h], bcr, bank=3)
                b.tt(qs[:, 2 * h, :], qs[:, 2 * h, :], bcr, ALU.mult)
                b.tt(qs[:, 2 * h + 1, :], qs[:, 2 * h + 1, :], bcr, ALU.mult)
            b.wait("gpsimd", "ve")
            if i >= 2:
                b.wait_dma("gpsimd", f"qw{i % 2}", None)
            b.dma("gpsimd", scr[i].rearrange("p (c f) -> p c f", c=8),
                  qs[:, :, :], f"qw{i % 2}")
        b.barrier()
    if limit == "g2pre":
        return

    SS = A1[:, 0:4096].rearrange("p (c f) -> p c f", c=8)
    qTc = A1[:, 4096:5120].rearrange("p (c f) -> p c f", c=8)
    kTc = A1[:, 5120:6144].rearrange("p (c f) -> p c f", c=8)
    ktok = A1[:, 6144:7168].rearrange("p (c f) -> p c f", c=8)
    kdec = A1[:, 7168:8192].rearrange("p (c f) -> p c f", c=8)
    gb2 = A1[:, 8192:8200]
    sm = A1[:, 8200:8232]
    negG4 = sm[:, 0:4]
    eG4 = sm[:, 4:8]
    ebg4 = sm[:, 8:12]
    gcm = sm[:, 12:16]
    egd = sm[:, 16:20]
    eGC = sm[:, 20:24]
    obt = A1[:, 8232:8744]
    ob4 = A1[:, 8744:10792].rearrange("p (c f) -> p c f", c=4)
    vc = A1[:, 10792:12840]
    oscb = A1[:, 12840:14888]
    A2f = A2[:]
    Gd4 = A2f[:, 0:512].rearrange("p (c f) -> p c f", c=4)
    nGd4 = A2f[:, 512:1024].rearrange("p (c f) -> p c f", c=4)
    DqT4 = A2f[:, 1024:1536].rearrange("p (c f) -> p c f", c=4)
    Dk4 = A2f[:, 1536:2048].rearrange("p (c f) -> p c f", c=4)
    Aqk4 = A2f[:, 2048:2560].rearrange("p (c f) -> p c f", c=4)
    NT4 = A2f[:, 2560:3072].rearrange("p (c f) -> p c f", c=4)
    N4 = A2f[:, 3072:3584].rearrange("p (c f) -> p c f", c=4)
    X4 = A2f[:, 3584:4096].rearrange("p (c f) -> p c f", c=4)
    XT4 = A2f[:, 4096:4608].rearrange("p (c f) -> p c f", c=4)
    TT4 = A2f[:, 4608:5120].rearrange("p (c f) -> p c f", c=4)
    qg4 = A2f[:, 5120:6144].rearrange("p (c f) -> p c f", c=8)
    Ub4 = A2f[:, 6144:8192].rearrange("p (c f) -> p c f", c=4)
    bV4 = A2f[:, 8192:10240].rearrange("p (c f) -> p c f", c=4)
    kd24 = A2f[:, 10240:11264].rearrange("p (c f) -> p c f", c=8)
    Gd4f, nGd4f = A2f[:, 0:512], A2f[:, 512:1024]
    DqT4f, Dk4f = A2f[:, 1024:1536], A2f[:, 1536:2048]
    Aqk4f, NT4f, N4f = A2f[:, 2048:2560], A2f[:, 2560:3072], A2f[:, 3072:3584]
    X4f, XT4f, TT4f = A2f[:, 3584:4096], A2f[:, 4096:4608], A2f[:, 4608:5120]
    msk4 = {}
    msk4f = {}
    for nm, off in (("mask_ui", 11264), ("mask_us", 11776), ("mask_ls", 12288)):
        msk4[nm] = A2f[:, off:off + 512].rearrange("p (c f) -> p c f", c=4)
        msk4f[nm] = A2f[:, off:off + 512]

    b.memset(SS[:, :, :], 0.0)
    onb = WK[:, 0:512]
    sq2 = WK[:, 512:1024]
    b.dma("sync", onb, dr["bc_onorm"], "onw")
    for nm in ("mask_ui", "mask_us", "mask_ls"):
        for h in range(4):
            b.cp(msk4[nm][:, h, :], C[nm][:], drain=False)
    b.barrier()
    for c in range(NT):
        b.wait("sync", "pe")
        b.wait("sync", "ve")
        b.dma("sync", qTc[:, :, :], scr_qT[c].rearrange("p (a f) -> p a f", a=8), "qrd")
        b.dma("sync", kTc[:, :, :], scr_kT[c].rearrange("p (a f) -> p a f", a=8), "krd")
        b.dma("sync", vc, scr_vhat[c * 128:(c + 1) * 128, :], "vrd")
        b.dma("sync", oscb, scr_sgate[c * 128:(c + 1) * 128, :], "grd")
        b.wait("tensor", "ve")
        b.wait("tensor", "se")
        b.mm(ps[7][:, 0:4], C["ones_ui"][:], gb[:, c, 0:4], start=True, stop=True)
        b.wait("vector", "pe")
        b.cp(gb2[:, 0:4], ps[7][:, 0:4])
        b.cp(gb2[:, 4:8], gb[:, c, 4:8], drain=False)
        vgaps(b, C)
        b.ts(negG4, gb2[:, 0:4], -1.0, None, ALU.mult)
        b.wait("scalar", "ve")
        b.act(eG4, gb2[:, 0:4], AF.Exp)
        b.wait("vector", "se")
        b.tt(ebg4, eG4, gb2[:, 4:8], ALU.mult)
        b.wait_dma("tensor", "qrd")
        b.wait_dma("tensor", "krd")
        for h in range(4):
            bcast_col(b, C, ps, gb2[:, h:h + 1], Gd4[:, h, :], bank=7)
        b.wait("vector", "ve")
        for h in range(4):
            b.ts(Gd4[:, h, :], Gd4[:, h, :], negG4[:, h:h + 1], None, ALU.add, drain=False)
        for h in range(4):
            b.mm(ps[0][:, h * 128:(h + 1) * 128], kTc[:, 2 * h, :], qTc[:, 2 * h, :],
                 start=True, stop=False)
            b.mm(ps[0][:, h * 128:(h + 1) * 128], kTc[:, 2 * h + 1, :], qTc[:, 2 * h + 1, :],
                 start=False, stop=True)
        for h in range(4):
            b.mm(ps[1][:, h * 128:(h + 1) * 128], kTc[:, 2 * h, :], kTc[:, 2 * h, :],
                 start=True, stop=False)
            b.mm(ps[1][:, h * 128:(h + 1) * 128], kTc[:, 2 * h + 1, :], kTc[:, 2 * h + 1, :],
                 start=False, stop=True)
        b.wait("tensor", "ve")
        tp_batch(b, C, ps, [(kTc[:, a, :], ktok[:, a, :]) for a in range(8)],
                 banks=(2, 3, 4, 5))
        b.ts(nGd4f, Gd4f, -1.0, None, ALU.mult)
        b.tt(DqT4f, Gd4f, msk4f["mask_ui"], ALU.add)
        b.tt(Dk4f, Gd4f, msk4f["mask_us"], ALU.add)
        b.wait("scalar", "ve")
        b.act(DqT4f, DqT4f, AF.Exp)
        b.act(Dk4f, Dk4f, AF.Exp)
        b.wait("vector", "se")
        b.wait("vector", "pe")
        b.tt(Aqk4f, ps[0][:, 0:512], DqT4f, ALU.mult)
        b.tt(NT4f, ps[1][:, 0:512], Dk4f, ALU.mult)
        for h in range(4):
            bcast_col(b, C, ps, gb2[:, 4 + h:5 + h], Dk4[:, h, :], bank=7)
        b.wait("vector", "ve")
        b.ts(Dk4f, Dk4f, -1.0, None, ALU.mult, drain=False)
        b.tt(NT4f, NT4f, Dk4f, ALU.mult)
        b.tt(N4f, nGd4f, msk4f["mask_ls"], ALU.add)
        b.wait("scalar", "ve")
        b.act(N4f, N4f, AF.Exp)
        b.wait("vector", "se")
        b.tt(N4f, N4f, ps[1][:, 0:512], ALU.mult)
        for h in range(4):
            b.ts(N4[:, h, :], N4[:, h, :], gb2[:, 4 + h:5 + h], -1.0, ALU.mult, ALU.mult,
                 drain=(h == 0))
        for h in range(4):
            b.tt(TT4[:, h, :], NT4[:, h, :], ident, ALU.add, drain=False)
        b.cp(X4f, N4f)
        b.cp(XT4f, NT4f)
        for lev in range(6):
            b.wait("tensor", "ve")
            for h in range(4):
                b.mm(ps[2][:, h * 128:(h + 1) * 128], XT4[:, h, :], X4[:, h, :],
                     start=True, stop=True)
                b.mm(ps[3][:, h * 128:(h + 1) * 128], X4[:, h, :], XT4[:, h, :],
                     start=True, stop=True)
            b.wait("vector", "pe")
            b.cp(X4f, ps[2][:, 0:512])
            b.cp(XT4f, ps[3][:, 0:512])
            b.wait("tensor", "ve")
            for h in range(4):
                b.mm(ps[4][:, h * 128:(h + 1) * 128], X4[:, h, :], TT4[:, h, :],
                     start=True, stop=True)
            b.wait("vector", "pe")
            b.tt(TT4f, TT4f, ps[4][:, 0:512], ALU.add)
        b.wait_dma("vector", "vrd")
        for h in range(4):
            b.ts(bV4[:, h, :], vc[:, h * 512:(h + 1) * 512], gb2[:, 4 + h:5 + h],
                 None, ALU.mult, drain=(h == 0))
        b.wait("tensor", "ve")
        for h in range(4):
            b.mm(ps[4 + h % 2][:, 0:512], TT4[:, h, :], bV4[:, h, :],
                 start=True, stop=True)
            if h % 2 == 1:
                b.wait("vector", "pe")
                b.cp(Ub4[:, h - 1, :], ps[4][:, 0:512])
                b.cp(Ub4[:, h, :], ps[5][:, 0:512])
                if h == 1:
                    b.wait("tensor", "ve")
        for h in range(4):
            b.ts(kdec[:, 2 * h, :], ktok[:, 2 * h, :], ebg4[:, h:h + 1], None, ALU.mult,
                 drain=(h == 0))
            b.ts(kdec[:, 2 * h + 1, :], ktok[:, 2 * h + 1, :], ebg4[:, h:h + 1],
                 None, ALU.mult, drain=False)
        b.wait("tensor", "ve")
        for h in range(4):
            b.mm(ps[6][:, h * 128:(h + 1) * 128], kdec[:, 2 * h, :], TT4[:, h, :],
                 start=True, stop=True)
            b.mm(ps[7][:, h * 128:(h + 1) * 128], kdec[:, 2 * h + 1, :], TT4[:, h, :],
                 start=True, stop=True)
        b.wait("vector", "pe")
        b.cp(qg4[:, 0:4, :], ps[6][:, 0:512])
        b.cp(qg4[:, 4:8, :], ps[7][:, 0:512])
        if c > 0:
            b.wait("tensor", "ve")
            for h in range(4):
                b.mm(ps[4 + h % 2][:, 0:512], qg4[:, h, :], SS[:, 2 * h, :],
                     start=True, stop=False)
                b.mm(ps[4 + h % 2][:, 0:512], qg4[:, 4 + h, :], SS[:, 2 * h + 1, :],
                     start=False, stop=True)
                if h % 2 == 1:
                    b.wait("vector", "pe")
                    b.tt(Ub4[:, h - 1, :], Ub4[:, h - 1, :], ps[4][:, 0:512], ALU.subtract)
                    b.tt(Ub4[:, h, :], Ub4[:, h, :], ps[5][:, 0:512], ALU.subtract)
                    if h == 1:
                        b.wait("tensor", "ve")
        b.wait("tensor", "ve")
        for h in range(4):
            b.mm(ps[4 + h % 2][:, 0:512], Aqk4[:, h, :], Ub4[:, h, :],
                 start=True, stop=True)
            if c > 0:
                b.mm(ps[2 + h % 2][:, 0:512], qTc[:, 2 * h, :], SS[:, 2 * h, :],
                     start=True, stop=False)
                b.mm(ps[2 + h % 2][:, 0:512], qTc[:, 2 * h + 1, :], SS[:, 2 * h + 1, :],
                     start=False, stop=True)
            if h % 2 == 1:
                b.wait("vector", "pe")
                for hh in (h - 1, h):
                    if c > 0:
                        b.cp(obt, ps[4 + hh % 2][:, 0:512], drain=True)
                        b.stt(ob4[:, hh, :], ps[2 + hh % 2][:, 0:512], eG4[:, hh:hh + 1],
                              obt, ALU.mult, ALU.add)
                    else:
                        b.cp(ob4[:, hh, :], ps[4 + hh % 2][:, 0:512], drain=False)
                if h == 1:
                    b.wait("tensor", "ve")
        b.wait("vector", "pe")
        b.wait("vector", "se")
        for h in range(4):
            b.cp(gcm[:, h:h + 1], Gd4[:, h, 127:128], drain=(h == 0))
        b.wait("scalar", "ve")
        b.act(egd, gcm, AF.Exp)
        b.wait("vector", "se")
        b.tt(eGC, gb2[:, 0:4], gcm, ALU.add)
        vgaps(b, C)
        b.wait("scalar", "ve")
        b.act(eGC, eGC, AF.Exp)
        b.wait("vector", "se")
        for h in range(4):
            b.ts(kd24[:, 2 * h, :], ktok[:, 2 * h, :], egd[:, h:h + 1], None, ALU.mult,
                 drain=False)
            b.ts(kd24[:, 2 * h + 1, :], ktok[:, 2 * h + 1, :], egd[:, h:h + 1],
                 None, ALU.mult, drain=False)
        b.wait("tensor", "ve")
        for h in range(4):
            b.mm(ps[4 + h % 2][:, 0:512], kd24[:, 2 * h, :], Ub4[:, h, :],
                 start=True, stop=True)
            b.mm(ps[2 + h % 2][:, 0:512], kd24[:, 2 * h + 1, :], Ub4[:, h, :],
                 start=True, stop=True)
            if h % 2 == 1:
                b.wait("vector", "pe")
                for hh in (h - 1, h):
                    if c == 0:
                        b.cp(SS[:, 2 * hh, :], ps[4 + hh % 2][:, 0:512], drain=False)
                        b.cp(SS[:, 2 * hh + 1, :], ps[2 + hh % 2][:, 0:512], drain=False)
                    else:
                        b.stt(SS[:, 2 * hh, :], SS[:, 2 * hh, :], eGC[:, hh:hh + 1],
                              ps[4 + hh % 2][:, 0:512], ALU.mult, ALU.add)
                        b.stt(SS[:, 2 * hh + 1, :], SS[:, 2 * hh + 1, :], eGC[:, hh:hh + 1],
                              ps[2 + hh % 2][:, 0:512], ALU.mult, ALU.add)
                if h == 1:
                    b.wait("tensor", "ve")
        b.wait_dma("vector", "grd")
        for h in range(4):
            b.stt(sq2, ob4[:, h, :], 1.0, ob4[:, h, :], ALU.mult, ALU.mult,
                  accum=stats[:, h:h + 1])
        vgaps(b, C)
        rstd_chain(b, C, stats[:, 0:4], stats[:, 4:8], 512, 1e-6)
        b.wait("scalar", "ve")
        b.wait_dma("vector", "onw")
        for h in range(4):
            b.wait("scalar", "ve")
            b.act(sq2, ob4[:, h, :], AF.Identity, scale=stats[:, 4 + h:5 + h])
            b.wait("vector", "se")
            b.tt(sq2, sq2, onb, ALU.mult)
            b.tt(oscb[:, h * 512:(h + 1) * 512], oscb[:, h * 512:(h + 1) * 512],
                 sq2, ALU.mult)
        b.wait("gpsimd", "ve")
        if c >= 1:
            b.wait_dma("gpsimd", "ow", None)
        b.dma("gpsimd", scr_osc[c * 128:(c + 1) * 128, :], oscb, "ow")
        b.wait_dma("sync", "ow")
    b.barrier()
    if limit == "g2c":
        return

    oT = WK[:, 0:2048].rearrange("p (c f) -> p c f", c=16)
    xtk = WK[:, 2048:4096].rearrange("p (c f) -> p c f", c=2)
    osb = WK[:, 4096:6144]
    xa16 = A2_16
    for half in range(2):
        b.barrier()
        b.dma("sync", wv_[:], dr["l1_wo"][half * 1024:(half + 1) * 1024, :]
              .rearrange("(c p) f -> p c f", p=128), "wbig")
        b.wait_dma("tensor", "wbig")
        for i in range(NT):
            if i >= 1:
                b.wait("sync", "pe")
                b.wait("sync", "ve")
            b.dma("sync", osb, scr_osc[i * 128:(i + 1) * 128, :], "osr")
            b.dma("sync", xtk[:, i % 2, :], scr_x2[i * 128:(i + 1) * 128, :], f"x2r{i % 2}")
            b.wait_dma("tensor", "osr")
            b.wait("tensor", "ve")
            items = [(osb[:, m * 128:(m + 1) * 128], oT[:, m, :]) for m in range(16)]
            tp_batch(b, C, ps, items, banks=(2, 3, 4, 5))
            b.wait("tensor", "ve")
            for mh in range(2):
                pp = ps[mh]
                for cch in range(8):
                    b.mm(pp[:, 0:512], oT[:, half * 8 + cch, :],
                         wv_[:, cch, mh * 512:(mh + 1) * 512],
                         start=(cch == 0), stop=(cch == 7))
            b.wait("vector", "pe")
            b.wait_dma("vector", f"x2r{i % 2}")
            if half == 0:
                b.tt(xa16[:, i, 0:512], ps[0][:, 0:512], xtk[:, i % 2, 0:512], ALU.add)
                b.tt(xa16[:, i, 512:1024], ps[1][:, 0:512], xtk[:, i % 2, 512:1024], ALU.add)
            else:
                b.tt(xa16[:, i, 0:512], ps[0][:, 0:512], xa16[:, i, 0:512], ALU.add)
                b.tt(xa16[:, i, 512:1024], ps[1][:, 0:512], xa16[:, i, 512:1024], ALU.add)
    b.barrier()
    g2bc = WK[:, 0:1024]
    b2bc2 = WK[:, 1024:2048]
    bb2 = WK[:, 2048:3072]
    h2t = WK[:, 3072:4096]
    sqd3 = WK[:, 4096:5120]
    h2T = WK[:, 5120:6144].rearrange("p (c f) -> p c f", c=8)
    xtk2 = WK[:, 6144:8192].rearrange("p (c f) -> p c f", c=2)
    b.dma("sync", g2bc, dr["bc_ln2g"], "bc0")
    b.dma("sync", b2bc2, dr["bc_ln2b"], "bc1")
    b.dma("sync", bb2, dr["bc_b2"], "bc2")
    b.dma("sync", WA[:, 0:8192].rearrange("p (c f) -> p c f", c=8)[:],
          dr["l1_w2"][:].rearrange("(c p) f -> p c f", p=128), "wbig")
    w2_ = WA[:, 0:8192].rearrange("p (c f) -> p c f", c=8)
    for i in range(NT):
        b.red(zsum[:, i:i + 1], xa16[:, i, :], op="sum")
    vgaps(b, C)
    b.ts(nmean, zsum, -1.0 / 1024.0, None, ALU.mult)
    b.wait("scalar", "ve")
    for i in range(NT):
        b.wait("scalar", "ve")
        b.act(h2t, xa16[:, i, :], AF.Identity, bias=nmean[:, i:i + 1])
        b.wait("vector", "se")
        b.stt(sqd3, h2t, 1.0, h2t, ALU.mult, ALU.mult, accum=ssq[:, i:i + 1])
        b.cp(stream[:, i, :], h2t)
    vgaps(b, C)
    rstd_chain(b, C, ssq, rstd, 1024, 1e-5)
    b.wait("scalar", "ve")
    b.wait_dma("vector", "bc0")
    b.wait_dma("vector", "bc1")
    b.wait_dma("tensor", "wbig")
    for i in range(NT):
        b.wait("scalar", "ve")
        b.wait("scalar", "pe")
        b.act(h2t, stream[:, i, :], AF.Identity, scale=rstd[:, i:i + 1])
        b.wait("vector", "se")
        b.tt(h2t, h2t, g2bc, ALU.mult)
        b.tt(h2t, h2t, b2bc2, ALU.add)
        b.wait("tensor", "ve")
        items = [(h2t[:, cch * 128:(cch + 1) * 128], h2T[:, cch, :]) for cch in range(8)]
        tp_batch(b, C, ps, items, banks=(2, 3, 4, 5))
        b.wait("tensor", "ve")
        for mh in range(2):
            pp = ps[mh]
            for cch in range(8):
                b.mm(pp[:, 0:512], h2T[:, cch, :], w2_[:, cch, mh * 512:(mh + 1) * 512],
                     start=(cch == 0), stop=(cch == 7))
        if i >= 2:
            b.wait("sync", "ve")
        b.dma("sync", xtk2[:, i % 2, :], scr_x2[i * 128:(i + 1) * 128, :], f"x2s{i % 2}")
        b.wait("scalar", "pe")
        b.wait("vector", "pe")
        b.wait_dma("vector", "bc2")
        b.tt(h2t[:, 0:512], ps[0][:, 0:512], bb2[:, 0:512], ALU.add)
        b.tt(h2t[:, 512:1024], ps[1][:, 0:512], bb2[:, 512:1024], ALU.add)
        b.wait("scalar", "ve")
        b.act(h2t, h2t, AF.Gelu)
        b.wait("vector", "se")
        b.wait_dma("vector", f"x2s{i % 2}")
        b.tt(h2t, h2t, xa16[:, i, :], ALU.add)
        b.tt(stream[:, i, :], h2t, xtk2[:, i % 2, :], ALU.add)
        b.red(C["stats"][:, i:i + 1], stream[:, i, :], op="sum")
    postln_finish(b, C, stream, WK[:, 0:1024], WK[:, 1024:2048], eps=1e-5)


SPARSE_MOE = True


def emit_s4(nc, b, C, dr, ps, A1, A2, WA, WK, stream, A2_16, scr_x3):
    if SPARSE_MOE:
        return emit_s4_sparse(nc, b, C, dr, ps, A1, A2, WA, WK, stream, A2_16, scr_x3)
    return emit_s4_dense(nc, b, C, dr, ps, A1, A2, WA, WK, stream, A2_16, scr_x3)


def emit_s4_sparse(nc, b, C, dr, ps, A1, A2, WA, WK, stream, A2_16, scr_x3):
    ident = C["ident"][:]
    stats = C["stats"]
    ytok = A2_16
    comb = stream

    mrbc = WK[:, 0:1024]
    ytmp = WK[:, 1024:2048]
    sqd = WK[:, 2048:3072]
    lgb = WK[:, 3072:3080]
    lg2 = WK[:, 3080:3088]
    t1 = WK[:, 3088:3096]
    msk1 = WK[:, 3096:3104]
    msk2 = WK[:, 3104:3112]
    m8 = WK[:, 3112:3120]
    asb = WK[:, 3120:3128]
    offbc = WK[:, 3128:3136]
    ytT = WK[:, 4096:5120].rearrange("p (c f) -> p c f", c=8)
    b.dma("sync", mrbc, dr["bc_moerms"], "bc0")
    ssqm, rstdm = stats[:, 0:16], stats[:, 16:32]
    m1c, m2c, d12 = stats[:, 32:33], stats[:, 33:34], stats[:, 34:35]
    w1c, w2c = stats[:, 35:36], stats[:, 36:37]
    for i in range(NT):
        b.stt(sqd, stream[:, i, :], 1.0, stream[:, i, :], ALU.mult, ALU.mult,
              accum=ssqm[:, i:i + 1])
    vgaps(b, C)
    rstd_chain(b, C, ssqm, rstdm, 1024, 1e-6)
    b.wait("scalar", "ve")
    b.wait_dma("vector", "bc0")
    b.memset(C["offrow"][:], 0.0)
    for i in range(NT):
        b.wait("scalar", "ve")
        b.wait("scalar", "pe")
        b.act(ytmp, stream[:, i, :], AF.Identity, scale=rstdm[:, i:i + 1])
        b.wait("vector", "se")
        b.tt(ytmp, ytmp, mrbc, ALU.mult)
        b.cp(ytok[:, i, :], ytmp)
        b.wait("tensor", "ve")
        items = [(ytmp[:, c * 128:(c + 1) * 128], ytT[:, c, :]) for c in range(8)]
        tp_batch(b, C, ps, items, banks=(0, 1, 2, 3))
        b.wait("tensor", "ve")
        for c in range(8):
            b.mm(ps[6][:, 0:8], ytT[:, c, :], C["gwT"][:, c, :],
                 start=(c == 0), stop=(c == 7))
        b.wait("vector", "pe")
        b.cp(lgb, ps[6][:, 0:8])
        b.red(m1c, lgb, op="max")
        b.tt(msk1, lgb, m1c.broadcast_to([128, 8]), ALU.is_equal)
        b.stt(lg2, msk1, -1e30, lgb, ALU.mult, ALU.add)
        b.red(m2c, lg2, op="max")
        b.tt(msk2, lg2, m2c.broadcast_to([128, 8]), ALU.is_equal)
        b.tt(d12, m1c, m2c, ALU.subtract)
        b.wait("scalar", "ve")
        b.act(w1c, d12, AF.Sigmoid)
        b.act(w2c, d12, AF.Sigmoid, scale=-1.0)
        b.wait("vector", "se")
        b.ts(t1, msk2, w2c, None, ALU.mult)
        b.stt(C["wfull"][:, i, :], msk1, w1c, t1, ALU.mult, ALU.add)
        b.tt(m8, msk1, msk2, ALU.max)
        b.wait("tensor", "ve")
        b.mm(ps[6][:, 8:16], C["ones_ui"][:], m8, start=True, stop=True)
        b.mm(ps[7][:, 0:8], C["ones_r"][:], C["offrow"][:], start=True, stop=True)
        b.wait("vector", "pe")
        b.cp(offbc, ps[7][:, 0:8])
        b.tt(asb, ps[6][:, 8:16], offbc, ALU.add)
        b.tt(t1, asb, m8, ALU.mult)
        b.ts(C["rank"][:, i, :], t1, 1.0, -1.0, ALU.mult, ALU.add)
        b.wait("tensor", "ve")
        b.mm(ps[7][0:1, 8:16], C["e127"][:], asb, start=True, stop=True)
        b.wait("vector", "pe")
        b.cp(C["offrow"][:], ps[7][0:1, 8:16])
    b.barrier()

    for i in range(NT):
        b.dma("gpsimd", scr_x3[i * 128:(i + 1) * 128, :], stream[:, i, :], "x3w")
    b.wait_dma("vector", "x3w")
    for i in range(NT):
        b.memset(comb[:, i, :], 0.0)
    b.barrier()

    NBW = [384] * 7 + [128]
    w1s = WA[:, 0:3072].rearrange("p (c f) -> p c f", c=8)
    w2s = WA[:, 3072:6144].rearrange("p (c f) -> p c f", c=8)
    w3s = WA[:, 6144:9216].rearrange("p (c f) -> p c f", c=3)
    yeT = WK[:, 0:3072].rearrange("p (c f) -> p c f", c=8)
    oute = WK[:, 3072:6144].rearrange("p (c f) -> p c f", c=3)
    ygt = WK[:, 3072:4096]
    Pslab = WK[:, 6144:7168].rearrange("p (c f) -> p c f", c=8)
    tmps = WK[:, 7168:7552]
    hb = WK[:, 7552:7936]
    hTe = WK[:, 7936:8320].rearrange("p (c f) -> p c f", c=3)
    rk8 = WK[:, 8320:8328]
    rkbc = WK[:, 8328:8456]
    wbc = WK[:, 8456:8584]
    atile = Pslab

    for e in range(8):
        for half in range(2):
            for rcl in range(3):
                rc = 3 * half + rcl
                for g in range(2):
                    b.wait("vector", "pe")
                    for j in range(8):
                        tc = g * 8 + j
                        b.ts(rk8[:, j:j + 1], C["rank"][:, tc, e:e + 1], 1.0,
                             -128.0 * rc, ALU.mult, ALU.add)
                    vgaps(b, C)
                    for j in range(8):
                        b.ts(Pslab[:, j, :], C["iota_f"][:], rk8[:, j:j + 1], None,
                             ALU.is_equal)
                    b.wait("tensor", "ve")
                    for j in range(8):
                        tc = g * 8 + j
                        for mh in range(2):
                            b.mm(ps[mh][:, 0:512], Pslab[:, j, :],
                                 ytok[:, tc, mh * 512:(mh + 1) * 512],
                                 start=(g == 0 and j == 0), stop=(g == 1 and j == 7))
                b.wait("vector", "pe")
                b.cp(ygt[:, 0:512], ps[0][:, 0:512])
                b.cp(ygt[:, 512:1024], ps[1][:, 0:512])
                b.wait("tensor", "ve")
                items = [(ygt[:, c * 128:(c + 1) * 128], yeT[:, c, rcl * 128:(rcl + 1) * 128])
                         for c in range(8)]
                tp_batch(b, C, ps, items, banks=(2, 3, 4, 5))
            b.barrier()
            wslot = [WA[:, 0:3072].rearrange("p (c f) -> p c f", c=8),
                     WA[:, 3072:6144].rearrange("p (c f) -> p c f", c=8)]
            wslot3 = [WA[:, 0:3072].rearrange("p (c f) -> p c f", c=3),
                      WA[:, 3072:6144].rearrange("p (c f) -> p c f", c=3)]
            upb = WA[:, 6144:7296].rearrange("p (c f) -> p c f", c=3)
            hb3 = WA[:, 7296:8448].rearrange("p (c f) -> p c f", c=3)
            loads = []
            for nb in range(8):
                loads += [(1, nb), (2, nb), (3, nb)]
            pe_done = {}

            def issue_load(li):
                kind, nb = loads[li]
                slot = li % 2
                n0 = sum(NBW[:nb])
                NW = NBW[nb]
                nc3 = NW // 128
                q = "sync" if li % 2 == 0 else "gpsimd"
                if li >= 2:
                    b.wait(q, "pe", pe_done.get(li - 2, 0))
                src = dr[("moe_w1i", "moe_w2i", "moe_w3i")[kind - 1]]
                if kind < 3:
                    b.dma(q, wslot[slot][:, :, 0:NW],
                          src[e, nb, :, 0:8 * NW].rearrange("p (c f) -> p c f", c=8),
                          f"wl{slot}")
                else:
                    b.dma(q, wslot3[slot][:, 0:nc3, :],
                          src[e, nb, :, 0:nc3 * 1024].rearrange("p (c f) -> p c f", c=nc3),
                          f"wl{slot}")

            issue_load(0)
            issue_load(1)
            for nb in range(8):
                n0 = sum(NBW[:nb])
                NW = NBW[nb]
                nc3 = NW // 128
                li = nb * 3
                s1_, s2_, s3_ = li % 2, (li + 1) % 2, (li + 2) % 2
                b.wait_dma("tensor", f"wl{s1_}", None)
                for r in range(3):
                    pp = ps[(0, 1, 5)[r]]
                    if nb > 0:
                        b.wait("tensor", "ve")
                    for c in range(8):
                        b.mm(pp[:, 0:NW], yeT[:, c, r * 128:(r + 1) * 128],
                             wslot[s1_][:, c, 0:NW], start=(c == 0), stop=(c == 7))
                    b.wait("vector", "pe")
                    b.cp(upb[:, r, 0:NW], pp[:, 0:NW], drain=False)
                pe_done[li] = b.n["pe"]
                if li + 2 < len(loads):
                    issue_load(li + 2)
                b.wait_dma("tensor", f"wl{s2_}", None)
                for r in range(3):
                    pp = ps[(0, 1, 5)[r]]
                    b.wait("tensor", "ve")
                    for c in range(8):
                        b.mm(pp[:, 0:NW], yeT[:, c, r * 128:(r + 1) * 128],
                             wslot[s2_][:, c, 0:NW], start=(c == 0), stop=(c == 7))
                    b.wait("scalar", "pe")
                    b.wait("scalar", "ve")
                    b.act(tmps[:, 0:NW], upb[:, r, 0:NW], AF.Silu)
                    b.wait("vector", "se")
                    b.tt(hb3[:, r, 0:NW], tmps[:, 0:NW], pp[:, 0:NW], ALU.mult)
                pe_done[li + 1] = b.n["pe"]
                if li + 3 < len(loads):
                    issue_load(li + 3)
                b.wait_dma("tensor", f"wl{s3_}", None)
                b.wait("tensor", "ve")
                for r in range(3):
                    items = [(hb3[:, r, c * 128:(c + 1) * 128],
                              hTe[:, c, :]) for c in range(nc3)]
                    tp_batch(b, C, ps, items, banks=(2, 6))
                    b.wait("tensor", "ve")
                    for mh in range(2):
                        for c in range(nc3):
                            b.mm(ps[3 + mh][:, 0:512], hTe[:, c, :],
                                 wslot3[s3_][:, c, mh * 512:(mh + 1) * 512],
                                 start=(c == 0), stop=(c == nc3 - 1))
                    b.wait("vector", "pe")
                    if nb == 0:
                        b.cp(oute[:, r, 0:512], ps[3][:, 0:512], drain=False)
                        b.cp(oute[:, r, 512:1024], ps[4][:, 0:512], drain=False)
                    else:
                        b.tt(oute[:, r, 0:512], oute[:, r, 0:512], ps[3][:, 0:512], ALU.add)
                        b.tt(oute[:, r, 512:1024], oute[:, r, 512:1024], ps[4][:, 0:512],
                             ALU.add)
                pe_done[li + 2] = b.n["pe"]
            b.barrier()
            rkbc2 = (rkbc, wbc)
            bcast_col(b, C, ps, C["rank"][:, 0, e:e + 1], rkbc2[0], bank=7)
            for tc in range(NT):
                cur = rkbc2[tc % 2]
                b.wait("vector", "pe")
                for rcl in range(3):
                    rc = 3 * half + rcl
                    b.ts(atile[:, 0, :], cur, 1.0, -128.0 * rc, ALU.mult, ALU.add)
                    b.tt(Pslab[:, 4 + rcl, :], C["iota_p"][:], atile[:, 0, :], ALU.is_equal)
                if tc + 1 < NT:
                    bcast_col(b, C, ps, C["rank"][:, tc + 1, e:e + 1], rkbc2[(tc + 1) % 2],
                              bank=7)
                b.wait("tensor", "ve")
                for mh in range(2):
                    for rcl in range(3):
                        b.mm(ps[mh][:, 0:512], Pslab[:, 4 + rcl, :],
                             oute[:, rcl, mh * 512:(mh + 1) * 512],
                             start=(rcl == 0), stop=(rcl == 2))
                b.wait("vector", "pe")
                b.stt(comb[:, tc, 0:512], ps[0][:, 0:512], C["wfull"][:, tc, e:e + 1],
                      comb[:, tc, 0:512], ALU.mult, ALU.add)
                b.stt(comb[:, tc, 512:1024], ps[1][:, 0:512], C["wfull"][:, tc, e:e + 1],
                      comb[:, tc, 512:1024], ALU.mult, ALU.add)
            b.barrier()

    xrd = WK[:, 0:2048].rearrange("p (c f) -> p c f", c=2)
    for i in range(NT):
        if i >= 2:
            b.wait("sync", "ve")
        b.dma("sync", xrd[:, i % 2, :], scr_x3[i * 128:(i + 1) * 128, :], f"x3r{i % 2}")
        b.wait_dma("vector", f"x3r{i % 2}")
        b.tt(stream[:, i, :], stream[:, i, :], xrd[:, i % 2, :], ALU.add)
        b.red(C["stats"][:, i:i + 1], stream[:, i, :], op="sum")
    postln_finish(b, C, stream, WK[:, 2048:3072], WK[:, 3072:4096], eps=1e-5)


def emit_s4_dense(nc, b, C, dr, ps, A1, A2, WA, WK, stream, A2_16, scr_x3):
    ident = C["ident"][:]
    stats = C["stats"]
    yT = A2[:].rearrange("p (c t) -> p c t", c=8)

    mrbc = WK[:, 0:1024]
    ytmp = WK[:, 1024:2048]
    sqd = WK[:, 2048:3072]
    lgb = WK[:, 3072:3080]
    lg2 = WK[:, 3080:3088]
    t1 = WK[:, 3088:3096]
    msk1 = WK[:, 3096:3104]
    msk2 = WK[:, 3104:3112]
    b.dma("sync", mrbc, dr["bc_moerms"], "bc0")
    ssqm, rstdm = stats[:, 0:16], stats[:, 16:32]
    m1c, m2c, d12 = stats[:, 32:33], stats[:, 33:34], stats[:, 34:35]
    w1c, w2c = stats[:, 35:36], stats[:, 36:37]
    for i in range(NT):
        b.stt(sqd, stream[:, i, :], 1.0, stream[:, i, :], ALU.mult, ALU.mult,
              accum=ssqm[:, i:i + 1])
    vgaps(b, C)
    rstd_chain(b, C, ssqm, rstdm, 1024, 1e-6)
    b.wait("scalar", "ve")
    b.wait_dma("vector", "bc0")
    for i in range(NT):
        b.wait("scalar", "ve")
        b.wait("scalar", "pe")
        b.act(ytmp, stream[:, i, :], AF.Identity, scale=rstdm[:, i:i + 1])
        b.wait("vector", "se")
        b.tt(ytmp, ytmp, mrbc, ALU.mult)
        b.wait("tensor", "ve")
        items = [(ytmp[:, c * 128:(c + 1) * 128], yT[:, c, i * 128:(i + 1) * 128])
                 for c in range(8)]
        tp_batch(b, C, ps, items, banks=(0, 1, 2, 3))
        b.wait("tensor", "ve")
        for c in range(8):
            b.mm(ps[6][:, 0:8], yT[:, c, i * 128:(i + 1) * 128], C["gwT"][:, c, :],
                 start=(c == 0), stop=(c == 7))
        b.wait("vector", "pe")
        b.cp(lgb, ps[6][:, 0:8])
        b.red(m1c, lgb, op="max")
        b.tt(msk1, lgb, m1c.broadcast_to([128, 8]), ALU.is_equal)
        b.stt(lg2, msk1, -1e30, lgb, ALU.mult, ALU.add)
        b.red(m2c, lg2, op="max")
        b.tt(msk2, lg2, m2c.broadcast_to([128, 8]), ALU.is_equal)
        b.tt(d12, m1c, m2c, ALU.subtract)
        b.wait("scalar", "ve")
        b.act(w1c, d12, AF.Sigmoid)
        b.act(w2c, d12, AF.Sigmoid, scale=-1.0)
        b.wait("vector", "se")
        b.ts(t1, msk2, w2c, None, ALU.mult)
        b.stt(C["wfull"][:, i, :], msk1, w1c, t1, ALU.mult, ALU.add)
    b.barrier()

    for i in range(NT):
        b.dma("gpsimd", scr_x3[i * 128:(i + 1) * 128, :], stream[:, i, :], "x3w")
    b.wait_dma("vector", "x3w")
    comb = stream
    for i in range(NT):
        b.memset(comb[:, i, :], 0.0)
    b.barrier()

    NBW = [384] * 7 + [128]
    w1s = WA[:, 0:3072].rearrange("p (c f) -> p c f", c=8)
    w2s = WA[:, 3072:6144].rearrange("p (c f) -> p c f", c=8)
    w3s = WA[:, 6144:9216].rearrange("p (c f) -> p c f", c=3)
    tmps = WK[:, 0:384]
    hb = WK[:, 384:768]
    hTe = WK[:, 768:1152].rearrange("p (c f) -> p c f", c=3)
    for e in range(8):
        for nb in range(8):
            n0 = sum(NBW[:nb])
            NW = NBW[nb]
            nc3 = NW // 128
            b.wait("sync", "pe")
            b.dma("sync", w1s[:, :, 0:NW],
                  dr["moe_w1T"][e, :, n0:n0 + NW].rearrange("(c p) f -> p c f", p=128), "we1")
            b.dma("sync", w2s[:, :, 0:NW],
                  dr["moe_w2T"][e, :, n0:n0 + NW].rearrange("(c p) f -> p c f", p=128), "we2")
            b.dma("sync", w3s[:, 0:nc3, :],
                  dr["moe_w3T"][e, n0:n0 + NW, :].rearrange("(c p) f -> p c f", p=128), "we3")
            b.wait_dma("tensor", "we1")
            b.wait_dma("tensor", "we2")
            b.wait_dma("tensor", "we3")
            pe_mm, se_silu, ve_tt, ve_add = {}, {}, {}, {}

            def stage_a(i):
                gb_, ub_ = (0, 1) if i % 2 == 0 else (5, 6)
                if i >= 2:
                    b.wait("tensor", "se", se_silu[i - 2])
                    b.wait("tensor", "ve", ve_tt[i - 2])
                for c in range(8):
                    b.mm(ps[ub_][:, 0:NW], yT[:, c, i * 128:(i + 1) * 128],
                         w1s[:, c, 0:NW], start=(c == 0), stop=(c == 7))
                for c in range(8):
                    b.mm(ps[gb_][:, 0:NW], yT[:, c, i * 128:(i + 1) * 128],
                         w2s[:, c, 0:NW], start=(c == 0), stop=(c == 7))
                pe_mm[i] = b.n["pe"]

            def stage_b(i):
                gb_, ub_ = (0, 1) if i % 2 == 0 else (5, 6)
                b.wait("scalar", "pe", pe_mm[i])
                b.wait("scalar", "ve")
                b.act(tmps[:, 0:NW], ps[ub_][:, 0:NW], AF.Silu)
                se_silu[i] = b.n["se"]
                b.wait("vector", "se", se_silu[i])
                b.wait("vector", "pe")
                b.tt(hb[:, 0:NW], tmps[:, 0:NW], ps[gb_][:, 0:NW], ALU.mult)
                ve_tt[i] = b.n["ve"]

            def stage_c(i):
                b.wait("tensor", "ve", ve_tt[i])
                if i >= 1:
                    b.wait("tensor", "ve", ve_add[i - 1])
                for c in range(nc3):
                    b.tp(ps[2][:, c * 128:(c + 1) * 128], hb[:, c * 128:(c + 1) * 128], ident)
                b.wait("vector", "pe")
                for c in range(nc3):
                    b.cp(hTe[:, c, :], ps[2][:, c * 128:(c + 1) * 128], drain=False)
                b.wait("tensor", "ve")
                for mh in range(2):
                    for c in range(nc3):
                        b.mm(ps[3 + mh][:, 0:512], hTe[:, c, :],
                             w3s[:, c, mh * 512:(mh + 1) * 512],
                             start=(c == 0), stop=(c == nc3 - 1))
                b.wait("vector", "pe")
                b.stt(comb[:, i, 0:512], ps[3][:, 0:512], C["wfull"][:, i, e:e + 1],
                      comb[:, i, 0:512], ALU.mult, ALU.add)
                b.stt(comb[:, i, 512:1024], ps[4][:, 0:512], C["wfull"][:, i, e:e + 1],
                      comb[:, i, 512:1024], ALU.mult, ALU.add)
                ve_add[i] = b.n["ve"]

            stage_a(0)
            stage_b(0)
            for i in range(1, NT):
                stage_a(i)
                stage_c(i - 1)
                stage_b(i)
            stage_c(NT - 1)
            b.barrier()

    xrd = WK[:, 0:2048].rearrange("p (c f) -> p c f", c=2)
    for i in range(NT):
        if i >= 2:
            b.wait("sync", "ve")
        b.dma("sync", xrd[:, i % 2, :], scr_x3[i * 128:(i + 1) * 128, :], f"x3r{i % 2}")
        b.wait_dma("vector", f"x3r{i % 2}")
        b.tt(stream[:, i, :], stream[:, i, :], xrd[:, i % 2, :], ALU.add)
        b.red(C["stats"][:, i:i + 1], stream[:, i, :], op="sum")
    postln_finish(b, C, stream, WK[:, 2048:3072], WK[:, 3072:4096], eps=1e-5)


def _consts():
    r = np.arange(128)
    cc = {}
    cc["ident"] = np.eye(128, dtype=np.float32)
    cc["mask_causal"] = np.where(r[None, :] > r[:, None], np.float32(MASK_VAL), np.float32(0.0)).astype(np.float32)
    cc["mask_ls"] = np.where(r[None, :] < r[:, None], np.float32(0.0), np.float32(NEG_BIG)).astype(np.float32)
    cc["mask_us"] = np.where(r[None, :] > r[:, None], np.float32(0.0), np.float32(NEG_BIG)).astype(np.float32)
    cc["mask_ui"] = np.where(r[None, :] >= r[:, None], np.float32(0.0), np.float32(NEG_BIG)).astype(np.float32)
    cc["ones_ui"] = np.triu(np.ones((128, 128), dtype=np.float32))
    cc["ones_r"] = np.ones((1, 128), dtype=np.float32)
    cc["ones_col"] = np.ones((128, 1), dtype=np.float32)
    cc["iota_f"] = np.broadcast_to(np.arange(128, dtype=np.float32)[None, :], (128, 128)).copy()
    cc["iota_p"] = np.broadcast_to(np.arange(128, dtype=np.float32)[:, None], (128, 128)).copy()
    cc["e127"] = np.zeros((128, 1), dtype=np.float32); cc["e127"][127, 0] = 1.0
    return cc


def _prep_inputs(inp):
    f = np.float32
    def t(a):
        return np.ascontiguousarray(np.asarray(a, dtype=f).T)
    def c(a):
        return np.ascontiguousarray(np.asarray(a, dtype=f))
    def bc(a, n=D):
        return np.ascontiguousarray(np.broadcast_to(np.asarray(a, dtype=f), (128, n)))

    m = {}
    m["l0_wqT"] = t(inp["L0_attn_q_proj_weight"])
    m["l0_wkT"] = t(inp["L0_attn_k_proj_weight"])
    m["l0_wvT"] = t(inp["L0_attn_v_proj_weight"])
    m["l0_woT"] = t(inp["L0_attn_o_proj_weight"])
    m["l0_wg"] = c(inp["L0_ffn_w_gate"])
    m["l0_wu"] = c(inp["L0_ffn_w_up"])
    m["l0_wd"] = c(inp["L0_ffn_w_down"])
    m["l1_wq"] = c(inp["L1_attn_Wq"]); m["l1_wk"] = c(inp["L1_attn_Wk"])
    m["l1_wv"] = c(inp["L1_attn_Wv"]); m["l1_wg"] = c(inp["L1_attn_Wg"])
    m["l1_wo"] = c(inp["L1_attn_Wo"]); m["l1_w2"] = c(inp["L1_attn_W2"])
    m["l1_wab"] = np.ascontiguousarray(
        np.concatenate([np.asarray(inp["L1_attn_Wa"], f), np.asarray(inp["L1_attn_Wb"], f)], axis=1))
    m["l1_convq"] = np.ascontiguousarray(np.asarray(inp["L1_attn_conv_q_w"], f).reshape(8, 128, 4).transpose(1, 0, 2))
    m["l1_convk"] = np.ascontiguousarray(np.asarray(inp["L1_attn_conv_k_w"], f).reshape(8, 128, 4).transpose(1, 0, 2))
    m["l1_convv"] = np.ascontiguousarray(np.asarray(inp["L1_attn_conv_v_w"], f).reshape(16, 128, 4).transpose(1, 0, 2))
    m["l1_dtb"] = bc(inp["L1_attn_dt_bias"], 4)
    m["l1_alog"] = bc(inp["L1_attn_A_log"], 4)
    m["bc_ln1g"] = bc(inp["L1_attn_ln1_g"]); m["bc_ln1b"] = bc(inp["L1_attn_ln1_b"])
    m["bc_ln2g"] = bc(inp["L1_attn_ln2_g"]); m["bc_ln2b"] = bc(inp["L1_attn_ln2_b"])
    m["bc_rmsin"] = bc(inp["L1_attn_rms_in_w"]); m["bc_b2"] = bc(inp["L1_attn_b2"])
    m["bc_moerms"] = bc(inp["L1_ffn_rms_w"]); m["bc_onorm"] = bc(inp["L1_attn_o_norm_w"], 512)
    gwT = np.asarray(inp["L1_ffn_gate_w"], f).T
    m["moe_gwT"] = np.ascontiguousarray(gwT.reshape(8, 128, 8).transpose(1, 0, 2))
    def img_ldT(wT, n0, nw):
        blk = wT[:, n0:n0 + nw]
        return blk.reshape(8, 128, nw).transpose(1, 0, 2).reshape(128, 8 * nw)
    NBW = [384] * 7 + [128]
    w1T = np.asarray(inp["L1_ffn_W1"], f).transpose(0, 2, 1)
    w2T = np.asarray(inp["L1_ffn_W2"], f).transpose(0, 2, 1)
    w3T = np.asarray(inp["L1_ffn_W3"], f).transpose(0, 2, 1)
    im1, im2, im3 = [], [], []
    for e in range(8):
        r1, r2, r3 = [], [], []
        for nb in range(8):
            n0, nw = sum(NBW[:nb]), NBW[nb]
            pad = 384 - nw
            a = img_ldT(w1T[e], n0, nw)
            bb = img_ldT(w2T[e], n0, nw)
            cc = w3T[e][n0:n0 + nw, :].reshape(nw // 128, 128, 1024).transpose(1, 0, 2).reshape(128, -1)
            if pad:
                a = np.concatenate([a, np.zeros((128, 8 * pad), f)], 1)
                bb = np.concatenate([bb, np.zeros((128, 8 * pad), f)], 1)
                cc = np.concatenate([cc, np.zeros((128, (pad // 128) * 1024), f)], 1)
            r1.append(a); r2.append(bb); r3.append(cc)
        im1.append(np.stack(r1)); im2.append(np.stack(r2)); im3.append(np.stack(r3))
    m["moe_w1i"] = np.ascontiguousarray(np.stack(im1))
    m["moe_w2i"] = np.ascontiguousarray(np.stack(im2))
    m["moe_w3i"] = np.ascontiguousarray(np.stack(im3))
    l0p = []
    for p in range(8):
        cols = slice(p * 128, (p + 1) * 128)
        l0p.append(np.concatenate([
            m["l0_wqT"][:, cols].reshape(8, 128, 128).transpose(1, 0, 2).reshape(128, 1024),
            m["l0_wkT"][:, cols].reshape(8, 128, 128).transpose(1, 0, 2).reshape(128, 1024),
            m["l0_wvT"][:, cols].reshape(8, 128, 128).transpose(1, 0, 2).reshape(128, 1024)], 1))
    m["l0_pairs"] = np.ascontiguousarray(np.stack(l0p))
    wd = m["l0_wd"]
    m["l0_wdi"] = np.ascontiguousarray(np.stack(
        [wd[:, nb * 256:(nb + 1) * 256].reshape(8, 128, 256).transpose(1, 0, 2).reshape(128, 2048)
         for nb in range(11)]))
    m.update(_consts())
    return m


def kernel(**inputs):
    return _run(inputs, upto="out")


def _run(inputs, upto="out", s1_limit=None, s2_limit=None, start="x0", s3_limit=None, xin=None):
    x0 = np.asarray(inputs["L0_attn_in_features"], dtype=np.float32)
    shared = _prep_inputs(inputs)
    in_maps = []
    for i in range(8):
        mi = dict(shared)
        mi["x0"] = np.ascontiguousarray(x0[i])
        in_maps.append(mi)
    nc, _ = _build(upto=upto, s1_limit=s1_limit, s2_limit=s2_limit, start=start, s3_limit=s3_limit)
    if xin is not None:
        for i in range(8):
            in_maps[i]["xin"] = np.ascontiguousarray(np.asarray(xin[i % len(xin)], np.float32))
    res = run_bass_kernel_spmd(nc, in_maps, core_ids=list(range(8)))
    return np.stack([np.asarray(res.results[i]["out"]) for i in range(8)], axis=0)


# version 07
# speedup vs baseline: 1.1121x; 1.0366x over previous
import numpy as np
from contextlib import ExitStack
import concourse.bass as bass
import concourse.mybir as mybir
from concourse.bass_utils import run_bass_kernel_spmd

F32 = mybir.dt.float32
AF = mybir.ActivationFunctionType
ALU = mybir.AluOpType
AX = mybir.AxisListType.X

T = 2048
D = 1024
NT = 16
MASK_VAL = -1.0e8
NEG_BIG = -1.0e9

FP4_MID = [0.25, 0.75, 1.25, 1.75, 2.5, 3.5, 5.0]
FP4_WGT = [0.5, 0.5, 0.5, 0.5, 1.0, 1.0, 2.0]


class B:
    ENGS = ("tensor", "vector", "scalar", "sync", "gpsimd")

    def __init__(self, nc, ctx):
        self.nc = nc
        self.ctx = ctx
        self.q = {e: [] for e in self.ENGS}
        self.n = {"pe": 0, "ve": 0, "se": 0}
        self.dn = {}
        self.sems = {}
        self.pend = {e: {} for e in self.ENGS}

    def get_sem(self, name):
        if name not in self.sems:
            self.sems[name] = self.ctx.enter_context(self.nc.semaphore(name))
        return self.sems[name]

    def _flush_pend(self, eng, keep_one):
        pend = self.pend[eng]
        emb = None
        if pend:
            items = sorted(pend.items())
            if keep_one:
                emb = items.pop()
            for ch, v in items:
                self.q[eng].append(lambda e, ch=ch, v=v: e.wait_ge(self.sems[ch], v))
            self.pend[eng] = {}
        return emb

    def emit(self, eng, fn, inc):
        emb = self._flush_pend(eng, keep_one=(eng in ("tensor", "vector", "scalar")))
        if inc is not None:
            self.n[inc] += 1
            self.get_sem(inc)
        def thunk(e, fn=fn, inc=inc, emb=emb):
            ins = fn(e)
            if emb is not None:
                ins._wait_ge(self.sems[emb[0]], emb[1])
            if inc is not None:
                ins.then_inc(self.sems[inc], 1)
        self.q[eng].append(thunk)

    def wait(self, eng, ch, val=None):
        v = self.n[ch] if val is None else val
        self.get_sem(ch)
        p = self.pend[eng]
        if p.get(ch, -1) < v:
            p[ch] = v

    def dma(self, queue, out, in_, sem):
        self._flush_pend(queue, keep_one=False)
        self.dn[sem] = self.dn.get(sem, 0) + 16
        self.get_sem(sem)
        def thunk(e, out=out, in_=in_, sem=sem):
            e.dma_start(out=out, in_=in_).then_inc(self.sems[sem], 16)
        self.q[queue].append(thunk)

    def wait_dma(self, eng, sem, val=None):
        v = self.dn[sem] if val is None else val
        self.get_sem(sem)
        p = self.pend[eng]
        if p.get(sem, -1) < v:
            p[sem] = v

    def barrier(self):
        vals = dict(self.n)
        dvals = dict(self.dn)
        for eng in self.ENGS:
            for ch, v in vals.items():
                if v > 0:
                    self.wait(eng, ch, v)
            for sm, v in dvals.items():
                if v > 0:
                    self.wait_dma(eng, sm, v)

    def _drain(self, eng):
        ch = {"vector": "ve", "scalar": "se", "tensor": "pe"}[eng]
        if self.n[ch] > 0:
            self.wait(eng, ch)

    def mm(self, out, lhsT, rhs, start, stop):
        self.emit("tensor", lambda e: e.matmul(out, lhsT, rhs, start=start, stop=stop), "pe")

    def tp(self, out, in_, ident):
        self.emit("tensor", lambda e: e.transpose(out, in_, ident), "pe")

    def cp(self, out, in_, eng="vector", drain=True):
        if drain:
            self._drain(eng)
        ch = "ve" if eng == "vector" else "se"
        if eng == "vector":
            self.emit(eng, lambda e: e.tensor_copy(out, in_), ch)
        else:
            self.emit(eng, lambda e: e.copy(out, in_), ch)

    def tt(self, out, in0, in1, op, drain=True):
        if drain:
            self._drain("vector")
        self.emit("vector", lambda e: e.tensor_tensor(out, in0, in1, op), "ve")

    def ts(self, out, in0, s1, s2, op0, op1=None, accum=None, drain=True):
        if drain:
            self._drain("vector")
        if op1 is None:
            self.emit("vector", lambda e: e.tensor_scalar(out, in0, s1, None, op0, accum_out=accum), "ve")
        else:
            self.emit("vector", lambda e: e.tensor_scalar(out, in0, s1, s2, op0, op1, accum_out=accum), "ve")

    def stt(self, out, in0, scalar, in1, op0, op1, accum=None, drain=True):
        if drain:
            self._drain("vector")
        self.emit("vector", lambda e: e.scalar_tensor_tensor(
            out=out, in0=in0, scalar=scalar, in1=in1, op0=op0, op1=op1, accum_out=accum), "ve")

    def red(self, out, in_, op="max", negate=False, absval=False, drain=True):
        if drain:
            self._drain("vector")
        if op == "max":
            self.emit("vector", lambda e: e.reduce_max(out, in_, axis=AX, negate=negate,
                                                       apply_absolute_value=absval), "ve")
        else:
            self.emit("vector", lambda e: e.reduce_sum(out, in_, axis=AX, negate=negate,
                                                       apply_absolute_value=absval), "ve")

    def recip(self, out, in_, drain=True):
        if drain:
            self._drain("vector")
        self.emit("vector", lambda e: e.reciprocal(out, in_), "ve")

    def memset(self, ap, val, drain=False):
        if drain:
            self._drain("vector")
        self.emit("vector", lambda e: e.memset(ap, val), "ve")

    def act(self, out, in_, func, bias=0.0, scale=1.0, accum=None, drain=True):
        if drain:
            self._drain("scalar")
        self.emit("scalar", lambda e: e.activation(out, in_, func, bias=bias, scale=scale,
                                                   accum_out=accum), "se")

    def smul(self, out, in_, mul, drain=True):
        if drain:
            self._drain("scalar")
        self.emit("scalar", lambda e: e.mul(out, in_, mul), "se")



def vgaps(b, C):
    b.wait("vector", "ve")


def sgaps(b, C):
    b.wait("scalar", "se")


def tp_batch(b, C, ps, items, banks=(0, 1, 2, 3)):
    ident = C["ident"][:]
    nb = len(banks)
    ve_cp = {}
    b.wait("tensor", "ve")
    for k, (src, dst) in enumerate(items):
        slot = ps[banks[k % nb]][:, (k // nb % 4) * 128:(k // nb % 4) * 128 + 128]
        if k >= nb:
            b.wait("tensor", "ve", ve_cp[k - nb])
        b.tp(slot, src, ident)
        b.wait("vector", "pe")
        b.cp(dst, slot, drain=False)
        ve_cp[k] = b.n["ve"]


def bcast_col(b, C, ps, col_ap, dst_tile, bank=7):
    ident = C["ident"][:]
    b.wait("tensor", "ve")
    b.tp(ps[bank][0:1, 0:128], col_ap, ident)
    b.wait("vector", "pe")
    b.cp(C["rowb"][0:1, :], ps[bank][0:1, 0:128], drain=False)
    b.wait("tensor", "ve")
    b.mm(ps[bank][:, 128:256], C["ones_r"][:], C["rowb"][0:1, :], start=True, stop=True)
    b.wait("vector", "pe")
    b.cp(dst_tile, ps[bank][:, 128:256], drain=False)


def rstd_chain(b, C, v_col, out_col, n_feats, eps, extra_scale=None, is_sum=False):
    t1, t2 = C["st_t1"], C["st_t2"]
    w = v_col.shape[1]
    if is_sum:
        b.ts(v_col, v_col, 1.0, eps, ALU.mult, ALU.add)
    else:
        b.ts(v_col, v_col, 1.0 / n_feats, eps, ALU.mult, ALU.add)
    b.wait("scalar", "ve")
    b.act(t1[:, 0:w], v_col, AF.Sqrt)
    b.wait("vector", "se")
    b.recip(out_col, t1[:, 0:w])
    vgaps(b, C)
    for _ in range(2):
        b.tt(t1[:, 0:w], out_col, out_col, ALU.mult)
        vgaps(b, C)
        b.tt(t1[:, 0:w], t1[:, 0:w], v_col, ALU.mult)
        vgaps(b, C)
        b.ts(t1[:, 0:w], t1[:, 0:w], -0.5, 1.5, ALU.mult, ALU.add)
        vgaps(b, C)
        b.tt(out_col, out_col, t1[:, 0:w], ALU.mult)
        vgaps(b, C)
    if extra_scale is not None:
        b.ts(out_col, out_col, extra_scale, None, ALU.mult)
        vgaps(b, C)


def fp4_quant(b, C, x_ap, out_ap, width, wrk):
    G = width // 16
    absx, cmp = wrk["absx"], wrk["cmp"]
    mx, scale = wrk["mx"], wrk["scale"]
    b.wait("scalar", "ve")
    b.act(absx, x_ap, AF.Abs)
    b.wait("vector", "se")
    xv = absx.rearrange("p (g c) -> p g c", c=16)
    b.red(mx, xv, op="max")
    b.ts(scale, mx, 1.0 / 6.0, 1e-12, ALU.mult, ALU.max)
    rsc = wrk["thr"]
    b.recip(rsc, scale)
    rs_b = rsc.unsqueeze(2).broadcast_to([128, G, 16])
    sc_b = scale.unsqueeze(2).broadcast_to([128, G, 16])
    b.tt(xv, xv, rs_b, ALU.mult)
    b.memset(out_ap, 0.0)
    for j in range(7):
        b.ts(cmp, absx, FP4_MID[j], None, ALU.is_gt)
        b.stt(out_ap, cmp, FP4_WGT[j], out_ap, ALU.mult, ALU.add)
    b.wait("scalar", "ve")
    b.act(cmp, x_ap, AF.Sign)
    b.wait("vector", "se")
    b.tt(out_ap, out_ap, cmp, ALU.mult)
    ov = out_ap.rearrange("p (g c) -> p g c", c=16)
    b.tt(ov, ov, sc_b, ALU.mult)


def _build(upto="out", s1_limit=None, s2_limit=None, start="x0", s3_limit=None):
    nc = bass.Bass("TRN2", target_bir_lowering=False)
    b = B(None, None)

    dr = {}
    def din(name, shape):
        dr[name] = nc.dram_tensor(name, shape, F32, kind="ExternalInput").ap()
        return dr[name]
    def dint(name, shape):
        dr[name] = nc.dram_tensor(name, shape, F32, kind="Internal").ap()
        return dr[name]

    x0_d = din("x0", [T, D])
    xin_d = nc.dram_tensor("xin", [T, D], F32, kind="ExternalInput").ap() if start != "x0" else None
    din("l0_woT", [D, D])
    din("l0_wg", [2816, D]); din("l0_wu", [2816, D])
    din("l1_wq", [D, 1024]); din("l1_wk", [D, 1024]); din("l1_wv", [D, 2048])
    din("l1_wg", [D, 2048]); din("l1_wo", [2048, D]); din("l1_w2", [D, D])
    din("l1_wab", [D, 8])
    din("l1_convq", [128, 8, 4]); din("l1_convk", [128, 8, 4]); din("l1_convv", [128, 16, 4])
    din("l1_dtb", [128, 4]); din("l1_alog", [128, 4])
    din("bc_ln1g", [128, D]); din("bc_ln1b", [128, D]); din("bc_ln2g", [128, D])
    din("bc_ln2b", [128, D]); din("bc_rmsin", [128, D]); din("bc_b2", [128, D])
    din("bc_moerms", [128, D]); din("bc_onorm", [128, 512])
    din("moe_gwT", [128, 8, 8])
    din("moe_w1i", [8, 8, 128, 3072]); din("moe_w2i", [8, 8, 128, 3072])
    din("moe_w3i", [8, 8, 128, 3072])
    din("l0_pairs", [8, 128, 3072]); din("l0_wdi", [11, 128, 2048])
    din("ones_col", [128, 1]); din("iota_f", [128, 128]); din("iota_p", [128, 128])
    din("e127", [128, 1])
    din("ident", [128, 128]); din("mask_causal", [128, 128])
    din("mask_ls", [128, 128]); din("mask_us", [128, 128]); din("mask_ui", [128, 128])
    din("ones_ui", [128, 128]); din("ones_r", [1, 128])

    out_d = nc.dram_tensor("out", [T, D], F32, kind="ExternalOutput").ap()

    scr_x1qT = dint("scr_x1qT", [NT, 128, 1024])
    scr_x2 = dint("scr_x2", [T, D])
    scr_qT = dint("scr_qT", [NT, 128, 1024])
    scr_kT = dint("scr_kT", [NT, 128, 1024])
    scr_vhat = dint("scr_vhat", [T, 2048])
    scr_sgate = dint("scr_sgate", [T, 2048])
    scr_osc = dint("scr_osc", [T, 2048])
    scr_x3 = dint("scr_x3", [T, D])

    with ExitStack() as ctx:
        def sb(name, shape, dt=F32):
            return ctx.enter_context(nc.sbuf_tensor(name, shape, dt))

        A1 = sb("A1", [128, 16384])
        A2 = sb("A2", [128, 16384])
        WA = sb("WA", [128, 9216])
        WK = sb("WK", [128, 8704])
        C = {}
        C["ident"] = sb("c_ident", [128, 128])
        C["mask_causal"] = sb("c_mcz", [128, 128])
        C["mask_ls"] = sb("c_mls", [128, 128])
        C["mask_us"] = sb("c_mus", [128, 128])
        C["mask_ui"] = sb("c_mui", [128, 128])
        C["ones_ui"] = sb("c_oui", [128, 128])
        C["ones_r"] = sb("c_or", [1, 128])
        C["junk1"] = sb("c_j1", [128, 4])
        C["junk2"] = sb("c_j2", [128, 4])
        C["junkS1"] = sb("c_js1", [128, 4])
        C["junkS2"] = sb("c_js2", [128, 4])
        stats = sb("c_stats", [128, 96])
        C["stats"] = stats
        C["st_t1"] = stats[:, 64:80]
        C["st_t2"] = stats[:, 80:96]
        C["gb"] = sb("c_gb", [128, 16, 8])
        C["wfull"] = sb("c_wfull", [128, 16, 8])
        C["gwT"] = sb("c_gwT", [128, 8, 8])
        C["convq"] = sb("c_convq", [128, 8, 4])
        C["convk"] = sb("c_convk", [128, 8, 4])
        C["convv"] = sb("c_convv", [128, 16, 4])
        C["coef"] = sb("c_coef", [128, 4])
        C["dtb"] = sb("c_dtb", [128, 4])
        C["ones_col"] = sb("c_onescol", [128, 1])
        C["rowb"] = sb("c_rowb", [1, 128])
        C["iota_f"] = sb("c_iotaf", [128, 128])
        C["iota_p"] = sb("c_iotap", [128, 128])
        C["e127"] = sb("c_e127", [128, 1])
        C["rank"] = sb("c_rank", [128, 16, 8])
        C["offrow"] = sb("c_offrow", [1, 8])
        C["wab"] = sb("c_wab", [128, 8, 8])

        ps = [ctx.enter_context(nc.psum_tensor(f"ps{i}", [128, 512], F32)) for i in range(8)]

        b.nc, b.ctx = nc, ctx
        blk = ctx.enter_context(nc.Block())

        ident = C["ident"]

        for nm in ("ident", "mask_causal", "mask_ls", "mask_us", "mask_ui", "ones_ui"):
            b.dma("sync", C[nm][:], dr[nm], "c0")
        b.dma("sync", C["ones_r"][:], dr["ones_r"], "c0")
        b.dma("sync", C["gwT"][:], dr["moe_gwT"], "c0")
        b.dma("sync", C["convq"][:], dr["l1_convq"], "c0")
        b.dma("sync", C["convk"][:], dr["l1_convk"], "c0")
        b.dma("sync", C["convv"][:], dr["l1_convv"], "c0")
        b.dma("sync", C["dtb"][:], dr["l1_dtb"], "c0")
        b.dma("sync", C["coef"][:], dr["l1_alog"], "c0")
        b.dma("sync", C["ones_col"][:], dr["ones_col"], "c0")
        b.dma("sync", C["iota_f"][:], dr["iota_f"], "c0")
        b.dma("sync", C["iota_p"][:], dr["iota_p"], "c0")
        b.dma("sync", C["e127"][:], dr["e127"], "c0")
        b.dma("sync", C["wab"][:], dr["l1_wab"][:].rearrange("(c p) f -> p c f", p=128), "c0")
        b.wait_dma("vector", "c0")
        b.memset(C["junk1"][:], 0.0)
        b.memset(C["junkS1"][:], 0.0)
        b.wait_dma("scalar", "c0")
        b.act(C["coef"][:], C["coef"][:], AF.Exp, scale=1.0)
        b.wait("vector", "se")
        b.ts(C["coef"][:], C["coef"][:], -1.0, None, ALU.mult)
        b.barrier()

        x0T = A1[:].rearrange("p (c t) -> p c t", c=8)
        stream = A1[:].rearrange("p (c t) -> p c t", c=16)
        aoT = A2[:].rearrange("p (c t) -> p c t", c=8)
        A2_16 = A2[:].rearrange("p (c t) -> p c t", c=16)

        if start == "x0":
            emit_s1(nc, b, C, dr, ps, A1, A2, WA, WK, x0T, stream, aoT, x0_d, limit=s1_limit)
        else:
            xld = WK[:, 0:2048].rearrange("p (c f) -> p c f", c=2)
            for i in range(NT):
                b.dma("sync", xld[:, i % 2, :], xin_d[i * 128:(i + 1) * 128, :], f"xi{i % 2}")
                b.wait_dma("vector", f"xi{i % 2}")
                b.cp(stream[:, i, :], xld[:, i % 2, :], drain=False)
                b.dma("gpsimd", scr_x2[i * 128:(i + 1) * 128, :], xin_d[i * 128:(i + 1) * 128, :], "x2w")
                if i < NT - 2:
                    b.wait("sync", "ve")
            b.barrier()

        if upto == "x1":
            finish(b, stream, out_d)
            run_block(b, blk)
            return nc, b

        if start == "x0":
            emit_s2(nc, b, C, dr, ps, A1, A2, WA, WK, stream, A2_16, scr_x1qT, scr_x2, limit=s2_limit)

        if upto == "x2":
            finish(b, stream, out_d)
            run_block(b, blk)
            return nc, b

        if start in ("x0", "x2"):
            emit_s3(nc, b, C, dr, ps, A1, A2, WA, WK, stream, A2_16,
                    scr_x2, scr_qT, scr_kT, scr_vhat, scr_sgate, scr_osc, limit=s3_limit)

        if upto == "x3":
            finish(b, stream, out_d)
            run_block(b, blk)
            return nc, b

        emit_s4(nc, b, C, dr, ps, A1, A2, WA, WK, stream, A2_16, scr_x3)

        finish(b, stream, out_d)
        run_block(b, blk)
    return nc, b


def finish(b, stream, out_d):
    b.barrier()
    for i in range(NT):
        b.dma("gpsimd", out_d[i * 128:(i + 1) * 128, :], stream[:, i, :], "outw")
    b.wait_dma("gpsimd", "outw")


def run_block(b, blk):
    for eng in B.ENGS:
        b._flush_pend(eng, keep_one=False)
    for eng in B.ENGS:
        thunks = b.q[eng]
        getattr(blk, eng)(lambda e, thunks=thunks: [t(e) for t in thunks])


def emit_s1(nc, b, C, dr, ps, A1, A2, WA, WK, x0T, stream, aoT, x0_d, limit=None):
    ident = C["ident"][:]
    Vp = WK[:, 0:2048].rearrange("p (c t) -> p c t", c=16)
    srow = WK[:, 2048:4096]
    prow = WK[:, 4096:6144]
    ptslab = WK[:, 6144:8192].rearrange("p (c t) -> p c t", c=16)
    avo2 = WK[:, 8192:8448]
    xtile = WK[:, 0:2048].rearrange("p (c t) -> p c t", c=2)
    wqb = WA[:, 0:1024].rearrange("p (c t) -> p c t", c=8)
    wkb = WA[:, 1024:2048].rearrange("p (c t) -> p c t", c=8)
    wvb = WA[:, 2048:3072].rearrange("p (c t) -> p c t", c=8)
    QT = WA[:, 3072:5120]
    KT = WA[:, 5120:7168]
    woT = WA[:, 0:8192].rearrange("p (c t) -> p c t", c=8)

    ve_tile = {}
    for i in range(NT):
        if i >= 2:
            b.wait("sync", "ve", ve_tile[i - 2])
        b.dma("sync", xtile[:, i % 2, :], x0_d[i * 128:(i + 1) * 128, :], f"xt{i % 2}")
        b.wait_dma("tensor", f"xt{i % 2}")
        if i >= 2:
            b.wait("tensor", "ve", ve_tile[i - 2])
        bk = 2 * (i % 2)
        for c in range(8):
            slot = ps[bk + c // 4][:, (c % 4) * 128:(c % 4) * 128 + 128]
            b.tp(slot, xtile[:, i % 2, c * 128:(c + 1) * 128], ident)
        b.wait("vector", "pe")
        for c in range(8):
            slot = ps[bk + c // 4][:, (c % 4) * 128:(c % 4) * 128 + 128]
            b.cp(x0T[:, c, i * 128:(i + 1) * 128], slot, drain=False)
        ve_tile[i] = b.n["ve"]
    b.barrier()
    if limit == "p0":
        return

    for p in range(8):
        if p > 0:
            b.wait("sync", "pe")
        b.dma("sync", WA[:, 0:3072].rearrange("p (c f) -> p c f", c=24),
              dr["l0_pairs"][p].rearrange("p (c f) -> p c f", c=24), "wpair")
        b.wait_dma("tensor", "wpair")

        for dst, wb, psl in ((QT, wqb, (0, 1)), (KT, wkb, (2, 3))):
            for n in range(4):
                pp = ps[psl[n % 2]]
                if n >= 2:
                    b.wait("tensor", "ve")
                for c in range(8):
                    b.mm(pp[:, 0:512], wb[:, c, :], x0T[:, c, n * 512:(n + 1) * 512],
                         start=(c == 0), stop=(c == 7))
                b.wait("vector", "pe")
                b.cp(dst[:, n * 512:(n + 1) * 512], pp[:, 0:512])
        if limit == "pqk" and p == 0:
            b.barrier()
            return
        for i in range(NT):
            pp = ps[4 + (i % 2)]
            if i >= 2:
                b.wait("tensor", "ve")
            for c in range(8):
                b.mm(pp[:, 0:128], x0T[:, c, i * 128:(i + 1) * 128], wvb[:, c, :],
                     start=(c == 0), stop=(c == 7))
            b.wait("vector", "pe")
            b.cp(Vp[:, i, :], pp[:, 0:128])
        b.barrier()
        if limit == "pv" and p == 0:
            return

        tasks = [(i, hh) for i in range(NT) for hh in range(2)]
        srow2 = (srow, WA[:, 7168:9216])
        se_exp, pe_tp, pe_av, ve_sm, se_mul = {}, {}, {}, {}, {}
        ve_b67 = {6: 0, 7: 0}
        ve_av45 = {4: 0, 5: 0}

        def stage_sc(k):
            i, hh = tasks[k]
            L = (i + 1) * 128
            hsl = slice(hh * 64, hh * 64 + 64)
            sr = srow2[k % 2]
            if k >= 2:
                b.wait("vector", "se", se_exp[k - 2])
            nkg = (L + 511) // 512
            for kg in range(nkg):
                nk = min(512, L - kg * 512)
                bank = 6 + kg % 2
                b.wait("tensor", "ve", ve_b67[bank])
                b.mm(ps[bank][:, 0:nk], QT[hsl, i * 128:(i + 1) * 128],
                     KT[hsl, kg * 512:kg * 512 + nk], start=True, stop=True)
                b.wait("vector", "pe")
                b.ts(sr[:, kg * 512:kg * 512 + nk], ps[bank][:, 0:nk], 0.125,
                     None, ALU.mult, drain=False)
                ve_b67[bank] = b.n["ve"]

        def stage_sm(k):
            i, hh = tasks[k]
            L = (i + 1) * 128
            sr = srow2[k % 2]
            b.tt(sr[:, i * 128:L], sr[:, i * 128:L], C["mask_causal"][:], ALU.add)
            nmx = C["stats"][:, 8 + k % 2:9 + k % 2]
            ssum = C["stats"][:, 10 + k % 2:11 + k % 2]
            b.red(nmx, sr[:, 0:L], op="max", negate=True)
            b.wait("scalar", "ve")
            if k >= 1:
                b.wait("scalar", "pe", pe_tp[k - 1])
            b.act(prow[:, 0:L], sr[:, 0:L], AF.Exp, bias=nmx, accum=ssum)
            se_exp[k] = b.n["se"]
            b.wait("vector", "se", se_exp[k])
            b.recip(C["stats"][:, 12 + k % 2:13 + k % 2], ssum)
            ve_sm[k] = b.n["ve"]

        def stage_ta(k):
            i, hh = tasks[k]
            hsl = slice(hh * 64, hh * 64 + 64)
            b.wait("tensor", "se", se_exp[k])
            b.wait("tensor", "ve")
            ve_copy = {}
            for j in range(i + 1):
                slot = ps[j % 4][:, (j // 4) * 128:(j // 4) * 128 + 128]
                if j >= 4:
                    b.wait("tensor", "ve", ve_copy[j - 4])
                b.tp(slot, prow[:, j * 128:(j + 1) * 128], ident)
                b.wait("vector", "pe")
                b.cp(ptslab[:, j, :], slot, drain=False)
                ve_copy[j] = b.n["ve"]
            pe_tp[k] = b.n["pe"]
            b.wait("tensor", "ve")
            bank = 4 + k % 2
            b.wait("tensor", "ve", ve_av45[bank])
            if k >= 2:
                b.wait("tensor", "se", se_mul[k - 2])
            for j in range(i + 1):
                b.mm(ps[bank][:, 0:64], ptslab[:, j, :], Vp[:, j, hsl],
                     start=(j == 0), stop=(j == i))
            pe_av[k] = b.n["pe"]
            b.wait("scalar", "pe", pe_av[k])
            b.wait("scalar", "ve", ve_sm[k])
            b.smul(avo2[:, (i % 2) * 128 + hh * 64:(i % 2) * 128 + hh * 64 + 64],
                   ps[bank][:, 0:64], C["stats"][:, 12 + k % 2:13 + k % 2])
            se_mul[k] = b.n["se"]
            if hh == 1:
                b.wait("tensor", "se", se_mul[k])
                b.tp(ps[bank][:, 128:256], avo2[:, (i % 2) * 128:(i % 2) * 128 + 128],
                     ident)
                b.wait("vector", "pe")
                b.cp(aoT[:, p, i * 128:(i + 1) * 128], ps[bank][:, 128:256], drain=False)
            ve_av45[bank] = b.n["ve"]

        stage_sc(0)
        stage_sm(0)
        for k in range(1, len(tasks)):
            stage_sc(k)
            stage_ta(k - 1)
            stage_sm(k)
        stage_ta(len(tasks) - 1)
        if limit == "p1one" and p == 0:
            b.barrier()
            return
    b.barrier()
    if limit == "p1":
        return

    b.dma("sync", woT[:], dr["l0_woT"][:].rearrange("(c p2) f -> p2 c f", p2=128), "wbig")
    ztmp = WK[:, 0:1024]
    xt0 = WK[:, 1024:3072].rearrange("p (c t) -> p c t", c=2)
    sqd = WK[:, 3072:4096]
    b.wait_dma("tensor", "wbig")
    ve_chunk = {}
    for i in range(NT):
        if i >= 2:
            b.wait("sync", "ve", ve_chunk[i - 2])
        b.dma("sync", xt0[:, i % 2, :], x0_d[i * 128:(i + 1) * 128, :], f"xt{i % 2}")
        if i >= 1:
            b.wait("tensor", "ve")
        for mh in range(2):
            pp = ps[mh]
            for c in range(8):
                b.mm(pp[:, 0:512], aoT[:, c, i * 128:(i + 1) * 128],
                     woT[:, c, mh * 512:(mh + 1) * 512], start=(c == 0), stop=(c == 7))
        b.wait("vector", "pe")
        b.wait_dma("vector", f"xt{i % 2}")
        b.tt(stream[:, i, 0:512], ps[0][:, 0:512], xt0[:, i % 2, 0:512], ALU.add)
        b.tt(stream[:, i, 512:1024], ps[1][:, 0:512], xt0[:, i % 2, 512:1024], ALU.add)
        b.red(C["stats"][:, i:i + 1], stream[:, i, :], op="sum")
        ve_chunk[i] = b.n["ve"]
    postln_finish(b, C, stream, ztmp, sqd, eps=1e-5)


def postln_finish(b, C, stream, ztmp, sqd, eps):
    stats = C["stats"]
    zsum, ssq = stats[:, 0:16], stats[:, 16:32]
    nmean, rstd = stats[:, 32:48], stats[:, 48:64]
    vgaps(b, C)
    b.ts(nmean, zsum, -1.0 / 1024.0, None, ALU.mult)
    b.wait("scalar", "ve")
    for i in range(NT):
        b.act(ztmp, stream[:, i, :], AF.Identity, bias=nmean[:, i:i + 1])
        b.cp(stream[:, i, :], ztmp, eng="scalar")
        b.wait("vector", "se")
        b.stt(sqd, stream[:, i, :], 1.0, stream[:, i, :], ALU.mult, ALU.mult,
              accum=ssq[:, i:i + 1])
    vgaps(b, C)
    rstd_chain(b, C, ssq, rstd, 1024, eps)
    b.wait("scalar", "ve")
    for i in range(NT):
        b.act(ztmp, stream[:, i, :], AF.Identity, scale=rstd[:, i:i + 1])
        b.cp(stream[:, i, :], ztmp, eng="scalar")
    b.barrier()


def emit_s2(nc, b, C, dr, ps, A1, A2, WA, WK, stream, A2_16, scr_x1qT, scr_x2, limit=None):
    ident = C["ident"][:]
    NB, NW = 11, 256

    xq = WK[:, 0:1024]
    wrk = {"absx": WK[:, 1024:2048], "cmp": WK[:, 2048:3072],
           "mx": WK[:, 3072:3136], "scale": WK[:, 3136:3200], "thr": WK[:, 3200:3264]}
    xqTb = WK[:, 3264:5312].rearrange("p (s c f) -> p s c f", s=2, c=8)
    ve_done = {}
    for i in range(NT):
        b.wait("vector", "pe")
        fp4_quant(b, C, stream[:, i, :], xq, 1024, wrk)
        bk = 2 * (i % 2)
        if i >= 2:
            b.wait("tensor", "ve", ve_done[i - 2])
        b.wait("tensor", "ve")
        for c in range(8):
            slot = ps[bk + c // 4][:, (c % 4) * 128:(c % 4) * 128 + 128]
            b.tp(slot, xq[:, c * 128:(c + 1) * 128], ident)
        b.wait("vector", "pe")
        if i >= 2:
            b.wait_dma("vector", f"xqw{i % 2}")
        for c in range(8):
            slot = ps[bk + c // 4][:, (c % 4) * 128:(c % 4) * 128 + 128]
            b.cp(xqTb[:, i % 2, c, :], slot, drain=False)
        ve_done[i] = b.n["ve"]
        b.wait("gpsimd", "ve", ve_done[i])
        b.dma("gpsimd", scr_x1qT[i].rearrange("p (c f) -> p c f", c=8),
              xqTb[:, i % 2, :, :], f"xqw{i % 2}")
    b.barrier()
    if limit == "q0":
        return

    for i in range(NT):
        b.memset(A2_16[:, i, :], 0.0)
    b.barrier()

    wrows = WK[:, 0:2048]
    wqo = WK[:, 2048:4096]
    wrkw = {"absx": WK[:, 4096:6144], "cmp": WK[:, 6144:8192],
            "mx": WK[:, 8192:8320], "scale": WK[:, 8320:8448], "thr": WK[:, 8448:8576]}
    wgqT = WA[:, 0:2048].rearrange("p (c f) -> p c f", c=8)
    wuqT = WA[:, 2048:4096].rearrange("p (c f) -> p c f", c=8)
    wdqT = WA[:, 4096:6144].rearrange("p (c f) -> p c f", c=2)
    for nb in range(NB):
        n0 = nb * NW
        for wi, (src, dstT) in enumerate(
                ((dr["l0_wg"], wgqT), (dr["l0_wu"], wuqT), (None, wdqT))):
            b.wait("sync", "ve")
            b.wait("sync", "se")
            if wi < 2:
                b.dma("sync", wrows.rearrange("p (c f) -> p c f", c=2),
                      src[n0:n0 + NW, :].rearrange("(c p) f -> p c f", p=128), "wrow")
            else:
                b.dma("sync", wrows.rearrange("p (c f) -> p c f", c=8),
                      dr["l0_wdi"][nb].rearrange("p (c f) -> p c f", c=8), "wrow")
            b.wait_dma("vector", "wrow")
            b.wait_dma("scalar", "wrow")
            b.wait("vector", "pe")
            fp4_quant(b, C, wrows, wqo, 2048, wrkw)
            b.wait("tensor", "ve")
            if wi < 2:
                items = [(wqo[:, r * 1024 + c * 128: r * 1024 + (c + 1) * 128],
                          dstT[:, c, r * 128:(r + 1) * 128])
                         for r in range(2) for c in range(8)]
            else:
                items = [(wqo[:, c * 256 + nn * 128: c * 256 + (nn + 1) * 128],
                          dstT[:, nn, c * 128:(c + 1) * 128])
                         for c in range(8) for nn in range(2)]
            tp_batch(b, C, ps, items)
        b.barrier()
        if limit is not None and limit == f"w{nb}":
            return

        xqTs = WK[:, 0:2048].rearrange("p (s c f) -> p s c f", s=2, c=8)
        hb = WK[:, 2048:2304]
        tmps = WK[:, 2304:2560]
        hq = WK[:, 2560:2816]
        hqT = WK[:, 2816:3072].rearrange("p (c f) -> p c f", c=2)
        wrkh = {"absx": WK[:, 3072:3328], "cmp": WK[:, 3328:3584],
                "mx": WK[:, 3584:3600], "scale": WK[:, 3600:3616], "thr": WK[:, 3616:3632]}
        pe_mm, se_silu, ve_tt, ve_q, ve_add = {}, {}, {}, {}, {}

        def stage_a(i):
            gb, ub = (0, 1) if i % 2 == 0 else (5, 6)
            if i >= 2:
                b.wait("sync", "pe", pe_mm[i - 2])
            b.dma("sync", xqTs[:, i % 2, :, :],
                  scr_x1qT[i].rearrange("p (c f) -> p c f", c=8), f"xqr{i % 2}")
            b.wait_dma("tensor", f"xqr{i % 2}")
            if i >= 2:
                b.wait("tensor", "se", se_silu[i - 2])
                b.wait("tensor", "ve", ve_tt[i - 2])
            for c in range(8):
                b.mm(ps[gb][:, 0:NW], xqTs[:, i % 2, c, :], wgqT[:, c, :],
                     start=(c == 0), stop=(c == 7))
            for c in range(8):
                b.mm(ps[ub][:, 0:NW], xqTs[:, i % 2, c, :], wuqT[:, c, :],
                     start=(c == 0), stop=(c == 7))
            pe_mm[i] = b.n["pe"]

        def stage_b(i):
            gb, ub = (0, 1) if i % 2 == 0 else (5, 6)
            b.wait("scalar", "pe", pe_mm[i])
            b.wait("scalar", "ve")
            b.act(tmps, ps[gb][:, 0:NW], AF.Silu)
            se_silu[i] = b.n["se"]
            b.wait("vector", "se", se_silu[i])
            b.tt(hb, tmps, ps[ub][:, 0:NW], ALU.mult)
            ve_tt[i] = b.n["ve"]
            if i >= 1:
                b.wait("vector", "pe")
            fp4_quant(b, C, hb, hq, NW, wrkh)
            ve_q[i] = b.n["ve"]

        def stage_c(i):
            b.wait("tensor", "ve", ve_q[i])
            if i >= 1:
                b.wait("tensor", "ve", ve_add[i - 1])
            for c in range(2):
                b.tp(ps[2][:, c * 128:(c + 1) * 128], hq[:, c * 128:(c + 1) * 128], ident)
            b.wait("vector", "pe")
            b.cp(hqT[:, 0, :], ps[2][:, 0:128], drain=False)
            b.cp(hqT[:, 1, :], ps[2][:, 128:256], drain=False)
            b.wait("tensor", "ve")
            for mh in range(2):
                for c in range(2):
                    b.mm(ps[3 + mh][:, 0:512], hqT[:, c, :],
                         wdqT[:, c, mh * 512:(mh + 1) * 512], start=(c == 0), stop=(c == 1))
            b.wait("vector", "pe")
            b.tt(A2_16[:, i, 0:512], A2_16[:, i, 0:512], ps[3][:, 0:512], ALU.add)
            b.tt(A2_16[:, i, 512:1024], A2_16[:, i, 512:1024], ps[4][:, 0:512], ALU.add)
            ve_add[i] = b.n["ve"]

        stage_a(0)
        stage_b(0)
        for i in range(1, NT):
            stage_a(i)
            stage_c(i - 1)
            stage_b(i)
        stage_c(NT - 1)
        b.barrier()
        if limit is not None and limit == f"n{nb}":
            return

    for i in range(NT):
        b.tt(stream[:, i, :], A2_16[:, i, :], stream[:, i, :], ALU.add)
        b.red(C["stats"][:, i:i + 1], stream[:, i, :], op="sum")
    postln_finish(b, C, stream, WK[:, 0:1024], WK[:, 1024:2048], eps=1e-5)
    for i in range(NT):
        b.dma("gpsimd", scr_x2[i * 128:(i + 1) * 128, :], stream[:, i, :], "x2w")
    b.barrier()


def emit_s3(nc, b, C, dr, ps, A1, A2, WA, WK, stream, A2_16,
            scr_x2, scr_qT, scr_kT, scr_vhat, scr_sgate, scr_osc, limit=None):
    ident = C["ident"][:]
    stats = C["stats"]
    hT = A2[:].rearrange("p (c t) -> p c t", c=8)

    g1bc = WK[:, 0:1024]
    b1bc = WK[:, 1024:2048]
    rmbc = WK[:, 2048:3072]
    ztmp = WK[:, 3072:4096]
    sqd = WK[:, 4096:5120]
    b.dma("sync", g1bc, dr["bc_ln1g"], "bc0")
    b.dma("sync", b1bc, dr["bc_ln1b"], "bc1")
    b.dma("sync", rmbc, dr["bc_rmsin"], "bc2")
    zsum, ssq = stats[:, 0:16], stats[:, 16:32]
    nmean, rstd = stats[:, 32:48], stats[:, 48:64]
    for i in range(NT):
        b.red(zsum[:, i:i + 1], stream[:, i, :], op="sum")
    vgaps(b, C)
    b.ts(nmean, zsum, -1.0 / 1024.0, None, ALU.mult)
    b.wait("scalar", "ve")
    for i in range(NT):
        b.act(ztmp, stream[:, i, :], AF.Identity, bias=nmean[:, i:i + 1])
        b.cp(stream[:, i, :], ztmp, eng="scalar")
        b.wait("vector", "se")
        b.stt(sqd, stream[:, i, :], 1.0, stream[:, i, :], ALU.mult, ALU.mult,
              accum=ssq[:, i:i + 1])
    vgaps(b, C)
    rstd_chain(b, C, ssq, rstd, 1024, 1e-5)
    b.wait("scalar", "ve")
    b.wait_dma("vector", "bc0")
    b.wait_dma("vector", "bc1")
    b.wait_dma("vector", "bc2")
    ssq2, rstd2 = stats[:, 0:16], stats[:, 16:32]
    for i in range(NT):
        b.wait("scalar", "ve")
        b.act(ztmp, stream[:, i, :], AF.Identity, scale=rstd[:, i:i + 1])
        b.wait("vector", "se")
        b.tt(ztmp, ztmp, g1bc, ALU.mult)
        b.tt(ztmp, ztmp, b1bc, ALU.add)
        b.cp(stream[:, i, :], ztmp)
        b.stt(sqd, stream[:, i, :], 1.0, stream[:, i, :], ALU.mult, ALU.mult,
              accum=ssq2[:, i:i + 1])
    vgaps(b, C)
    rstd_chain(b, C, ssq2, rstd2, 1024, 1e-6)
    b.wait("scalar", "ve")
    for i in range(NT):
        b.wait("scalar", "ve")
        b.wait("scalar", "pe")
        b.act(ztmp, stream[:, i, :], AF.Identity, scale=rstd2[:, i:i + 1])
        b.wait("vector", "se")
        b.tt(ztmp, ztmp, rmbc, ALU.mult)
        b.wait("tensor", "ve")
        items = [(ztmp[:, c * 128:(c + 1) * 128], hT[:, c, i * 128:(i + 1) * 128])
                 for c in range(8)]
        tp_batch(b, C, ps, items)
    b.barrier()
    if limit == "g0":
        for i in range(NT):
            b.cp(stream[:, i, :], A2_16[:, i, :])
        b.barrier()
        return

    gb = C["gb"]
    sp1 = WK[:, 0:64].rearrange("p (a f) -> p a f", a=8)
    for i in range(NT):
        b.wait("tensor", "ve")
        for c in range(8):
            b.mm(ps[i % 2][:, 0:8], hT[:, c, i * 128:(i + 1) * 128], C["wab"][:, c, :],
                 start=(c == 0), stop=(c == 7))
        b.wait("vector", "pe")
        b.wait("vector", "se")
        b.tt(sp1[:, 0, 0:4], ps[i % 2][:, 0:4], C["dtb"][:], ALU.add)
        b.wait("scalar", "ve")
        b.act(sp1[:, 1, 0:4], sp1[:, 0, 0:4], AF.Abs)
        b.act(sp1[:, 2, 0:4], sp1[:, 1, 0:4], AF.Exp, scale=-1.0)
        b.act(sp1[:, 3, 0:4], sp1[:, 2, 0:4], AF.Ln, bias=1.0)
        b.act(sp1[:, 4, 0:4], sp1[:, 0, 0:4], AF.Relu)
        b.act(gb[:, i, 4:8], ps[i % 2][:, 4:8], AF.Sigmoid)
        b.wait("vector", "se")
        b.tt(sp1[:, 5, 0:4], sp1[:, 3, 0:4], sp1[:, 4, 0:4], ALU.add)
        b.tt(gb[:, i, 0:4], sp1[:, 5, 0:4], C["coef"][:], ALU.mult)
    b.barrier()

    vpre = WK[:, 0:1048].rearrange("p (c f) -> p c f", c=8)
    cvo = WK[:, 1056:2080].rearrange("p (c f) -> p c f", c=8)
    cvs = WK[:, 2080:3104].rearrange("p (c f) -> p c f", c=8)
    slab = WK[:, 3104:4128]
    wv_ = WA[:, 0:8192].rearrange("p (c f) -> p c f", c=8)
    for (wname, cw, scr, do_conv) in (("l1_wv", "convv", scr_vhat, True),
                                      ("l1_wg", None, scr_sgate, False)):
        for half in range(2):
            b.barrier()
            b.dma("sync", wv_[:], dr[wname][:, half * 1024:(half + 1) * 1024]
                  .rearrange("(c p) f -> p c f", p=128), "wbig")
            b.wait_dma("tensor", "wbig")
            for i in range(NT):
                if i >= 1:
                    if do_conv:
                        b.wait_dma("vector", "sw", None)
                    else:
                        b.wait_dma("scalar", "sw", None)
                if do_conv:
                    b.wait("tensor", "ve")
                    for m in range(8):
                        pp = ps[m % 2]
                        if m >= 2:
                            b.wait("tensor", "ve")
                        if i == 0:
                            for c in range(8):
                                b.mm(pp[:, 0:128], wv_[:, c, m * 128:(m + 1) * 128],
                                     hT[:, c, 0:128], start=(c == 0), stop=(c == 7))
                        else:
                            for c in range(8):
                                b.mm(pp[:, 0:131], wv_[:, c, m * 128:(m + 1) * 128],
                                     hT[:, c, i * 128 - 3:(i + 1) * 128],
                                     start=(c == 0), stop=(c == 7))
                        b.wait("vector", "pe")
                        if i == 0:
                            b.memset(vpre[:, m, 0:3], 0.0)
                            b.cp(vpre[:, m, 3:131], pp[:, 0:128], drain=False)
                        else:
                            b.cp(vpre[:, m, :], pp[:, 0:131], drain=False)
                    cwt = C[cw]
                    hc0 = half * 8
                    for m in range(8):
                        b.ts(cvo[:, m, :], vpre[:, m, 0:128], cwt[:, hc0 + m, 0:1], None,
                             ALU.mult)
                        for j in range(1, 4):
                            b.stt(cvo[:, m, :], vpre[:, m, j:j + 128],
                                  cwt[:, hc0 + m, j:j + 1], cvo[:, m, :], ALU.mult, ALU.add)
                    b.wait("scalar", "ve")
                    b.wait("scalar", "pe")
                    b.act(cvs[:, :, :], cvo[:, :, :], AF.Silu)
                    b.wait("tensor", "se")
                    items = [(cvs[:, m, :], slab[:, m * 128:(m + 1) * 128]) for m in range(8)]
                    tp_batch(b, C, ps, items, banks=(2, 3, 4, 5))
                else:
                    b.wait("tensor", "se")
                    for nh in range(2):
                        pp = ps[nh]
                        for c in range(8):
                            b.mm(pp[:, 0:512], hT[:, c, i * 128:(i + 1) * 128],
                                 wv_[:, c, nh * 512:(nh + 1) * 512],
                                 start=(c == 0), stop=(c == 7))
                    b.wait("scalar", "pe")
                    b.act(slab[:, 0:512], ps[0][:, 0:512], AF.Silu)
                    b.act(slab[:, 512:1024], ps[1][:, 0:512], AF.Silu)
                ch = "ve" if do_conv else "se"
                b.wait("gpsimd", ch)
                b.dma("gpsimd", scr[i * 128:(i + 1) * 128, half * 1024:(half + 1) * 1024],
                      slab, "sw")
        b.barrier()
    if limit == "g2ab":
        return

    qpre = WK[:, 0:1048].rearrange("p (c f) -> p c f", c=8)
    qc = WK[:, 1056:2080].rearrange("p (c f) -> p c f", c=8)
    qs = WK[:, 2080:3104].rearrange("p (c f) -> p c f", c=8)
    qq = WK[:, 3104:4128].rearrange("p (c f) -> p c f", c=8)
    rstdT = WK[:, 4128:4256]
    bcr = WK[:, 4256:4384]
    for (wname, cw, scr, extra) in (("l1_wq", "convq", scr_qT, 0.0625),
                                    ("l1_wk", "convk", scr_kT, None)):
        b.barrier()
        b.dma("sync", wv_[:], dr[wname][:].rearrange("(c p) f -> p c f", p=128), "wbig")
        b.wait_dma("tensor", "wbig")
        for i in range(NT):
            b.wait("tensor", "ve")
            for m in range(8):
                pp = ps[m % 2]
                if m >= 2:
                    b.wait("tensor", "ve")
                if i == 0:
                    for c in range(8):
                        b.mm(pp[:, 0:128], wv_[:, c, m * 128:(m + 1) * 128],
                             hT[:, c, 0:128], start=(c == 0), stop=(c == 7))
                else:
                    for c in range(8):
                        b.mm(pp[:, 0:131], wv_[:, c, m * 128:(m + 1) * 128],
                             hT[:, c, i * 128 - 3:(i + 1) * 128],
                             start=(c == 0), stop=(c == 7))
                b.wait("vector", "pe")
                if i == 0:
                    b.memset(qpre[:, m, 0:3], 0.0)
                    b.cp(qpre[:, m, 3:131], pp[:, 0:128], drain=False)
                else:
                    b.cp(qpre[:, m, :], pp[:, 0:131], drain=False)
            cwt = C[cw]
            for m in range(8):
                b.ts(qc[:, m, :], qpre[:, m, 0:128], cwt[:, m, 0:1], None, ALU.mult)
                for j in range(1, 4):
                    b.stt(qc[:, m, :], qpre[:, m, j:j + 128], cwt[:, m, j:j + 1],
                          qc[:, m, :], ALU.mult, ALU.add)
            b.wait("scalar", "ve")
            b.wait("scalar", "pe")
            if i >= 1:
                b.wait_dma("scalar", f"qw{(i - 1) % 2}", None)
                b.wait_dma("vector", f"qw{(i - 1) % 2}", None)
            b.act(qs[:, :, :], qc[:, :, :], AF.Silu)
            b.act(qq[:, :, :], qs[:, :, :], AF.Square)
            b.wait("tensor", "se")
            for h in range(4):
                b.mm(ps[2][:, h:h + 1], qq[:, 2 * h, :], C["ones_col"][:],
                     start=True, stop=False)
                b.mm(ps[2][:, h:h + 1], qq[:, 2 * h + 1, :], C["ones_col"][:],
                     start=False, stop=True)
            b.wait("vector", "pe")
            b.cp(stats[:, 0:4], ps[2][:, 0:4])
            vgaps(b, C)
            rstd_chain(b, C, stats[:, 0:4], stats[:, 4:8], 1, 1e-6,
                       extra_scale=extra, is_sum=True)
            for h in range(4):
                bcast_col(b, C, ps, stats[:, 4 + h:5 + h], bcr, bank=3)
                b.tt(qs[:, 2 * h, :], qs[:, 2 * h, :], bcr, ALU.mult)
                b.tt(qs[:, 2 * h + 1, :], qs[:, 2 * h + 1, :], bcr, ALU.mult)
            b.wait("gpsimd", "ve")
            if i >= 2:
                b.wait_dma("gpsimd", f"qw{i % 2}", None)
            b.dma("gpsimd", scr[i].rearrange("p (c f) -> p c f", c=8),
                  qs[:, :, :], f"qw{i % 2}")
        b.barrier()
    if limit == "g2pre":
        return

    SS = A1[:, 0:4096].rearrange("p (c f) -> p c f", c=8)
    qTc = A1[:, 4096:5120].rearrange("p (c f) -> p c f", c=8)
    kTc = A1[:, 5120:6144].rearrange("p (c f) -> p c f", c=8)
    ktok = A1[:, 6144:7168].rearrange("p (c f) -> p c f", c=8)
    kdec = A1[:, 7168:8192].rearrange("p (c f) -> p c f", c=8)
    gb2 = A1[:, 8192:8200]
    sm = A1[:, 8200:8232]
    negG4 = sm[:, 0:4]
    eG4 = sm[:, 4:8]
    ebg4 = sm[:, 8:12]
    gcm = sm[:, 12:16]
    egd = sm[:, 16:20]
    eGC = sm[:, 20:24]
    obt = A1[:, 8232:8744]
    ob4 = A1[:, 8744:10792].rearrange("p (c f) -> p c f", c=4)
    vc = A1[:, 10792:12840]
    oscb = A1[:, 12840:14888]
    A2f = A2[:]
    Gd4 = A2f[:, 0:512].rearrange("p (c f) -> p c f", c=4)
    nGd4 = A2f[:, 512:1024].rearrange("p (c f) -> p c f", c=4)
    DqT4 = A2f[:, 1024:1536].rearrange("p (c f) -> p c f", c=4)
    Dk4 = A2f[:, 1536:2048].rearrange("p (c f) -> p c f", c=4)
    Aqk4 = A2f[:, 2048:2560].rearrange("p (c f) -> p c f", c=4)
    NT4 = A2f[:, 2560:3072].rearrange("p (c f) -> p c f", c=4)
    N4 = A2f[:, 3072:3584].rearrange("p (c f) -> p c f", c=4)
    X4 = A2f[:, 3584:4096].rearrange("p (c f) -> p c f", c=4)
    XT4 = A2f[:, 4096:4608].rearrange("p (c f) -> p c f", c=4)
    TT4 = A2f[:, 4608:5120].rearrange("p (c f) -> p c f", c=4)
    qg4 = A2f[:, 5120:6144].rearrange("p (c f) -> p c f", c=8)
    Ub4 = A2f[:, 6144:8192].rearrange("p (c f) -> p c f", c=4)
    bV4 = A2f[:, 8192:10240].rearrange("p (c f) -> p c f", c=4)
    kd24 = A2f[:, 10240:11264].rearrange("p (c f) -> p c f", c=8)
    Gd4f, nGd4f = A2f[:, 0:512], A2f[:, 512:1024]
    DqT4f, Dk4f = A2f[:, 1024:1536], A2f[:, 1536:2048]
    Aqk4f, NT4f, N4f = A2f[:, 2048:2560], A2f[:, 2560:3072], A2f[:, 3072:3584]
    X4f, XT4f, TT4f = A2f[:, 3584:4096], A2f[:, 4096:4608], A2f[:, 4608:5120]
    msk4 = {}
    msk4f = {}
    for nm, off in (("mask_ui", 11264), ("mask_us", 11776), ("mask_ls", 12288)):
        msk4[nm] = A2f[:, off:off + 512].rearrange("p (c f) -> p c f", c=4)
        msk4f[nm] = A2f[:, off:off + 512]

    b.memset(SS[:, :, :], 0.0)
    onb = WK[:, 0:512]
    sq2 = WK[:, 512:1024]
    b.dma("sync", onb, dr["bc_onorm"], "onw")
    for nm in ("mask_ui", "mask_us", "mask_ls"):
        for h in range(4):
            b.cp(msk4[nm][:, h, :], C[nm][:], drain=False)
    b.barrier()
    for c in range(NT):
        b.wait("sync", "pe")
        b.wait("sync", "ve")
        b.dma("sync", qTc[:, :, :], scr_qT[c].rearrange("p (a f) -> p a f", a=8), "qrd")
        b.dma("sync", kTc[:, :, :], scr_kT[c].rearrange("p (a f) -> p a f", a=8), "krd")
        b.dma("sync", vc, scr_vhat[c * 128:(c + 1) * 128, :], "vrd")
        b.dma("sync", oscb, scr_sgate[c * 128:(c + 1) * 128, :], "grd")
        b.wait("tensor", "ve")
        b.wait("tensor", "se")
        b.mm(ps[7][:, 0:4], C["ones_ui"][:], gb[:, c, 0:4], start=True, stop=True)
        b.wait("vector", "pe")
        b.cp(gb2[:, 0:4], ps[7][:, 0:4])
        b.cp(gb2[:, 4:8], gb[:, c, 4:8], drain=False)
        vgaps(b, C)
        b.ts(negG4, gb2[:, 0:4], -1.0, None, ALU.mult)
        b.wait("scalar", "ve")
        b.act(eG4, gb2[:, 0:4], AF.Exp)
        b.wait("vector", "se")
        b.tt(ebg4, eG4, gb2[:, 4:8], ALU.mult)
        b.wait_dma("tensor", "qrd")
        b.wait_dma("tensor", "krd")
        for h in range(4):
            bcast_col(b, C, ps, gb2[:, h:h + 1], Gd4[:, h, :], bank=7)
        b.wait("vector", "ve")
        for h in range(4):
            b.ts(Gd4[:, h, :], Gd4[:, h, :], negG4[:, h:h + 1], None, ALU.add, drain=False)
        for h in range(4):
            b.mm(ps[0][:, h * 128:(h + 1) * 128], kTc[:, 2 * h, :], qTc[:, 2 * h, :],
                 start=True, stop=False)
            b.mm(ps[0][:, h * 128:(h + 1) * 128], kTc[:, 2 * h + 1, :], qTc[:, 2 * h + 1, :],
                 start=False, stop=True)
        for h in range(4):
            b.mm(ps[1][:, h * 128:(h + 1) * 128], kTc[:, 2 * h, :], kTc[:, 2 * h, :],
                 start=True, stop=False)
            b.mm(ps[1][:, h * 128:(h + 1) * 128], kTc[:, 2 * h + 1, :], kTc[:, 2 * h + 1, :],
                 start=False, stop=True)
        b.wait("tensor", "ve")
        tp_batch(b, C, ps, [(kTc[:, a, :], ktok[:, a, :]) for a in range(8)],
                 banks=(2, 3, 4, 5))
        b.ts(nGd4f, Gd4f, -1.0, None, ALU.mult)
        b.tt(DqT4f, Gd4f, msk4f["mask_ui"], ALU.add)
        b.tt(Dk4f, Gd4f, msk4f["mask_us"], ALU.add)
        b.wait("scalar", "ve")
        b.act(DqT4f, DqT4f, AF.Exp)
        b.act(Dk4f, Dk4f, AF.Exp)
        b.wait("vector", "se")
        b.wait("vector", "pe")
        b.tt(Aqk4f, ps[0][:, 0:512], DqT4f, ALU.mult)
        b.tt(NT4f, ps[1][:, 0:512], Dk4f, ALU.mult)
        for h in range(4):
            bcast_col(b, C, ps, gb2[:, 4 + h:5 + h], Dk4[:, h, :], bank=7)
        b.wait("vector", "ve")
        b.ts(Dk4f, Dk4f, -1.0, None, ALU.mult, drain=False)
        b.tt(NT4f, NT4f, Dk4f, ALU.mult)
        b.tt(N4f, nGd4f, msk4f["mask_ls"], ALU.add)
        b.wait("scalar", "ve")
        b.act(N4f, N4f, AF.Exp)
        b.wait("vector", "se")
        b.tt(N4f, N4f, ps[1][:, 0:512], ALU.mult)
        for h in range(4):
            b.ts(N4[:, h, :], N4[:, h, :], gb2[:, 4 + h:5 + h], -1.0, ALU.mult, ALU.mult,
                 drain=(h == 0))
        for h in range(4):
            b.tt(TT4[:, h, :], NT4[:, h, :], ident, ALU.add, drain=False)
        b.cp(X4f, N4f)
        b.cp(XT4f, NT4f)
        for lev in range(6):
            b.wait("tensor", "ve")
            for h in range(4):
                b.mm(ps[2][:, h * 128:(h + 1) * 128], XT4[:, h, :], X4[:, h, :],
                     start=True, stop=True)
                b.mm(ps[3][:, h * 128:(h + 1) * 128], X4[:, h, :], XT4[:, h, :],
                     start=True, stop=True)
            b.wait("vector", "pe")
            b.cp(X4f, ps[2][:, 0:512])
            b.cp(XT4f, ps[3][:, 0:512])
            b.wait("tensor", "ve")
            for h in range(4):
                b.mm(ps[4][:, h * 128:(h + 1) * 128], X4[:, h, :], TT4[:, h, :],
                     start=True, stop=True)
            b.wait("vector", "pe")
            b.tt(TT4f, TT4f, ps[4][:, 0:512], ALU.add)
        b.wait_dma("vector", "vrd")
        for h in range(4):
            b.ts(bV4[:, h, :], vc[:, h * 512:(h + 1) * 512], gb2[:, 4 + h:5 + h],
                 None, ALU.mult, drain=(h == 0))
        b.wait("tensor", "ve")
        for h in range(4):
            b.mm(ps[4 + h % 2][:, 0:512], TT4[:, h, :], bV4[:, h, :],
                 start=True, stop=True)
            if h % 2 == 1:
                b.wait("vector", "pe")
                b.cp(Ub4[:, h - 1, :], ps[4][:, 0:512])
                b.cp(Ub4[:, h, :], ps[5][:, 0:512])
                if h == 1:
                    b.wait("tensor", "ve")
        for h in range(4):
            b.ts(kdec[:, 2 * h, :], ktok[:, 2 * h, :], ebg4[:, h:h + 1], None, ALU.mult,
                 drain=(h == 0))
            b.ts(kdec[:, 2 * h + 1, :], ktok[:, 2 * h + 1, :], ebg4[:, h:h + 1],
                 None, ALU.mult, drain=False)
        b.wait("tensor", "ve")
        for h in range(4):
            b.mm(ps[6][:, h * 128:(h + 1) * 128], kdec[:, 2 * h, :], TT4[:, h, :],
                 start=True, stop=True)
            b.mm(ps[7][:, h * 128:(h + 1) * 128], kdec[:, 2 * h + 1, :], TT4[:, h, :],
                 start=True, stop=True)
        b.wait("vector", "pe")
        b.cp(qg4[:, 0:4, :], ps[6][:, 0:512])
        b.cp(qg4[:, 4:8, :], ps[7][:, 0:512])
        if c > 0:
            b.wait("tensor", "ve")
            for h in range(4):
                b.mm(ps[4 + h % 2][:, 0:512], qg4[:, h, :], SS[:, 2 * h, :],
                     start=True, stop=False)
                b.mm(ps[4 + h % 2][:, 0:512], qg4[:, 4 + h, :], SS[:, 2 * h + 1, :],
                     start=False, stop=True)
                if h % 2 == 1:
                    b.wait("vector", "pe")
                    b.tt(Ub4[:, h - 1, :], Ub4[:, h - 1, :], ps[4][:, 0:512], ALU.subtract)
                    b.tt(Ub4[:, h, :], Ub4[:, h, :], ps[5][:, 0:512], ALU.subtract)
                    if h == 1:
                        b.wait("tensor", "ve")
        b.wait("tensor", "ve")
        for h in range(4):
            b.mm(ps[4 + h % 2][:, 0:512], Aqk4[:, h, :], Ub4[:, h, :],
                 start=True, stop=True)
            if c > 0:
                b.mm(ps[2 + h % 2][:, 0:512], qTc[:, 2 * h, :], SS[:, 2 * h, :],
                     start=True, stop=False)
                b.mm(ps[2 + h % 2][:, 0:512], qTc[:, 2 * h + 1, :], SS[:, 2 * h + 1, :],
                     start=False, stop=True)
            if h % 2 == 1:
                b.wait("vector", "pe")
                for hh in (h - 1, h):
                    if c > 0:
                        b.cp(obt, ps[4 + hh % 2][:, 0:512], drain=True)
                        b.stt(ob4[:, hh, :], ps[2 + hh % 2][:, 0:512], eG4[:, hh:hh + 1],
                              obt, ALU.mult, ALU.add)
                    else:
                        b.cp(ob4[:, hh, :], ps[4 + hh % 2][:, 0:512], drain=False)
                if h == 1:
                    b.wait("tensor", "ve")
        b.wait("vector", "pe")
        b.wait("vector", "se")
        for h in range(4):
            b.cp(gcm[:, h:h + 1], Gd4[:, h, 127:128], drain=(h == 0))
        b.wait("scalar", "ve")
        b.act(egd, gcm, AF.Exp)
        b.wait("vector", "se")
        b.tt(eGC, gb2[:, 0:4], gcm, ALU.add)
        vgaps(b, C)
        b.wait("scalar", "ve")
        b.act(eGC, eGC, AF.Exp)
        b.wait("vector", "se")
        for h in range(4):
            b.ts(kd24[:, 2 * h, :], ktok[:, 2 * h, :], egd[:, h:h + 1], None, ALU.mult,
                 drain=False)
            b.ts(kd24[:, 2 * h + 1, :], ktok[:, 2 * h + 1, :], egd[:, h:h + 1],
                 None, ALU.mult, drain=False)
        b.wait("tensor", "ve")
        for h in range(4):
            b.mm(ps[4 + h % 2][:, 0:512], kd24[:, 2 * h, :], Ub4[:, h, :],
                 start=True, stop=True)
            b.mm(ps[2 + h % 2][:, 0:512], kd24[:, 2 * h + 1, :], Ub4[:, h, :],
                 start=True, stop=True)
            if h % 2 == 1:
                b.wait("vector", "pe")
                for hh in (h - 1, h):
                    if c == 0:
                        b.cp(SS[:, 2 * hh, :], ps[4 + hh % 2][:, 0:512], drain=False)
                        b.cp(SS[:, 2 * hh + 1, :], ps[2 + hh % 2][:, 0:512], drain=False)
                    else:
                        b.stt(SS[:, 2 * hh, :], SS[:, 2 * hh, :], eGC[:, hh:hh + 1],
                              ps[4 + hh % 2][:, 0:512], ALU.mult, ALU.add)
                        b.stt(SS[:, 2 * hh + 1, :], SS[:, 2 * hh + 1, :], eGC[:, hh:hh + 1],
                              ps[2 + hh % 2][:, 0:512], ALU.mult, ALU.add)
                if h == 1:
                    b.wait("tensor", "ve")
        b.wait_dma("vector", "grd")
        for h in range(4):
            b.stt(sq2, ob4[:, h, :], 1.0, ob4[:, h, :], ALU.mult, ALU.mult,
                  accum=stats[:, h:h + 1])
        vgaps(b, C)
        rstd_chain(b, C, stats[:, 0:4], stats[:, 4:8], 512, 1e-6)
        b.wait("scalar", "ve")
        b.wait_dma("vector", "onw")
        for h in range(4):
            b.wait("scalar", "ve")
            b.act(sq2, ob4[:, h, :], AF.Identity, scale=stats[:, 4 + h:5 + h])
            b.wait("vector", "se")
            b.tt(sq2, sq2, onb, ALU.mult)
            b.tt(oscb[:, h * 512:(h + 1) * 512], oscb[:, h * 512:(h + 1) * 512],
                 sq2, ALU.mult)
        b.wait("gpsimd", "ve")
        if c >= 1:
            b.wait_dma("gpsimd", "ow", None)
        b.dma("gpsimd", scr_osc[c * 128:(c + 1) * 128, :], oscb, "ow")
        b.wait_dma("sync", "ow")
    b.barrier()
    if limit == "g2c":
        return

    oT = WK[:, 0:2048].rearrange("p (c f) -> p c f", c=16)
    xtk = WK[:, 2048:4096].rearrange("p (c f) -> p c f", c=2)
    osb = WK[:, 4096:6144]
    xa16 = A2_16
    for half in range(2):
        b.barrier()
        b.dma("sync", wv_[:], dr["l1_wo"][half * 1024:(half + 1) * 1024, :]
              .rearrange("(c p) f -> p c f", p=128), "wbig")
        b.wait_dma("tensor", "wbig")
        for i in range(NT):
            if i >= 1:
                b.wait("sync", "pe")
                b.wait("sync", "ve")
            b.dma("sync", osb, scr_osc[i * 128:(i + 1) * 128, :], "osr")
            b.dma("sync", xtk[:, i % 2, :], scr_x2[i * 128:(i + 1) * 128, :], f"x2r{i % 2}")
            b.wait_dma("tensor", "osr")
            b.wait("tensor", "ve")
            items = [(osb[:, m * 128:(m + 1) * 128], oT[:, m, :]) for m in range(16)]
            tp_batch(b, C, ps, items, banks=(2, 3, 4, 5))
            b.wait("tensor", "ve")
            for mh in range(2):
                pp = ps[mh]
                for cch in range(8):
                    b.mm(pp[:, 0:512], oT[:, half * 8 + cch, :],
                         wv_[:, cch, mh * 512:(mh + 1) * 512],
                         start=(cch == 0), stop=(cch == 7))
            b.wait("vector", "pe")
            b.wait_dma("vector", f"x2r{i % 2}")
            if half == 0:
                b.tt(xa16[:, i, 0:512], ps[0][:, 0:512], xtk[:, i % 2, 0:512], ALU.add)
                b.tt(xa16[:, i, 512:1024], ps[1][:, 0:512], xtk[:, i % 2, 512:1024], ALU.add)
            else:
                b.tt(xa16[:, i, 0:512], ps[0][:, 0:512], xa16[:, i, 0:512], ALU.add)
                b.tt(xa16[:, i, 512:1024], ps[1][:, 0:512], xa16[:, i, 512:1024], ALU.add)
    b.barrier()
    g2bc = WK[:, 0:1024]
    b2bc2 = WK[:, 1024:2048]
    bb2 = WK[:, 2048:3072]
    h2t = WK[:, 3072:4096]
    sqd3 = WK[:, 4096:5120]
    h2T = WK[:, 5120:6144].rearrange("p (c f) -> p c f", c=8)
    xtk2 = WK[:, 6144:8192].rearrange("p (c f) -> p c f", c=2)
    b.dma("sync", g2bc, dr["bc_ln2g"], "bc0")
    b.dma("sync", b2bc2, dr["bc_ln2b"], "bc1")
    b.dma("sync", bb2, dr["bc_b2"], "bc2")
    b.dma("sync", WA[:, 0:8192].rearrange("p (c f) -> p c f", c=8)[:],
          dr["l1_w2"][:].rearrange("(c p) f -> p c f", p=128), "wbig")
    w2_ = WA[:, 0:8192].rearrange("p (c f) -> p c f", c=8)
    for i in range(NT):
        b.red(zsum[:, i:i + 1], xa16[:, i, :], op="sum")
    vgaps(b, C)
    b.ts(nmean, zsum, -1.0 / 1024.0, None, ALU.mult)
    b.wait("scalar", "ve")
    for i in range(NT):
        b.wait("scalar", "ve")
        b.act(h2t, xa16[:, i, :], AF.Identity, bias=nmean[:, i:i + 1])
        b.wait("vector", "se")
        b.stt(sqd3, h2t, 1.0, h2t, ALU.mult, ALU.mult, accum=ssq[:, i:i + 1])
        b.cp(stream[:, i, :], h2t)
    vgaps(b, C)
    rstd_chain(b, C, ssq, rstd, 1024, 1e-5)
    b.wait("scalar", "ve")
    b.wait_dma("vector", "bc0")
    b.wait_dma("vector", "bc1")
    b.wait_dma("tensor", "wbig")
    for i in range(NT):
        b.wait("scalar", "ve")
        b.wait("scalar", "pe")
        b.act(h2t, stream[:, i, :], AF.Identity, scale=rstd[:, i:i + 1])
        b.wait("vector", "se")
        b.tt(h2t, h2t, g2bc, ALU.mult)
        b.tt(h2t, h2t, b2bc2, ALU.add)
        b.wait("tensor", "ve")
        items = [(h2t[:, cch * 128:(cch + 1) * 128], h2T[:, cch, :]) for cch in range(8)]
        tp_batch(b, C, ps, items, banks=(2, 3, 4, 5))
        b.wait("tensor", "ve")
        for mh in range(2):
            pp = ps[mh]
            for cch in range(8):
                b.mm(pp[:, 0:512], h2T[:, cch, :], w2_[:, cch, mh * 512:(mh + 1) * 512],
                     start=(cch == 0), stop=(cch == 7))
        if i >= 2:
            b.wait("sync", "ve")
        b.dma("sync", xtk2[:, i % 2, :], scr_x2[i * 128:(i + 1) * 128, :], f"x2s{i % 2}")
        b.wait("scalar", "pe")
        b.wait("vector", "pe")
        b.wait_dma("vector", "bc2")
        b.tt(h2t[:, 0:512], ps[0][:, 0:512], bb2[:, 0:512], ALU.add)
        b.tt(h2t[:, 512:1024], ps[1][:, 0:512], bb2[:, 512:1024], ALU.add)
        b.wait("scalar", "ve")
        b.act(h2t, h2t, AF.Gelu)
        b.wait("vector", "se")
        b.wait_dma("vector", f"x2s{i % 2}")
        b.tt(h2t, h2t, xa16[:, i, :], ALU.add)
        b.tt(stream[:, i, :], h2t, xtk2[:, i % 2, :], ALU.add)
        b.red(C["stats"][:, i:i + 1], stream[:, i, :], op="sum")
    postln_finish(b, C, stream, WK[:, 0:1024], WK[:, 1024:2048], eps=1e-5)


SPARSE_MOE = True


def emit_s4(nc, b, C, dr, ps, A1, A2, WA, WK, stream, A2_16, scr_x3):
    if SPARSE_MOE:
        return emit_s4_sparse(nc, b, C, dr, ps, A1, A2, WA, WK, stream, A2_16, scr_x3)
    return emit_s4_dense(nc, b, C, dr, ps, A1, A2, WA, WK, stream, A2_16, scr_x3)


def emit_s4_sparse(nc, b, C, dr, ps, A1, A2, WA, WK, stream, A2_16, scr_x3):
    ident = C["ident"][:]
    stats = C["stats"]
    ytok = A2_16
    comb = stream

    mrbc = WK[:, 0:1024]
    ytmp = WK[:, 1024:2048]
    sqd = WK[:, 2048:3072]
    lgb = WK[:, 3072:3080]
    lg2 = WK[:, 3080:3088]
    t1 = WK[:, 3088:3096]
    msk1 = WK[:, 3096:3104]
    msk2 = WK[:, 3104:3112]
    m8 = WK[:, 3112:3120]
    asb = WK[:, 3120:3128]
    offbc = WK[:, 3128:3136]
    ytT = WK[:, 4096:5120].rearrange("p (c f) -> p c f", c=8)
    b.dma("sync", mrbc, dr["bc_moerms"], "bc0")
    ssqm, rstdm = stats[:, 0:16], stats[:, 16:32]
    m1c, m2c, d12 = stats[:, 32:33], stats[:, 33:34], stats[:, 34:35]
    w1c, w2c = stats[:, 35:36], stats[:, 36:37]
    for i in range(NT):
        b.stt(sqd, stream[:, i, :], 1.0, stream[:, i, :], ALU.mult, ALU.mult,
              accum=ssqm[:, i:i + 1])
    vgaps(b, C)
    rstd_chain(b, C, ssqm, rstdm, 1024, 1e-6)
    b.wait("scalar", "ve")
    b.wait_dma("vector", "bc0")
    b.memset(C["offrow"][:], 0.0)
    for i in range(NT):
        b.wait("scalar", "ve")
        b.wait("scalar", "pe")
        b.act(ytmp, stream[:, i, :], AF.Identity, scale=rstdm[:, i:i + 1])
        b.wait("vector", "se")
        b.tt(ytmp, ytmp, mrbc, ALU.mult)
        b.cp(ytok[:, i, :], ytmp)
        b.wait("tensor", "ve")
        items = [(ytmp[:, c * 128:(c + 1) * 128], ytT[:, c, :]) for c in range(8)]
        tp_batch(b, C, ps, items, banks=(0, 1, 2, 3))
        b.wait("tensor", "ve")
        for c in range(8):
            b.mm(ps[6][:, 0:8], ytT[:, c, :], C["gwT"][:, c, :],
                 start=(c == 0), stop=(c == 7))
        b.wait("vector", "pe")
        b.cp(lgb, ps[6][:, 0:8])
        b.red(m1c, lgb, op="max")
        b.tt(msk1, lgb, m1c.broadcast_to([128, 8]), ALU.is_equal)
        b.stt(lg2, msk1, -1e30, lgb, ALU.mult, ALU.add)
        b.red(m2c, lg2, op="max")
        b.tt(msk2, lg2, m2c.broadcast_to([128, 8]), ALU.is_equal)
        b.tt(d12, m1c, m2c, ALU.subtract)
        b.wait("scalar", "ve")
        b.act(w1c, d12, AF.Sigmoid)
        b.act(w2c, d12, AF.Sigmoid, scale=-1.0)
        b.wait("vector", "se")
        b.ts(t1, msk2, w2c, None, ALU.mult)
        b.stt(C["wfull"][:, i, :], msk1, w1c, t1, ALU.mult, ALU.add)
        b.tt(m8, msk1, msk2, ALU.max)
        b.wait("tensor", "ve")
        b.mm(ps[6][:, 8:16], C["ones_ui"][:], m8, start=True, stop=True)
        b.mm(ps[7][:, 0:8], C["ones_r"][:], C["offrow"][:], start=True, stop=True)
        b.wait("vector", "pe")
        b.cp(offbc, ps[7][:, 0:8])
        b.tt(asb, ps[6][:, 8:16], offbc, ALU.add)
        b.tt(t1, asb, m8, ALU.mult)
        b.ts(C["rank"][:, i, :], t1, 1.0, -1.0, ALU.mult, ALU.add)
        b.wait("tensor", "ve")
        b.mm(ps[7][0:1, 8:16], C["e127"][:], asb, start=True, stop=True)
        b.wait("vector", "pe")
        b.cp(C["offrow"][:], ps[7][0:1, 8:16])
    b.barrier()

    for i in range(NT):
        b.dma("gpsimd", scr_x3[i * 128:(i + 1) * 128, :], stream[:, i, :], "x3w")
    b.wait_dma("vector", "x3w")
    for i in range(NT):
        b.memset(comb[:, i, :], 0.0)
    b.barrier()

    NBW = [384] * 7 + [128]
    w1s = WA[:, 0:3072].rearrange("p (c f) -> p c f", c=8)
    w2s = WA[:, 3072:6144].rearrange("p (c f) -> p c f", c=8)
    w3s = WA[:, 6144:9216].rearrange("p (c f) -> p c f", c=3)
    yeT = WK[:, 0:3072].rearrange("p (c f) -> p c f", c=8)
    oute = WK[:, 3072:6144].rearrange("p (c f) -> p c f", c=3)
    ygt = WK[:, 3072:4096]
    Pslab = WK[:, 6144:7168].rearrange("p (c f) -> p c f", c=8)
    tmps = WK[:, 7168:7552]
    hb = WK[:, 7552:7936]
    hTe = WK[:, 7936:8320].rearrange("p (c f) -> p c f", c=3)
    rk8 = WK[:, 8320:8328]
    rkbc = WK[:, 8328:8456]
    wbc = WK[:, 8456:8584]
    atile = Pslab

    for e in range(8):
        for half in range(2):
            for rcl in range(3):
                rc = 3 * half + rcl
                for g in range(2):
                    b.wait("vector", "pe")
                    for j in range(8):
                        tc = g * 8 + j
                        b.ts(rk8[:, j:j + 1], C["rank"][:, tc, e:e + 1], 1.0,
                             -128.0 * rc, ALU.mult, ALU.add)
                    vgaps(b, C)
                    for j in range(8):
                        b.ts(Pslab[:, j, :], C["iota_f"][:], rk8[:, j:j + 1], None,
                             ALU.is_equal)
                    b.wait("tensor", "ve")
                    for j in range(8):
                        tc = g * 8 + j
                        for mh in range(2):
                            b.mm(ps[mh][:, 0:512], Pslab[:, j, :],
                                 ytok[:, tc, mh * 512:(mh + 1) * 512],
                                 start=(g == 0 and j == 0), stop=(g == 1 and j == 7))
                b.wait("vector", "pe")
                b.cp(ygt[:, 0:512], ps[0][:, 0:512])
                b.cp(ygt[:, 512:1024], ps[1][:, 0:512])
                b.wait("tensor", "ve")
                items = [(ygt[:, c * 128:(c + 1) * 128], yeT[:, c, rcl * 128:(rcl + 1) * 128])
                         for c in range(8)]
                tp_batch(b, C, ps, items, banks=(2, 3, 4, 5))
            b.barrier()
            wslot = [WA[:, 0:3072].rearrange("p (c f) -> p c f", c=8),
                     WA[:, 3072:6144].rearrange("p (c f) -> p c f", c=8)]
            wslot3 = [WA[:, 0:3072].rearrange("p (c f) -> p c f", c=3),
                      WA[:, 3072:6144].rearrange("p (c f) -> p c f", c=3)]
            upb = WA[:, 6144:7296].rearrange("p (c f) -> p c f", c=3)
            hb3 = WA[:, 7296:8448].rearrange("p (c f) -> p c f", c=3)
            loads = []
            for nb in range(8):
                loads += [(1, nb), (2, nb), (3, nb)]
            pe_done = {}

            def issue_load(li):
                kind, nb = loads[li]
                slot = li % 2
                n0 = sum(NBW[:nb])
                NW = NBW[nb]
                nc3 = NW // 128
                q = "sync" if li % 2 == 0 else "gpsimd"
                if li >= 2:
                    b.wait(q, "pe", pe_done.get(li - 2, 0))
                src = dr[("moe_w1i", "moe_w2i", "moe_w3i")[kind - 1]]
                if kind < 3:
                    b.dma(q, wslot[slot][:, :, 0:NW],
                          src[e, nb, :, 0:8 * NW].rearrange("p (c f) -> p c f", c=8),
                          f"wl{slot}")
                else:
                    b.dma(q, wslot3[slot][:, 0:nc3, :],
                          src[e, nb, :, 0:nc3 * 1024].rearrange("p (c f) -> p c f", c=nc3),
                          f"wl{slot}")

            issue_load(0)
            issue_load(1)
            for nb in range(8):
                n0 = sum(NBW[:nb])
                NW = NBW[nb]
                nc3 = NW // 128
                li = nb * 3
                s1_, s2_, s3_ = li % 2, (li + 1) % 2, (li + 2) % 2
                b.wait_dma("tensor", f"wl{s1_}", None)
                for r in range(3):
                    pp = ps[(0, 1, 5)[r]]
                    if nb > 0:
                        b.wait("tensor", "ve")
                    for c in range(8):
                        b.mm(pp[:, 0:NW], yeT[:, c, r * 128:(r + 1) * 128],
                             wslot[s1_][:, c, 0:NW], start=(c == 0), stop=(c == 7))
                    b.wait("vector", "pe")
                    b.cp(upb[:, r, 0:NW], pp[:, 0:NW], drain=False)
                pe_done[li] = b.n["pe"]
                if li + 2 < len(loads):
                    issue_load(li + 2)
                b.wait_dma("tensor", f"wl{s2_}", None)
                for r in range(3):
                    pp = ps[(0, 1, 5)[r]]
                    b.wait("tensor", "ve")
                    for c in range(8):
                        b.mm(pp[:, 0:NW], yeT[:, c, r * 128:(r + 1) * 128],
                             wslot[s2_][:, c, 0:NW], start=(c == 0), stop=(c == 7))
                    b.wait("scalar", "pe")
                    b.wait("scalar", "ve")
                    b.act(tmps[:, 0:NW], upb[:, r, 0:NW], AF.Silu)
                    b.wait("vector", "se")
                    b.tt(hb3[:, r, 0:NW], tmps[:, 0:NW], pp[:, 0:NW], ALU.mult)
                pe_done[li + 1] = b.n["pe"]
                if li + 3 < len(loads):
                    issue_load(li + 3)
                b.wait_dma("tensor", f"wl{s3_}", None)
                b.wait("tensor", "ve")
                for r in range(3):
                    items = [(hb3[:, r, c * 128:(c + 1) * 128],
                              hTe[:, c, :]) for c in range(nc3)]
                    tp_batch(b, C, ps, items, banks=(2, 6))
                    b.wait("tensor", "ve")
                    for mh in range(2):
                        for c in range(nc3):
                            b.mm(ps[3 + mh][:, 0:512], hTe[:, c, :],
                                 wslot3[s3_][:, c, mh * 512:(mh + 1) * 512],
                                 start=(c == 0), stop=(c == nc3 - 1))
                    b.wait("vector", "pe")
                    if nb == 0:
                        b.cp(oute[:, r, 0:512], ps[3][:, 0:512], drain=False)
                        b.cp(oute[:, r, 512:1024], ps[4][:, 0:512], drain=False)
                    else:
                        b.tt(oute[:, r, 0:512], oute[:, r, 0:512], ps[3][:, 0:512], ALU.add)
                        b.tt(oute[:, r, 512:1024], oute[:, r, 512:1024], ps[4][:, 0:512],
                             ALU.add)
                pe_done[li + 2] = b.n["pe"]
            b.barrier()
            rkbc2 = (rkbc, wbc)
            bcast_col(b, C, ps, C["rank"][:, 0, e:e + 1], rkbc2[0], bank=7)
            for tc in range(NT):
                cur = rkbc2[tc % 2]
                b.wait("vector", "pe")
                for rcl in range(3):
                    rc = 3 * half + rcl
                    b.ts(atile[:, 0, :], cur, 1.0, -128.0 * rc, ALU.mult, ALU.add)
                    b.tt(Pslab[:, 4 + rcl, :], C["iota_p"][:], atile[:, 0, :], ALU.is_equal)
                if tc + 1 < NT:
                    bcast_col(b, C, ps, C["rank"][:, tc + 1, e:e + 1], rkbc2[(tc + 1) % 2],
                              bank=7)
                b.wait("tensor", "ve")
                for mh in range(2):
                    for rcl in range(3):
                        b.mm(ps[mh][:, 0:512], Pslab[:, 4 + rcl, :],
                             oute[:, rcl, mh * 512:(mh + 1) * 512],
                             start=(rcl == 0), stop=(rcl == 2))
                b.wait("vector", "pe")
                b.stt(comb[:, tc, 0:512], ps[0][:, 0:512], C["wfull"][:, tc, e:e + 1],
                      comb[:, tc, 0:512], ALU.mult, ALU.add)
                b.stt(comb[:, tc, 512:1024], ps[1][:, 0:512], C["wfull"][:, tc, e:e + 1],
                      comb[:, tc, 512:1024], ALU.mult, ALU.add)
            b.barrier()

    xrd = WK[:, 0:2048].rearrange("p (c f) -> p c f", c=2)
    for i in range(NT):
        if i >= 2:
            b.wait("sync", "ve")
        b.dma("sync", xrd[:, i % 2, :], scr_x3[i * 128:(i + 1) * 128, :], f"x3r{i % 2}")
        b.wait_dma("vector", f"x3r{i % 2}")
        b.tt(stream[:, i, :], stream[:, i, :], xrd[:, i % 2, :], ALU.add)
        b.red(C["stats"][:, i:i + 1], stream[:, i, :], op="sum")
    postln_finish(b, C, stream, WK[:, 2048:3072], WK[:, 3072:4096], eps=1e-5)


def emit_s4_dense(nc, b, C, dr, ps, A1, A2, WA, WK, stream, A2_16, scr_x3):
    ident = C["ident"][:]
    stats = C["stats"]
    yT = A2[:].rearrange("p (c t) -> p c t", c=8)

    mrbc = WK[:, 0:1024]
    ytmp = WK[:, 1024:2048]
    sqd = WK[:, 2048:3072]
    lgb = WK[:, 3072:3080]
    lg2 = WK[:, 3080:3088]
    t1 = WK[:, 3088:3096]
    msk1 = WK[:, 3096:3104]
    msk2 = WK[:, 3104:3112]
    b.dma("sync", mrbc, dr["bc_moerms"], "bc0")
    ssqm, rstdm = stats[:, 0:16], stats[:, 16:32]
    m1c, m2c, d12 = stats[:, 32:33], stats[:, 33:34], stats[:, 34:35]
    w1c, w2c = stats[:, 35:36], stats[:, 36:37]
    for i in range(NT):
        b.stt(sqd, stream[:, i, :], 1.0, stream[:, i, :], ALU.mult, ALU.mult,
              accum=ssqm[:, i:i + 1])
    vgaps(b, C)
    rstd_chain(b, C, ssqm, rstdm, 1024, 1e-6)
    b.wait("scalar", "ve")
    b.wait_dma("vector", "bc0")
    for i in range(NT):
        b.wait("scalar", "ve")
        b.wait("scalar", "pe")
        b.act(ytmp, stream[:, i, :], AF.Identity, scale=rstdm[:, i:i + 1])
        b.wait("vector", "se")
        b.tt(ytmp, ytmp, mrbc, ALU.mult)
        b.wait("tensor", "ve")
        items = [(ytmp[:, c * 128:(c + 1) * 128], yT[:, c, i * 128:(i + 1) * 128])
                 for c in range(8)]
        tp_batch(b, C, ps, items, banks=(0, 1, 2, 3))
        b.wait("tensor", "ve")
        for c in range(8):
            b.mm(ps[6][:, 0:8], yT[:, c, i * 128:(i + 1) * 128], C["gwT"][:, c, :],
                 start=(c == 0), stop=(c == 7))
        b.wait("vector", "pe")
        b.cp(lgb, ps[6][:, 0:8])
        b.red(m1c, lgb, op="max")
        b.tt(msk1, lgb, m1c.broadcast_to([128, 8]), ALU.is_equal)
        b.stt(lg2, msk1, -1e30, lgb, ALU.mult, ALU.add)
        b.red(m2c, lg2, op="max")
        b.tt(msk2, lg2, m2c.broadcast_to([128, 8]), ALU.is_equal)
        b.tt(d12, m1c, m2c, ALU.subtract)
        b.wait("scalar", "ve")
        b.act(w1c, d12, AF.Sigmoid)
        b.act(w2c, d12, AF.Sigmoid, scale=-1.0)
        b.wait("vector", "se")
        b.ts(t1, msk2, w2c, None, ALU.mult)
        b.stt(C["wfull"][:, i, :], msk1, w1c, t1, ALU.mult, ALU.add)
    b.barrier()

    for i in range(NT):
        b.dma("gpsimd", scr_x3[i * 128:(i + 1) * 128, :], stream[:, i, :], "x3w")
    b.wait_dma("vector", "x3w")
    comb = stream
    for i in range(NT):
        b.memset(comb[:, i, :], 0.0)
    b.barrier()

    NBW = [384] * 7 + [128]
    w1s = WA[:, 0:3072].rearrange("p (c f) -> p c f", c=8)
    w2s = WA[:, 3072:6144].rearrange("p (c f) -> p c f", c=8)
    w3s = WA[:, 6144:9216].rearrange("p (c f) -> p c f", c=3)
    tmps = WK[:, 0:384]
    hb = WK[:, 384:768]
    hTe = WK[:, 768:1152].rearrange("p (c f) -> p c f", c=3)
    for e in range(8):
        for nb in range(8):
            n0 = sum(NBW[:nb])
            NW = NBW[nb]
            nc3 = NW // 128
            b.wait("sync", "pe")
            b.dma("sync", w1s[:, :, 0:NW],
                  dr["moe_w1T"][e, :, n0:n0 + NW].rearrange("(c p) f -> p c f", p=128), "we1")
            b.dma("sync", w2s[:, :, 0:NW],
                  dr["moe_w2T"][e, :, n0:n0 + NW].rearrange("(c p) f -> p c f", p=128), "we2")
            b.dma("sync", w3s[:, 0:nc3, :],
                  dr["moe_w3T"][e, n0:n0 + NW, :].rearrange("(c p) f -> p c f", p=128), "we3")
            b.wait_dma("tensor", "we1")
            b.wait_dma("tensor", "we2")
            b.wait_dma("tensor", "we3")
            pe_mm, se_silu, ve_tt, ve_add = {}, {}, {}, {}

            def stage_a(i):
                gb_, ub_ = (0, 1) if i % 2 == 0 else (5, 6)
                if i >= 2:
                    b.wait("tensor", "se", se_silu[i - 2])
                    b.wait("tensor", "ve", ve_tt[i - 2])
                for c in range(8):
                    b.mm(ps[ub_][:, 0:NW], yT[:, c, i * 128:(i + 1) * 128],
                         w1s[:, c, 0:NW], start=(c == 0), stop=(c == 7))
                for c in range(8):
                    b.mm(ps[gb_][:, 0:NW], yT[:, c, i * 128:(i + 1) * 128],
                         w2s[:, c, 0:NW], start=(c == 0), stop=(c == 7))
                pe_mm[i] = b.n["pe"]

            def stage_b(i):
                gb_, ub_ = (0, 1) if i % 2 == 0 else (5, 6)
                b.wait("scalar", "pe", pe_mm[i])
                b.wait("scalar", "ve")
                b.act(tmps[:, 0:NW], ps[ub_][:, 0:NW], AF.Silu)
                se_silu[i] = b.n["se"]
                b.wait("vector", "se", se_silu[i])
                b.wait("vector", "pe")
                b.tt(hb[:, 0:NW], tmps[:, 0:NW], ps[gb_][:, 0:NW], ALU.mult)
                ve_tt[i] = b.n["ve"]

            def stage_c(i):
                b.wait("tensor", "ve", ve_tt[i])
                if i >= 1:
                    b.wait("tensor", "ve", ve_add[i - 1])
                for c in range(nc3):
                    b.tp(ps[2][:, c * 128:(c + 1) * 128], hb[:, c * 128:(c + 1) * 128], ident)
                b.wait("vector", "pe")
                for c in range(nc3):
                    b.cp(hTe[:, c, :], ps[2][:, c * 128:(c + 1) * 128], drain=False)
                b.wait("tensor", "ve")
                for mh in range(2):
                    for c in range(nc3):
                        b.mm(ps[3 + mh][:, 0:512], hTe[:, c, :],
                             w3s[:, c, mh * 512:(mh + 1) * 512],
                             start=(c == 0), stop=(c == nc3 - 1))
                b.wait("vector", "pe")
                b.stt(comb[:, i, 0:512], ps[3][:, 0:512], C["wfull"][:, i, e:e + 1],
                      comb[:, i, 0:512], ALU.mult, ALU.add)
                b.stt(comb[:, i, 512:1024], ps[4][:, 0:512], C["wfull"][:, i, e:e + 1],
                      comb[:, i, 512:1024], ALU.mult, ALU.add)
                ve_add[i] = b.n["ve"]

            stage_a(0)
            stage_b(0)
            for i in range(1, NT):
                stage_a(i)
                stage_c(i - 1)
                stage_b(i)
            stage_c(NT - 1)
            b.barrier()

    xrd = WK[:, 0:2048].rearrange("p (c f) -> p c f", c=2)
    for i in range(NT):
        if i >= 2:
            b.wait("sync", "ve")
        b.dma("sync", xrd[:, i % 2, :], scr_x3[i * 128:(i + 1) * 128, :], f"x3r{i % 2}")
        b.wait_dma("vector", f"x3r{i % 2}")
        b.tt(stream[:, i, :], stream[:, i, :], xrd[:, i % 2, :], ALU.add)
        b.red(C["stats"][:, i:i + 1], stream[:, i, :], op="sum")
    postln_finish(b, C, stream, WK[:, 2048:3072], WK[:, 3072:4096], eps=1e-5)


def _consts():
    r = np.arange(128)
    cc = {}
    cc["ident"] = np.eye(128, dtype=np.float32)
    cc["mask_causal"] = np.where(r[None, :] > r[:, None], np.float32(MASK_VAL), np.float32(0.0)).astype(np.float32)
    cc["mask_ls"] = np.where(r[None, :] < r[:, None], np.float32(0.0), np.float32(NEG_BIG)).astype(np.float32)
    cc["mask_us"] = np.where(r[None, :] > r[:, None], np.float32(0.0), np.float32(NEG_BIG)).astype(np.float32)
    cc["mask_ui"] = np.where(r[None, :] >= r[:, None], np.float32(0.0), np.float32(NEG_BIG)).astype(np.float32)
    cc["ones_ui"] = np.triu(np.ones((128, 128), dtype=np.float32))
    cc["ones_r"] = np.ones((1, 128), dtype=np.float32)
    cc["ones_col"] = np.ones((128, 1), dtype=np.float32)
    cc["iota_f"] = np.broadcast_to(np.arange(128, dtype=np.float32)[None, :], (128, 128)).copy()
    cc["iota_p"] = np.broadcast_to(np.arange(128, dtype=np.float32)[:, None], (128, 128)).copy()
    cc["e127"] = np.zeros((128, 1), dtype=np.float32); cc["e127"][127, 0] = 1.0
    return cc


def _prep_inputs(inp):
    f = np.float32
    def t(a):
        return np.ascontiguousarray(np.asarray(a, dtype=f).T)
    def c(a):
        return np.ascontiguousarray(np.asarray(a, dtype=f))
    def bc(a, n=D):
        return np.ascontiguousarray(np.broadcast_to(np.asarray(a, dtype=f), (128, n)))

    m = {}
    m["l0_wqT"] = t(inp["L0_attn_q_proj_weight"])
    m["l0_wkT"] = t(inp["L0_attn_k_proj_weight"])
    m["l0_wvT"] = t(inp["L0_attn_v_proj_weight"])
    m["l0_woT"] = t(inp["L0_attn_o_proj_weight"])
    m["l0_wg"] = c(inp["L0_ffn_w_gate"])
    m["l0_wu"] = c(inp["L0_ffn_w_up"])
    m["l0_wd"] = c(inp["L0_ffn_w_down"])
    m["l1_wq"] = c(inp["L1_attn_Wq"]); m["l1_wk"] = c(inp["L1_attn_Wk"])
    m["l1_wv"] = c(inp["L1_attn_Wv"]); m["l1_wg"] = c(inp["L1_attn_Wg"])
    m["l1_wo"] = c(inp["L1_attn_Wo"]); m["l1_w2"] = c(inp["L1_attn_W2"])
    m["l1_wab"] = np.ascontiguousarray(
        np.concatenate([np.asarray(inp["L1_attn_Wa"], f), np.asarray(inp["L1_attn_Wb"], f)], axis=1))
    m["l1_convq"] = np.ascontiguousarray(np.asarray(inp["L1_attn_conv_q_w"], f).reshape(8, 128, 4).transpose(1, 0, 2))
    m["l1_convk"] = np.ascontiguousarray(np.asarray(inp["L1_attn_conv_k_w"], f).reshape(8, 128, 4).transpose(1, 0, 2))
    m["l1_convv"] = np.ascontiguousarray(np.asarray(inp["L1_attn_conv_v_w"], f).reshape(16, 128, 4).transpose(1, 0, 2))
    m["l1_dtb"] = bc(inp["L1_attn_dt_bias"], 4)
    m["l1_alog"] = bc(inp["L1_attn_A_log"], 4)
    m["bc_ln1g"] = bc(inp["L1_attn_ln1_g"]); m["bc_ln1b"] = bc(inp["L1_attn_ln1_b"])
    m["bc_ln2g"] = bc(inp["L1_attn_ln2_g"]); m["bc_ln2b"] = bc(inp["L1_attn_ln2_b"])
    m["bc_rmsin"] = bc(inp["L1_attn_rms_in_w"]); m["bc_b2"] = bc(inp["L1_attn_b2"])
    m["bc_moerms"] = bc(inp["L1_ffn_rms_w"]); m["bc_onorm"] = bc(inp["L1_attn_o_norm_w"], 512)
    gwT = np.asarray(inp["L1_ffn_gate_w"], f).T
    m["moe_gwT"] = np.ascontiguousarray(gwT.reshape(8, 128, 8).transpose(1, 0, 2))
    def img_ldT(wT, n0, nw):
        blk = wT[:, n0:n0 + nw]
        return blk.reshape(8, 128, nw).transpose(1, 0, 2).reshape(128, 8 * nw)
    NBW = [384] * 7 + [128]
    w1T = np.asarray(inp["L1_ffn_W1"], f).transpose(0, 2, 1)
    w2T = np.asarray(inp["L1_ffn_W2"], f).transpose(0, 2, 1)
    w3T = np.asarray(inp["L1_ffn_W3"], f).transpose(0, 2, 1)
    im1, im2, im3 = [], [], []
    for e in range(8):
        r1, r2, r3 = [], [], []
        for nb in range(8):
            n0, nw = sum(NBW[:nb]), NBW[nb]
            pad = 384 - nw
            a = img_ldT(w1T[e], n0, nw)
            bb = img_ldT(w2T[e], n0, nw)
            cc = w3T[e][n0:n0 + nw, :].reshape(nw // 128, 128, 1024).transpose(1, 0, 2).reshape(128, -1)
            if pad:
                a = np.concatenate([a, np.zeros((128, 8 * pad), f)], 1)
                bb = np.concatenate([bb, np.zeros((128, 8 * pad), f)], 1)
                cc = np.concatenate([cc, np.zeros((128, (pad // 128) * 1024), f)], 1)
            r1.append(a); r2.append(bb); r3.append(cc)
        im1.append(np.stack(r1)); im2.append(np.stack(r2)); im3.append(np.stack(r3))
    m["moe_w1i"] = np.ascontiguousarray(np.stack(im1))
    m["moe_w2i"] = np.ascontiguousarray(np.stack(im2))
    m["moe_w3i"] = np.ascontiguousarray(np.stack(im3))
    l0p = []
    for p in range(8):
        cols = slice(p * 128, (p + 1) * 128)
        l0p.append(np.concatenate([
            m["l0_wqT"][:, cols].reshape(8, 128, 128).transpose(1, 0, 2).reshape(128, 1024),
            m["l0_wkT"][:, cols].reshape(8, 128, 128).transpose(1, 0, 2).reshape(128, 1024),
            m["l0_wvT"][:, cols].reshape(8, 128, 128).transpose(1, 0, 2).reshape(128, 1024)], 1))
    m["l0_pairs"] = np.ascontiguousarray(np.stack(l0p))
    wd = m["l0_wd"]
    m["l0_wdi"] = np.ascontiguousarray(np.stack(
        [wd[:, nb * 256:(nb + 1) * 256].reshape(8, 128, 256).transpose(1, 0, 2).reshape(128, 2048)
         for nb in range(11)]))
    m.update(_consts())
    return m


def kernel(**inputs):
    return _run(inputs, upto="out")


def _run(inputs, upto="out", s1_limit=None, s2_limit=None, start="x0", s3_limit=None, xin=None):
    x0 = np.asarray(inputs["L0_attn_in_features"], dtype=np.float32)
    shared = _prep_inputs(inputs)
    in_maps = []
    for i in range(8):
        mi = dict(shared)
        mi["x0"] = np.ascontiguousarray(x0[i])
        in_maps.append(mi)
    nc, _ = _build(upto=upto, s1_limit=s1_limit, s2_limit=s2_limit, start=start, s3_limit=s3_limit)
    if xin is not None:
        for i in range(8):
            in_maps[i]["xin"] = np.ascontiguousarray(np.asarray(xin[i % len(xin)], np.float32))
    res = run_bass_kernel_spmd(nc, in_maps, core_ids=list(range(8)))
    return np.stack([np.asarray(res.results[i]["out"]) for i in range(8)], axis=0)


# version 08
# speedup vs baseline: 1.1174x; 1.0047x over previous
import numpy as np
from contextlib import ExitStack
import concourse.bass as bass
import concourse.mybir as mybir
from concourse.bass_utils import run_bass_kernel_spmd

F32 = mybir.dt.float32
AF = mybir.ActivationFunctionType
ALU = mybir.AluOpType
AX = mybir.AxisListType.X

T = 2048
D = 1024
NT = 16
MASK_VAL = -1.0e8
NEG_BIG = -1.0e9

FP4_MID = [0.25, 0.75, 1.25, 1.75, 2.5, 3.5, 5.0]
FP4_WGT = [0.5, 0.5, 0.5, 0.5, 1.0, 1.0, 2.0]


class B:
    ENGS = ("tensor", "vector", "scalar", "sync", "gpsimd")

    def __init__(self, nc, ctx):
        self.nc = nc
        self.ctx = ctx
        self.q = {e: [] for e in self.ENGS}
        self.n = {"pe": 0, "ve": 0, "se": 0}
        self.dn = {}
        self.sems = {}
        self.pend = {e: {} for e in self.ENGS}

    def get_sem(self, name):
        if name not in self.sems:
            self.sems[name] = self.ctx.enter_context(self.nc.semaphore(name))
        return self.sems[name]

    def _flush_pend(self, eng, keep_one):
        pend = self.pend[eng]
        emb = None
        if pend:
            items = sorted(pend.items())
            if keep_one:
                emb = items.pop()
            for ch, v in items:
                self.q[eng].append(lambda e, ch=ch, v=v: e.wait_ge(self.sems[ch], v))
            self.pend[eng] = {}
        return emb

    def emit(self, eng, fn, inc):
        emb = self._flush_pend(eng, keep_one=(eng in ("tensor", "vector", "scalar")))
        if inc is not None:
            self.n[inc] += 1
            self.get_sem(inc)
        def thunk(e, fn=fn, inc=inc, emb=emb):
            ins = fn(e)
            if emb is not None:
                ins._wait_ge(self.sems[emb[0]], emb[1])
            if inc is not None:
                ins.then_inc(self.sems[inc], 1)
        self.q[eng].append(thunk)

    def wait(self, eng, ch, val=None):
        v = self.n[ch] if val is None else val
        self.get_sem(ch)
        p = self.pend[eng]
        if p.get(ch, -1) < v:
            p[ch] = v

    def dma(self, queue, out, in_, sem):
        self._flush_pend(queue, keep_one=False)
        self.dn[sem] = self.dn.get(sem, 0) + 16
        self.get_sem(sem)
        def thunk(e, out=out, in_=in_, sem=sem):
            e.dma_start(out=out, in_=in_).then_inc(self.sems[sem], 16)
        self.q[queue].append(thunk)

    def wait_dma(self, eng, sem, val=None):
        v = self.dn[sem] if val is None else val
        self.get_sem(sem)
        p = self.pend[eng]
        if p.get(sem, -1) < v:
            p[sem] = v

    def barrier(self):
        vals = dict(self.n)
        dvals = dict(self.dn)
        for eng in self.ENGS:
            for ch, v in vals.items():
                if v > 0:
                    self.wait(eng, ch, v)
            for sm, v in dvals.items():
                if v > 0:
                    self.wait_dma(eng, sm, v)

    def _drain(self, eng):
        ch = {"vector": "ve", "scalar": "se", "tensor": "pe"}[eng]
        if self.n[ch] > 0:
            self.wait(eng, ch)

    def mm(self, out, lhsT, rhs, start, stop):
        self.emit("tensor", lambda e: e.matmul(out, lhsT, rhs, start=start, stop=stop), "pe")

    def tp(self, out, in_, ident):
        self.emit("tensor", lambda e: e.transpose(out, in_, ident), "pe")

    def cp(self, out, in_, eng="vector", drain=True):
        if drain:
            self._drain(eng)
        ch = "ve" if eng == "vector" else "se"
        if eng == "vector":
            self.emit(eng, lambda e: e.tensor_copy(out, in_), ch)
        else:
            self.emit(eng, lambda e: e.copy(out, in_), ch)

    def tt(self, out, in0, in1, op, drain=True):
        if drain:
            self._drain("vector")
        self.emit("vector", lambda e: e.tensor_tensor(out, in0, in1, op), "ve")

    def ts(self, out, in0, s1, s2, op0, op1=None, accum=None, drain=True):
        if drain:
            self._drain("vector")
        if op1 is None:
            self.emit("vector", lambda e: e.tensor_scalar(out, in0, s1, None, op0, accum_out=accum), "ve")
        else:
            self.emit("vector", lambda e: e.tensor_scalar(out, in0, s1, s2, op0, op1, accum_out=accum), "ve")

    def stt(self, out, in0, scalar, in1, op0, op1, accum=None, drain=True):
        if drain:
            self._drain("vector")
        self.emit("vector", lambda e: e.scalar_tensor_tensor(
            out=out, in0=in0, scalar=scalar, in1=in1, op0=op0, op1=op1, accum_out=accum), "ve")

    def red(self, out, in_, op="max", negate=False, absval=False, drain=True):
        if drain:
            self._drain("vector")
        if op == "max":
            self.emit("vector", lambda e: e.reduce_max(out, in_, axis=AX, negate=negate,
                                                       apply_absolute_value=absval), "ve")
        else:
            self.emit("vector", lambda e: e.reduce_sum(out, in_, axis=AX, negate=negate,
                                                       apply_absolute_value=absval), "ve")

    def recip(self, out, in_, drain=True):
        if drain:
            self._drain("vector")
        self.emit("vector", lambda e: e.reciprocal(out, in_), "ve")

    def memset(self, ap, val, drain=False):
        if drain:
            self._drain("vector")
        self.emit("vector", lambda e: e.memset(ap, val), "ve")

    def act(self, out, in_, func, bias=0.0, scale=1.0, accum=None, drain=True):
        if drain:
            self._drain("scalar")
        self.emit("scalar", lambda e: e.activation(out, in_, func, bias=bias, scale=scale,
                                                   accum_out=accum), "se")

    def smul(self, out, in_, mul, drain=True):
        if drain:
            self._drain("scalar")
        self.emit("scalar", lambda e: e.mul(out, in_, mul), "se")



def vgaps(b, C):
    b.wait("vector", "ve")


def sgaps(b, C):
    b.wait("scalar", "se")


def tp_batch(b, C, ps, items, banks=(0, 1, 2, 3)):
    ident = C["ident"][:]
    nb = len(banks)
    ve_cp = {}
    b.wait("tensor", "ve")
    for k, (src, dst) in enumerate(items):
        slot = ps[banks[k % nb]][:, (k // nb % 4) * 128:(k // nb % 4) * 128 + 128]
        if k >= nb:
            b.wait("tensor", "ve", ve_cp[k - nb])
        b.tp(slot, src, ident)
        b.wait("vector", "pe")
        b.cp(dst, slot, drain=False)
        ve_cp[k] = b.n["ve"]


def bcast_col(b, C, ps, col_ap, dst_tile, bank=7):
    ident = C["ident"][:]
    b.wait("tensor", "ve")
    b.tp(ps[bank][0:1, 0:128], col_ap, ident)
    b.wait("vector", "pe")
    b.cp(C["rowb"][0:1, :], ps[bank][0:1, 0:128], drain=False)
    b.wait("tensor", "ve")
    b.mm(ps[bank][:, 128:256], C["ones_r"][:], C["rowb"][0:1, :], start=True, stop=True)
    b.wait("vector", "pe")
    b.cp(dst_tile, ps[bank][:, 128:256], drain=False)


def rstd_chain(b, C, v_col, out_col, n_feats, eps, extra_scale=None, is_sum=False):
    t1, t2 = C["st_t1"], C["st_t2"]
    w = v_col.shape[1]
    if is_sum:
        b.ts(v_col, v_col, 1.0, eps, ALU.mult, ALU.add)
    else:
        b.ts(v_col, v_col, 1.0 / n_feats, eps, ALU.mult, ALU.add)
    b.wait("scalar", "ve")
    b.act(t1[:, 0:w], v_col, AF.Sqrt)
    b.wait("vector", "se")
    b.recip(out_col, t1[:, 0:w])
    vgaps(b, C)
    for _ in range(2):
        b.tt(t1[:, 0:w], out_col, out_col, ALU.mult)
        vgaps(b, C)
        b.tt(t1[:, 0:w], t1[:, 0:w], v_col, ALU.mult)
        vgaps(b, C)
        b.ts(t1[:, 0:w], t1[:, 0:w], -0.5, 1.5, ALU.mult, ALU.add)
        vgaps(b, C)
        b.tt(out_col, out_col, t1[:, 0:w], ALU.mult)
        vgaps(b, C)
    if extra_scale is not None:
        b.ts(out_col, out_col, extra_scale, None, ALU.mult)
        vgaps(b, C)


def fp4_quant(b, C, x_ap, out_ap, width, wrk):
    G = width // 16
    absx, cmp = wrk["absx"], wrk["cmp"]
    mx, scale = wrk["mx"], wrk["scale"]
    b.wait("scalar", "ve")
    b.act(absx, x_ap, AF.Abs)
    b.wait("vector", "se")
    xv = absx.rearrange("p (g c) -> p g c", c=16)
    b.red(mx, xv, op="max")
    b.ts(scale, mx, 1.0 / 6.0, 1e-12, ALU.mult, ALU.max)
    rsc = wrk["thr"]
    b.recip(rsc, scale)
    rs_b = rsc.unsqueeze(2).broadcast_to([128, G, 16])
    sc_b = scale.unsqueeze(2).broadcast_to([128, G, 16])
    b.tt(xv, xv, rs_b, ALU.mult)
    if "cmp2" in wrk:
        cmps = (cmp, wrk["cmp2"])
        se_j, ve_j = {}, {}
        b.wait("scalar", "ve")
        for j in range(7):
            if j >= 2:
                b.wait("scalar", "ve", ve_j[j - 2])
            b.act(cmps[j % 2], absx, AF.Sign, bias=C["fp4mids"][:, j:j + 1])
            se_j[j] = b.n["se"]
            b.wait("vector", "se", se_j[j])
            if j == 0:
                b.ts(out_ap, cmps[0], FP4_WGT[0] / 2.0, None, ALU.mult, drain=False)
            else:
                b.stt(out_ap, cmps[j % 2], FP4_WGT[j] / 2.0, out_ap, ALU.mult, ALU.add)
            ve_j[j] = b.n["ve"]
        b.wait("scalar", "ve")
        b.act(cmp, x_ap, AF.Sign)
        b.wait("vector", "se")
        b.ts(out_ap, out_ap, 1.0, 3.0, ALU.mult, ALU.add)
        b.tt(out_ap, out_ap, cmp, ALU.mult)
    else:
        b.memset(out_ap, 0.0)
        for j in range(7):
            b.ts(cmp, absx, FP4_MID[j], None, ALU.is_gt)
            b.stt(out_ap, cmp, FP4_WGT[j], out_ap, ALU.mult, ALU.add)
        b.wait("scalar", "ve")
        b.act(cmp, x_ap, AF.Sign)
        b.wait("vector", "se")
        b.tt(out_ap, out_ap, cmp, ALU.mult)
    ov = out_ap.rearrange("p (g c) -> p g c", c=16)
    b.tt(ov, ov, sc_b, ALU.mult)


def _build(upto="out", s1_limit=None, s2_limit=None, start="x0", s3_limit=None):
    nc = bass.Bass("TRN2", target_bir_lowering=False)
    b = B(None, None)

    dr = {}
    def din(name, shape):
        dr[name] = nc.dram_tensor(name, shape, F32, kind="ExternalInput").ap()
        return dr[name]
    def dint(name, shape):
        dr[name] = nc.dram_tensor(name, shape, F32, kind="Internal").ap()
        return dr[name]

    x0_d = din("x0", [T, D])
    xin_d = nc.dram_tensor("xin", [T, D], F32, kind="ExternalInput").ap() if start != "x0" else None
    din("l0_woT", [D, D])
    din("l0_wg", [2816, D]); din("l0_wu", [2816, D])
    din("l1_wq", [D, 1024]); din("l1_wk", [D, 1024]); din("l1_wv", [D, 2048])
    din("l1_wg", [D, 2048]); din("l1_wo", [2048, D]); din("l1_w2", [D, D])
    din("l1_wab", [D, 8])
    din("l1_convq", [128, 8, 4]); din("l1_convk", [128, 8, 4]); din("l1_convv", [128, 16, 4])
    din("l1_dtb", [128, 4]); din("l1_alog", [128, 4])
    din("bc_ln1g", [128, D]); din("bc_ln1b", [128, D]); din("bc_ln2g", [128, D])
    din("bc_ln2b", [128, D]); din("bc_rmsin", [128, D]); din("bc_b2", [128, D])
    din("bc_moerms", [128, D]); din("bc_onorm", [128, 512])
    din("moe_gwT", [128, 8, 8])
    din("moe_w1i", [8, 8, 128, 3072]); din("moe_w2i", [8, 8, 128, 3072])
    din("moe_w3i", [8, 8, 128, 3072])
    din("l0_pairs", [8, 128, 3072]); din("l0_wdi", [11, 128, 2048])
    din("ones_col", [128, 1]); din("iota_f", [128, 128]); din("iota_p", [128, 128])
    din("e127", [128, 1]); din("fp4mids", [128, 8])
    din("ident", [128, 128]); din("mask_causal", [128, 128])
    din("mask_ls", [128, 128]); din("mask_us", [128, 128]); din("mask_ui", [128, 128])
    din("ones_ui", [128, 128]); din("ones_r", [1, 128])

    out_d = nc.dram_tensor("out", [T, D], F32, kind="ExternalOutput").ap()

    scr_x1qT = dint("scr_x1qT", [NT, 128, 1024])
    scr_x2 = dint("scr_x2", [T, D])
    scr_qT = dint("scr_qT", [NT, 128, 1024])
    scr_kT = dint("scr_kT", [NT, 128, 1024])
    scr_vhat = dint("scr_vhat", [T, 2048])
    scr_sgate = dint("scr_sgate", [T, 2048])
    scr_osc = dint("scr_osc", [T, 2048])
    scr_x3 = dint("scr_x3", [T, D])

    with ExitStack() as ctx:
        def sb(name, shape, dt=F32):
            return ctx.enter_context(nc.sbuf_tensor(name, shape, dt))

        A1 = sb("A1", [128, 16384])
        A2 = sb("A2", [128, 16384])
        WA = sb("WA", [128, 9216])
        WK = sb("WK", [128, 8704])
        C = {}
        C["ident"] = sb("c_ident", [128, 128])
        C["mask_causal"] = sb("c_mcz", [128, 128])
        C["mask_ls"] = sb("c_mls", [128, 128])
        C["mask_us"] = sb("c_mus", [128, 128])
        C["mask_ui"] = sb("c_mui", [128, 128])
        C["ones_ui"] = sb("c_oui", [128, 128])
        C["ones_r"] = sb("c_or", [1, 128])
        C["junk1"] = sb("c_j1", [128, 4])
        C["junk2"] = sb("c_j2", [128, 4])
        C["junkS1"] = sb("c_js1", [128, 4])
        C["junkS2"] = sb("c_js2", [128, 4])
        stats = sb("c_stats", [128, 96])
        C["stats"] = stats
        C["st_t1"] = stats[:, 64:80]
        C["st_t2"] = stats[:, 80:96]
        C["gb"] = sb("c_gb", [128, 16, 8])
        C["wfull"] = sb("c_wfull", [128, 16, 8])
        C["gwT"] = sb("c_gwT", [128, 8, 8])
        C["convq"] = sb("c_convq", [128, 8, 4])
        C["convk"] = sb("c_convk", [128, 8, 4])
        C["convv"] = sb("c_convv", [128, 16, 4])
        C["coef"] = sb("c_coef", [128, 4])
        C["dtb"] = sb("c_dtb", [128, 4])
        C["ones_col"] = sb("c_onescol", [128, 1])
        C["rowb"] = sb("c_rowb", [1, 128])
        C["iota_f"] = sb("c_iotaf", [128, 128])
        C["iota_p"] = sb("c_iotap", [128, 128])
        C["e127"] = sb("c_e127", [128, 1])
        C["fp4mids"] = sb("c_fp4mids", [128, 8])
        C["rank"] = sb("c_rank", [128, 16, 8])
        C["offrow"] = sb("c_offrow", [1, 8])
        C["wab"] = sb("c_wab", [128, 8, 8])

        ps = [ctx.enter_context(nc.psum_tensor(f"ps{i}", [128, 512], F32)) for i in range(8)]

        b.nc, b.ctx = nc, ctx
        blk = ctx.enter_context(nc.Block())

        ident = C["ident"]

        for nm in ("ident", "mask_causal", "mask_ls", "mask_us", "mask_ui", "ones_ui"):
            b.dma("sync", C[nm][:], dr[nm], "c0")
        b.dma("sync", C["ones_r"][:], dr["ones_r"], "c0")
        b.dma("sync", C["gwT"][:], dr["moe_gwT"], "c0")
        b.dma("sync", C["convq"][:], dr["l1_convq"], "c0")
        b.dma("sync", C["convk"][:], dr["l1_convk"], "c0")
        b.dma("sync", C["convv"][:], dr["l1_convv"], "c0")
        b.dma("sync", C["dtb"][:], dr["l1_dtb"], "c0")
        b.dma("sync", C["coef"][:], dr["l1_alog"], "c0")
        b.dma("sync", C["ones_col"][:], dr["ones_col"], "c0")
        b.dma("sync", C["iota_f"][:], dr["iota_f"], "c0")
        b.dma("sync", C["iota_p"][:], dr["iota_p"], "c0")
        b.dma("sync", C["e127"][:], dr["e127"], "c0")
        b.dma("sync", C["fp4mids"][:], dr["fp4mids"], "c0")
        b.dma("sync", C["wab"][:], dr["l1_wab"][:].rearrange("(c p) f -> p c f", p=128), "c0")
        b.wait_dma("vector", "c0")
        b.memset(C["junk1"][:], 0.0)
        b.memset(C["junkS1"][:], 0.0)
        b.wait_dma("scalar", "c0")
        b.act(C["coef"][:], C["coef"][:], AF.Exp, scale=1.0)
        b.wait("vector", "se")
        b.ts(C["coef"][:], C["coef"][:], -1.0, None, ALU.mult)
        b.barrier()

        x0T = A1[:].rearrange("p (c t) -> p c t", c=8)
        stream = A1[:].rearrange("p (c t) -> p c t", c=16)
        aoT = A2[:].rearrange("p (c t) -> p c t", c=8)
        A2_16 = A2[:].rearrange("p (c t) -> p c t", c=16)

        if start == "x0":
            emit_s1(nc, b, C, dr, ps, A1, A2, WA, WK, x0T, stream, aoT, x0_d, limit=s1_limit)
        else:
            xld = WK[:, 0:2048].rearrange("p (c f) -> p c f", c=2)
            for i in range(NT):
                b.dma("sync", xld[:, i % 2, :], xin_d[i * 128:(i + 1) * 128, :], f"xi{i % 2}")
                b.wait_dma("vector", f"xi{i % 2}")
                b.cp(stream[:, i, :], xld[:, i % 2, :], drain=False)
                b.dma("gpsimd", scr_x2[i * 128:(i + 1) * 128, :], xin_d[i * 128:(i + 1) * 128, :], "x2w")
                if i < NT - 2:
                    b.wait("sync", "ve")
            b.barrier()

        if upto == "x1":
            finish(b, stream, out_d)
            run_block(b, blk)
            return nc, b

        if start == "x0":
            emit_s2(nc, b, C, dr, ps, A1, A2, WA, WK, stream, A2_16, scr_x1qT, scr_x2, limit=s2_limit)

        if upto == "x2":
            finish(b, stream, out_d)
            run_block(b, blk)
            return nc, b

        if start in ("x0", "x2"):
            emit_s3(nc, b, C, dr, ps, A1, A2, WA, WK, stream, A2_16,
                    scr_x2, scr_qT, scr_kT, scr_vhat, scr_sgate, scr_osc, limit=s3_limit)

        if upto == "x3":
            finish(b, stream, out_d)
            run_block(b, blk)
            return nc, b

        emit_s4(nc, b, C, dr, ps, A1, A2, WA, WK, stream, A2_16, scr_x3)

        finish(b, stream, out_d)
        run_block(b, blk)
    return nc, b


def finish(b, stream, out_d):
    b.barrier()
    for i in range(NT):
        b.dma("gpsimd", out_d[i * 128:(i + 1) * 128, :], stream[:, i, :], "outw")
    b.wait_dma("gpsimd", "outw")


def run_block(b, blk):
    for eng in B.ENGS:
        b._flush_pend(eng, keep_one=False)
    for eng in B.ENGS:
        thunks = b.q[eng]
        getattr(blk, eng)(lambda e, thunks=thunks: [t(e) for t in thunks])


def emit_s1(nc, b, C, dr, ps, A1, A2, WA, WK, x0T, stream, aoT, x0_d, limit=None):
    ident = C["ident"][:]
    Vp = WK[:, 0:2048].rearrange("p (c t) -> p c t", c=16)
    srow = WK[:, 2048:4096]
    prow = WK[:, 4096:6144]
    ptslab = WK[:, 6144:8192].rearrange("p (c t) -> p c t", c=16)
    avo2 = WK[:, 8192:8448]
    xtile = WK[:, 0:2048].rearrange("p (c t) -> p c t", c=2)
    wqb = WA[:, 0:1024].rearrange("p (c t) -> p c t", c=8)
    wkb = WA[:, 1024:2048].rearrange("p (c t) -> p c t", c=8)
    wvb = WA[:, 2048:3072].rearrange("p (c t) -> p c t", c=8)
    QT = WA[:, 3072:5120]
    KT = WA[:, 5120:7168]
    woT = WA[:, 0:8192].rearrange("p (c t) -> p c t", c=8)

    ve_tile = {}
    for i in range(NT):
        if i >= 2:
            b.wait("sync", "ve", ve_tile[i - 2])
        b.dma("sync", xtile[:, i % 2, :], x0_d[i * 128:(i + 1) * 128, :], f"xt{i % 2}")
        b.wait_dma("tensor", f"xt{i % 2}")
        if i >= 2:
            b.wait("tensor", "ve", ve_tile[i - 2])
        bk = 2 * (i % 2)
        for c in range(8):
            slot = ps[bk + c // 4][:, (c % 4) * 128:(c % 4) * 128 + 128]
            b.tp(slot, xtile[:, i % 2, c * 128:(c + 1) * 128], ident)
        b.wait("vector", "pe")
        for c in range(8):
            slot = ps[bk + c // 4][:, (c % 4) * 128:(c % 4) * 128 + 128]
            b.cp(x0T[:, c, i * 128:(i + 1) * 128], slot, drain=False)
        ve_tile[i] = b.n["ve"]
    b.barrier()
    if limit == "p0":
        return

    for p in range(8):
        if p > 0:
            b.wait("sync", "pe")
        b.dma("sync", WA[:, 0:3072].rearrange("p (c f) -> p c f", c=24),
              dr["l0_pairs"][p].rearrange("p (c f) -> p c f", c=24), "wpair")
        b.wait_dma("tensor", "wpair")

        for dst, wb, psl in ((QT, wqb, (0, 1)), (KT, wkb, (2, 3))):
            for n in range(4):
                pp = ps[psl[n % 2]]
                if n >= 2:
                    b.wait("tensor", "ve")
                for c in range(8):
                    b.mm(pp[:, 0:512], wb[:, c, :], x0T[:, c, n * 512:(n + 1) * 512],
                         start=(c == 0), stop=(c == 7))
                b.wait("vector", "pe")
                b.cp(dst[:, n * 512:(n + 1) * 512], pp[:, 0:512])
        if limit == "pqk" and p == 0:
            b.barrier()
            return
        for i in range(NT):
            pp = ps[4 + (i % 2)]
            if i >= 2:
                b.wait("tensor", "ve")
            for c in range(8):
                b.mm(pp[:, 0:128], x0T[:, c, i * 128:(i + 1) * 128], wvb[:, c, :],
                     start=(c == 0), stop=(c == 7))
            b.wait("vector", "pe")
            b.cp(Vp[:, i, :], pp[:, 0:128])
        b.barrier()
        if limit == "pv" and p == 0:
            return

        tasks = [(i, hh) for i in range(NT) for hh in range(2)]
        srow2 = (srow, WA[:, 7168:9216])
        se_exp, pe_tp, pe_av, ve_sm, se_mul = {}, {}, {}, {}, {}
        ve_b67 = {6: 0, 7: 0}
        ve_av45 = {4: 0, 5: 0}

        def stage_sc(k):
            i, hh = tasks[k]
            L = (i + 1) * 128
            hsl = slice(hh * 64, hh * 64 + 64)
            sr = srow2[k % 2]
            if k >= 2:
                b.wait("vector", "se", se_exp[k - 2])
            nkg = (L + 511) // 512
            for kg in range(nkg):
                nk = min(512, L - kg * 512)
                bank = 6 + kg % 2
                b.wait("tensor", "ve", ve_b67[bank])
                b.mm(ps[bank][:, 0:nk], QT[hsl, i * 128:(i + 1) * 128],
                     KT[hsl, kg * 512:kg * 512 + nk], start=True, stop=True)
                b.wait("vector", "pe")
                b.ts(sr[:, kg * 512:kg * 512 + nk], ps[bank][:, 0:nk], 0.125,
                     None, ALU.mult, drain=False)
                ve_b67[bank] = b.n["ve"]

        def stage_sm(k):
            i, hh = tasks[k]
            L = (i + 1) * 128
            sr = srow2[k % 2]
            b.tt(sr[:, i * 128:L], sr[:, i * 128:L], C["mask_causal"][:], ALU.add)
            nmx = C["stats"][:, 8 + k % 2:9 + k % 2]
            ssum = C["stats"][:, 10 + k % 2:11 + k % 2]
            b.red(nmx, sr[:, 0:L], op="max", negate=True)
            b.wait("scalar", "ve")
            if k >= 1:
                b.wait("scalar", "pe", pe_tp[k - 1])
            b.act(prow[:, 0:L], sr[:, 0:L], AF.Exp, bias=nmx, accum=ssum)
            se_exp[k] = b.n["se"]
            b.wait("vector", "se", se_exp[k])
            b.recip(C["stats"][:, 12 + k % 2:13 + k % 2], ssum)
            ve_sm[k] = b.n["ve"]

        def stage_ta(k):
            i, hh = tasks[k]
            hsl = slice(hh * 64, hh * 64 + 64)
            b.wait("tensor", "se", se_exp[k])
            b.wait("tensor", "ve")
            ve_copy = {}
            for j in range(i + 1):
                slot = ps[j % 4][:, (j // 4) * 128:(j // 4) * 128 + 128]
                if j >= 4:
                    b.wait("tensor", "ve", ve_copy[j - 4])
                b.tp(slot, prow[:, j * 128:(j + 1) * 128], ident)
                b.wait("vector", "pe")
                b.cp(ptslab[:, j, :], slot, drain=False)
                ve_copy[j] = b.n["ve"]
            pe_tp[k] = b.n["pe"]
            b.wait("tensor", "ve")
            bank = 4 + k % 2
            b.wait("tensor", "ve", ve_av45[bank])
            if k >= 2:
                b.wait("tensor", "se", se_mul[k - 2])
            for j in range(i + 1):
                b.mm(ps[bank][:, 0:64], ptslab[:, j, :], Vp[:, j, hsl],
                     start=(j == 0), stop=(j == i))
            pe_av[k] = b.n["pe"]
            b.wait("scalar", "pe", pe_av[k])
            b.wait("scalar", "ve", ve_sm[k])
            b.smul(avo2[:, (i % 2) * 128 + hh * 64:(i % 2) * 128 + hh * 64 + 64],
                   ps[bank][:, 0:64], C["stats"][:, 12 + k % 2:13 + k % 2])
            se_mul[k] = b.n["se"]
            if hh == 1:
                b.wait("tensor", "se", se_mul[k])
                b.tp(ps[bank][:, 128:256], avo2[:, (i % 2) * 128:(i % 2) * 128 + 128],
                     ident)
                b.wait("vector", "pe")
                b.cp(aoT[:, p, i * 128:(i + 1) * 128], ps[bank][:, 128:256], drain=False)
            ve_av45[bank] = b.n["ve"]

        stage_sc(0)
        stage_sm(0)
        for k in range(1, len(tasks)):
            stage_sc(k)
            stage_ta(k - 1)
            stage_sm(k)
        stage_ta(len(tasks) - 1)
        if limit == "p1one" and p == 0:
            b.barrier()
            return
    b.barrier()
    if limit == "p1":
        return

    b.dma("sync", woT[:], dr["l0_woT"][:].rearrange("(c p2) f -> p2 c f", p2=128), "wbig")
    ztmp = WK[:, 0:1024]
    xt0 = WK[:, 1024:3072].rearrange("p (c t) -> p c t", c=2)
    sqd = WK[:, 3072:4096]
    b.wait_dma("tensor", "wbig")
    ve_chunk = {}
    for i in range(NT):
        if i >= 2:
            b.wait("sync", "ve", ve_chunk[i - 2])
        b.dma("sync", xt0[:, i % 2, :], x0_d[i * 128:(i + 1) * 128, :], f"xt{i % 2}")
        if i >= 1:
            b.wait("tensor", "ve")
        for mh in range(2):
            pp = ps[mh]
            for c in range(8):
                b.mm(pp[:, 0:512], aoT[:, c, i * 128:(i + 1) * 128],
                     woT[:, c, mh * 512:(mh + 1) * 512], start=(c == 0), stop=(c == 7))
        b.wait("vector", "pe")
        b.wait_dma("vector", f"xt{i % 2}")
        b.tt(stream[:, i, 0:512], ps[0][:, 0:512], xt0[:, i % 2, 0:512], ALU.add)
        b.tt(stream[:, i, 512:1024], ps[1][:, 0:512], xt0[:, i % 2, 512:1024], ALU.add)
        b.red(C["stats"][:, i:i + 1], stream[:, i, :], op="sum")
        ve_chunk[i] = b.n["ve"]
    postln_finish(b, C, stream, ztmp, sqd, eps=1e-5)


def postln_finish(b, C, stream, ztmp, sqd, eps):
    stats = C["stats"]
    zsum, ssq = stats[:, 0:16], stats[:, 16:32]
    nmean, rstd = stats[:, 32:48], stats[:, 48:64]
    vgaps(b, C)
    b.ts(nmean, zsum, -1.0 / 1024.0, None, ALU.mult)
    b.wait("scalar", "ve")
    for i in range(NT):
        b.act(ztmp, stream[:, i, :], AF.Identity, bias=nmean[:, i:i + 1])
        b.cp(stream[:, i, :], ztmp, eng="scalar")
        b.wait("vector", "se")
        b.stt(sqd, stream[:, i, :], 1.0, stream[:, i, :], ALU.mult, ALU.mult,
              accum=ssq[:, i:i + 1])
    vgaps(b, C)
    rstd_chain(b, C, ssq, rstd, 1024, eps)
    b.wait("scalar", "ve")
    for i in range(NT):
        b.act(ztmp, stream[:, i, :], AF.Identity, scale=rstd[:, i:i + 1])
        b.cp(stream[:, i, :], ztmp, eng="scalar")
    b.barrier()


def emit_s2(nc, b, C, dr, ps, A1, A2, WA, WK, stream, A2_16, scr_x1qT, scr_x2, limit=None):
    ident = C["ident"][:]
    NB, NW = 11, 256

    xq = WK[:, 0:1024]
    wrk = {"absx": WK[:, 1024:2048], "cmp": WK[:, 2048:3072],
           "mx": WK[:, 3072:3136], "scale": WK[:, 3136:3200], "thr": WK[:, 3200:3264],
           "cmp2": WK[:, 5312:6336]}
    xqTb = WK[:, 3264:5312].rearrange("p (s c f) -> p s c f", s=2, c=8)
    ve_done = {}
    for i in range(NT):
        b.wait("vector", "pe")
        fp4_quant(b, C, stream[:, i, :], xq, 1024, wrk)
        bk = 2 * (i % 2)
        if i >= 2:
            b.wait("tensor", "ve", ve_done[i - 2])
        b.wait("tensor", "ve")
        for c in range(8):
            slot = ps[bk + c // 4][:, (c % 4) * 128:(c % 4) * 128 + 128]
            b.tp(slot, xq[:, c * 128:(c + 1) * 128], ident)
        b.wait("vector", "pe")
        if i >= 2:
            b.wait_dma("vector", f"xqw{i % 2}")
        for c in range(8):
            slot = ps[bk + c // 4][:, (c % 4) * 128:(c % 4) * 128 + 128]
            b.cp(xqTb[:, i % 2, c, :], slot, drain=False)
        ve_done[i] = b.n["ve"]
        b.wait("gpsimd", "ve", ve_done[i])
        b.dma("gpsimd", scr_x1qT[i].rearrange("p (c f) -> p c f", c=8),
              xqTb[:, i % 2, :, :], f"xqw{i % 2}")
    b.barrier()
    if limit == "q0":
        return

    for i in range(NT):
        b.memset(A2_16[:, i, :], 0.0)
    b.barrier()

    wrows = WK[:, 0:2048]
    wqo = WK[:, 2048:4096]
    wrkw = {"absx": WK[:, 4096:6144], "cmp": WK[:, 6144:8192],
            "mx": WK[:, 8192:8320], "scale": WK[:, 8320:8448], "thr": WK[:, 8448:8576]}
    wgqT = WA[:, 0:2048].rearrange("p (c f) -> p c f", c=8)
    wuqT = WA[:, 2048:4096].rearrange("p (c f) -> p c f", c=8)
    wdqT = WA[:, 4096:6144].rearrange("p (c f) -> p c f", c=2)
    for nb in range(NB):
        n0 = nb * NW
        for wi, (src, dstT) in enumerate(
                ((dr["l0_wg"], wgqT), (dr["l0_wu"], wuqT), (None, wdqT))):
            b.wait("sync", "ve")
            b.wait("sync", "se")
            if wi < 2:
                b.dma("sync", wrows.rearrange("p (c f) -> p c f", c=2),
                      src[n0:n0 + NW, :].rearrange("(c p) f -> p c f", p=128), "wrow")
            else:
                b.dma("sync", wrows.rearrange("p (c f) -> p c f", c=8),
                      dr["l0_wdi"][nb].rearrange("p (c f) -> p c f", c=8), "wrow")
            b.wait_dma("vector", "wrow")
            b.wait_dma("scalar", "wrow")
            b.wait("vector", "pe")
            fp4_quant(b, C, wrows, wqo, 2048, wrkw)
            b.wait("tensor", "ve")
            if wi < 2:
                items = [(wqo[:, r * 1024 + c * 128: r * 1024 + (c + 1) * 128],
                          dstT[:, c, r * 128:(r + 1) * 128])
                         for r in range(2) for c in range(8)]
            else:
                items = [(wqo[:, c * 256 + nn * 128: c * 256 + (nn + 1) * 128],
                          dstT[:, nn, c * 128:(c + 1) * 128])
                         for c in range(8) for nn in range(2)]
            tp_batch(b, C, ps, items)
        b.barrier()
        if limit is not None and limit == f"w{nb}":
            return

        xqTs = WK[:, 0:2048].rearrange("p (s c f) -> p s c f", s=2, c=8)
        hb = WK[:, 2048:2304]
        tmps = WK[:, 2304:2560]
        hq = WK[:, 2560:2816]
        hqT = WK[:, 2816:3072].rearrange("p (c f) -> p c f", c=2)
        wrkh = {"absx": WK[:, 3072:3328], "cmp": WK[:, 3328:3584],
                "mx": WK[:, 3584:3600], "scale": WK[:, 3600:3616], "thr": WK[:, 3616:3632],
                "cmp2": WK[:, 3632:3888]}
        pe_mm, se_silu, ve_tt, ve_q, ve_add = {}, {}, {}, {}, {}

        def stage_a(i):
            gb, ub = (0, 1) if i % 2 == 0 else (5, 6)
            if i >= 2:
                b.wait("sync", "pe", pe_mm[i - 2])
            b.dma("sync", xqTs[:, i % 2, :, :],
                  scr_x1qT[i].rearrange("p (c f) -> p c f", c=8), f"xqr{i % 2}")
            b.wait_dma("tensor", f"xqr{i % 2}")
            if i >= 2:
                b.wait("tensor", "se", se_silu[i - 2])
                b.wait("tensor", "ve", ve_tt[i - 2])
            for c in range(8):
                b.mm(ps[gb][:, 0:NW], xqTs[:, i % 2, c, :], wgqT[:, c, :],
                     start=(c == 0), stop=(c == 7))
            for c in range(8):
                b.mm(ps[ub][:, 0:NW], xqTs[:, i % 2, c, :], wuqT[:, c, :],
                     start=(c == 0), stop=(c == 7))
            pe_mm[i] = b.n["pe"]

        def stage_b(i):
            gb, ub = (0, 1) if i % 2 == 0 else (5, 6)
            b.wait("scalar", "pe", pe_mm[i])
            b.wait("scalar", "ve")
            b.act(tmps, ps[gb][:, 0:NW], AF.Silu)
            se_silu[i] = b.n["se"]
            b.wait("vector", "se", se_silu[i])
            b.tt(hb, tmps, ps[ub][:, 0:NW], ALU.mult)
            ve_tt[i] = b.n["ve"]
            if i >= 1:
                b.wait("vector", "pe")
            fp4_quant(b, C, hb, hq, NW, wrkh)
            ve_q[i] = b.n["ve"]

        def stage_c(i):
            b.wait("tensor", "ve", ve_q[i])
            if i >= 1:
                b.wait("tensor", "ve", ve_add[i - 1])
            for c in range(2):
                b.tp(ps[2][:, c * 128:(c + 1) * 128], hq[:, c * 128:(c + 1) * 128], ident)
            b.wait("vector", "pe")
            b.cp(hqT[:, 0, :], ps[2][:, 0:128], drain=False)
            b.cp(hqT[:, 1, :], ps[2][:, 128:256], drain=False)
            b.wait("tensor", "ve")
            for mh in range(2):
                for c in range(2):
                    b.mm(ps[3 + mh][:, 0:512], hqT[:, c, :],
                         wdqT[:, c, mh * 512:(mh + 1) * 512], start=(c == 0), stop=(c == 1))
            b.wait("vector", "pe")
            b.tt(A2_16[:, i, 0:512], A2_16[:, i, 0:512], ps[3][:, 0:512], ALU.add)
            b.tt(A2_16[:, i, 512:1024], A2_16[:, i, 512:1024], ps[4][:, 0:512], ALU.add)
            ve_add[i] = b.n["ve"]

        stage_a(0)
        stage_b(0)
        for i in range(1, NT):
            stage_a(i)
            stage_c(i - 1)
            stage_b(i)
        stage_c(NT - 1)
        b.barrier()
        if limit is not None and limit == f"n{nb}":
            return

    for i in range(NT):
        b.tt(stream[:, i, :], A2_16[:, i, :], stream[:, i, :], ALU.add)
        b.red(C["stats"][:, i:i + 1], stream[:, i, :], op="sum")
    postln_finish(b, C, stream, WK[:, 0:1024], WK[:, 1024:2048], eps=1e-5)
    for i in range(NT):
        b.dma("gpsimd", scr_x2[i * 128:(i + 1) * 128, :], stream[:, i, :], "x2w")
    b.barrier()


def emit_s3(nc, b, C, dr, ps, A1, A2, WA, WK, stream, A2_16,
            scr_x2, scr_qT, scr_kT, scr_vhat, scr_sgate, scr_osc, limit=None):
    ident = C["ident"][:]
    stats = C["stats"]
    hT = A2[:].rearrange("p (c t) -> p c t", c=8)

    g1bc = WK[:, 0:1024]
    b1bc = WK[:, 1024:2048]
    rmbc = WK[:, 2048:3072]
    ztmp = WK[:, 3072:4096]
    sqd = WK[:, 4096:5120]
    b.dma("sync", g1bc, dr["bc_ln1g"], "bc0")
    b.dma("sync", b1bc, dr["bc_ln1b"], "bc1")
    b.dma("sync", rmbc, dr["bc_rmsin"], "bc2")
    zsum, ssq = stats[:, 0:16], stats[:, 16:32]
    nmean, rstd = stats[:, 32:48], stats[:, 48:64]
    for i in range(NT):
        b.red(zsum[:, i:i + 1], stream[:, i, :], op="sum")
    vgaps(b, C)
    b.ts(nmean, zsum, -1.0 / 1024.0, None, ALU.mult)
    b.wait("scalar", "ve")
    for i in range(NT):
        b.act(ztmp, stream[:, i, :], AF.Identity, bias=nmean[:, i:i + 1])
        b.cp(stream[:, i, :], ztmp, eng="scalar")
        b.wait("vector", "se")
        b.stt(sqd, stream[:, i, :], 1.0, stream[:, i, :], ALU.mult, ALU.mult,
              accum=ssq[:, i:i + 1])
    vgaps(b, C)
    rstd_chain(b, C, ssq, rstd, 1024, 1e-5)
    b.wait("scalar", "ve")
    b.wait_dma("vector", "bc0")
    b.wait_dma("vector", "bc1")
    b.wait_dma("vector", "bc2")
    ssq2, rstd2 = stats[:, 0:16], stats[:, 16:32]
    for i in range(NT):
        b.wait("scalar", "ve")
        b.act(ztmp, stream[:, i, :], AF.Identity, scale=rstd[:, i:i + 1])
        b.wait("vector", "se")
        b.tt(ztmp, ztmp, g1bc, ALU.mult)
        b.tt(ztmp, ztmp, b1bc, ALU.add)
        b.cp(stream[:, i, :], ztmp)
        b.stt(sqd, stream[:, i, :], 1.0, stream[:, i, :], ALU.mult, ALU.mult,
              accum=ssq2[:, i:i + 1])
    vgaps(b, C)
    rstd_chain(b, C, ssq2, rstd2, 1024, 1e-6)
    b.wait("scalar", "ve")
    for i in range(NT):
        b.wait("scalar", "ve")
        b.wait("scalar", "pe")
        b.act(ztmp, stream[:, i, :], AF.Identity, scale=rstd2[:, i:i + 1])
        b.wait("vector", "se")
        b.tt(ztmp, ztmp, rmbc, ALU.mult)
        b.wait("tensor", "ve")
        items = [(ztmp[:, c * 128:(c + 1) * 128], hT[:, c, i * 128:(i + 1) * 128])
                 for c in range(8)]
        tp_batch(b, C, ps, items)
    b.barrier()
    if limit == "g0":
        for i in range(NT):
            b.cp(stream[:, i, :], A2_16[:, i, :])
        b.barrier()
        return

    gb = C["gb"]
    sp1 = WK[:, 0:64].rearrange("p (a f) -> p a f", a=8)
    for i in range(NT):
        b.wait("tensor", "ve")
        for c in range(8):
            b.mm(ps[i % 2][:, 0:8], hT[:, c, i * 128:(i + 1) * 128], C["wab"][:, c, :],
                 start=(c == 0), stop=(c == 7))
        b.wait("vector", "pe")
        b.wait("vector", "se")
        b.tt(sp1[:, 0, 0:4], ps[i % 2][:, 0:4], C["dtb"][:], ALU.add)
        b.wait("scalar", "ve")
        b.act(sp1[:, 1, 0:4], sp1[:, 0, 0:4], AF.Abs)
        b.act(sp1[:, 2, 0:4], sp1[:, 1, 0:4], AF.Exp, scale=-1.0)
        b.act(sp1[:, 3, 0:4], sp1[:, 2, 0:4], AF.Ln, bias=1.0)
        b.act(sp1[:, 4, 0:4], sp1[:, 0, 0:4], AF.Relu)
        b.act(gb[:, i, 4:8], ps[i % 2][:, 4:8], AF.Sigmoid)
        b.wait("vector", "se")
        b.tt(sp1[:, 5, 0:4], sp1[:, 3, 0:4], sp1[:, 4, 0:4], ALU.add)
        b.tt(gb[:, i, 0:4], sp1[:, 5, 0:4], C["coef"][:], ALU.mult)
    b.barrier()

    vpre = WK[:, 0:1048].rearrange("p (c f) -> p c f", c=8)
    cvo = WK[:, 1056:2080].rearrange("p (c f) -> p c f", c=8)
    cvs = WK[:, 2080:3104].rearrange("p (c f) -> p c f", c=8)
    slab = WK[:, 3104:4128]
    wv_ = WA[:, 0:8192].rearrange("p (c f) -> p c f", c=8)
    for (wname, cw, scr, do_conv) in (("l1_wv", "convv", scr_vhat, True),
                                      ("l1_wg", None, scr_sgate, False)):
        for half in range(2):
            b.barrier()
            b.dma("sync", wv_[:], dr[wname][:, half * 1024:(half + 1) * 1024]
                  .rearrange("(c p) f -> p c f", p=128), "wbig")
            b.wait_dma("tensor", "wbig")
            for i in range(NT):
                if i >= 1:
                    if do_conv:
                        b.wait_dma("vector", "sw", None)
                    else:
                        b.wait_dma("scalar", "sw", None)
                if do_conv:
                    b.wait("tensor", "ve")
                    for m in range(8):
                        pp = ps[m % 2]
                        if m >= 2:
                            b.wait("tensor", "ve")
                        if i == 0:
                            for c in range(8):
                                b.mm(pp[:, 0:128], wv_[:, c, m * 128:(m + 1) * 128],
                                     hT[:, c, 0:128], start=(c == 0), stop=(c == 7))
                        else:
                            for c in range(8):
                                b.mm(pp[:, 0:131], wv_[:, c, m * 128:(m + 1) * 128],
                                     hT[:, c, i * 128 - 3:(i + 1) * 128],
                                     start=(c == 0), stop=(c == 7))
                        b.wait("vector", "pe")
                        if i == 0:
                            b.memset(vpre[:, m, 0:3], 0.0)
                            b.cp(vpre[:, m, 3:131], pp[:, 0:128], drain=False)
                        else:
                            b.cp(vpre[:, m, :], pp[:, 0:131], drain=False)
                    cwt = C[cw]
                    hc0 = half * 8
                    for m in range(8):
                        b.ts(cvo[:, m, :], vpre[:, m, 0:128], cwt[:, hc0 + m, 0:1], None,
                             ALU.mult)
                        for j in range(1, 4):
                            b.stt(cvo[:, m, :], vpre[:, m, j:j + 128],
                                  cwt[:, hc0 + m, j:j + 1], cvo[:, m, :], ALU.mult, ALU.add)
                    b.wait("scalar", "ve")
                    b.wait("scalar", "pe")
                    b.act(cvs[:, :, :], cvo[:, :, :], AF.Silu)
                    b.wait("tensor", "se")
                    items = [(cvs[:, m, :], slab[:, m * 128:(m + 1) * 128]) for m in range(8)]
                    tp_batch(b, C, ps, items, banks=(2, 3, 4, 5))
                else:
                    b.wait("tensor", "se")
                    for nh in range(2):
                        pp = ps[nh]
                        for c in range(8):
                            b.mm(pp[:, 0:512], hT[:, c, i * 128:(i + 1) * 128],
                                 wv_[:, c, nh * 512:(nh + 1) * 512],
                                 start=(c == 0), stop=(c == 7))
                    b.wait("scalar", "pe")
                    b.act(slab[:, 0:512], ps[0][:, 0:512], AF.Silu)
                    b.act(slab[:, 512:1024], ps[1][:, 0:512], AF.Silu)
                ch = "ve" if do_conv else "se"
                b.wait("gpsimd", ch)
                b.dma("gpsimd", scr[i * 128:(i + 1) * 128, half * 1024:(half + 1) * 1024],
                      slab, "sw")
        b.barrier()
    if limit == "g2ab":
        return

    qpre = WK[:, 0:1048].rearrange("p (c f) -> p c f", c=8)
    qc = WK[:, 1056:2080].rearrange("p (c f) -> p c f", c=8)
    qs = WK[:, 2080:3104].rearrange("p (c f) -> p c f", c=8)
    qq = WK[:, 3104:4128].rearrange("p (c f) -> p c f", c=8)
    rstdT = WK[:, 4128:4256]
    bcr = WK[:, 4256:4384]
    for (wname, cw, scr, extra) in (("l1_wq", "convq", scr_qT, 0.0625),
                                    ("l1_wk", "convk", scr_kT, None)):
        b.barrier()
        b.dma("sync", wv_[:], dr[wname][:].rearrange("(c p) f -> p c f", p=128), "wbig")
        b.wait_dma("tensor", "wbig")
        for i in range(NT):
            b.wait("tensor", "ve")
            for m in range(8):
                pp = ps[m % 2]
                if m >= 2:
                    b.wait("tensor", "ve")
                if i == 0:
                    for c in range(8):
                        b.mm(pp[:, 0:128], wv_[:, c, m * 128:(m + 1) * 128],
                             hT[:, c, 0:128], start=(c == 0), stop=(c == 7))
                else:
                    for c in range(8):
                        b.mm(pp[:, 0:131], wv_[:, c, m * 128:(m + 1) * 128],
                             hT[:, c, i * 128 - 3:(i + 1) * 128],
                             start=(c == 0), stop=(c == 7))
                b.wait("vector", "pe")
                if i == 0:
                    b.memset(qpre[:, m, 0:3], 0.0)
                    b.cp(qpre[:, m, 3:131], pp[:, 0:128], drain=False)
                else:
                    b.cp(qpre[:, m, :], pp[:, 0:131], drain=False)
            cwt = C[cw]
            for m in range(8):
                b.ts(qc[:, m, :], qpre[:, m, 0:128], cwt[:, m, 0:1], None, ALU.mult)
                for j in range(1, 4):
                    b.stt(qc[:, m, :], qpre[:, m, j:j + 128], cwt[:, m, j:j + 1],
                          qc[:, m, :], ALU.mult, ALU.add)
            b.wait("scalar", "ve")
            b.wait("scalar", "pe")
            if i >= 1:
                b.wait_dma("scalar", f"qw{(i - 1) % 2}", None)
                b.wait_dma("vector", f"qw{(i - 1) % 2}", None)
            b.act(qs[:, :, :], qc[:, :, :], AF.Silu)
            b.act(qq[:, :, :], qs[:, :, :], AF.Square)
            b.wait("tensor", "se")
            for h in range(4):
                b.mm(ps[2][:, h:h + 1], qq[:, 2 * h, :], C["ones_col"][:],
                     start=True, stop=False)
                b.mm(ps[2][:, h:h + 1], qq[:, 2 * h + 1, :], C["ones_col"][:],
                     start=False, stop=True)
            b.wait("vector", "pe")
            b.cp(stats[:, 0:4], ps[2][:, 0:4])
            vgaps(b, C)
            rstd_chain(b, C, stats[:, 0:4], stats[:, 4:8], 1, 1e-6,
                       extra_scale=extra, is_sum=True)
            for h in range(4):
                bcast_col(b, C, ps, stats[:, 4 + h:5 + h], bcr, bank=3)
                b.tt(qs[:, 2 * h, :], qs[:, 2 * h, :], bcr, ALU.mult)
                b.tt(qs[:, 2 * h + 1, :], qs[:, 2 * h + 1, :], bcr, ALU.mult)
            b.wait("gpsimd", "ve")
            if i >= 2:
                b.wait_dma("gpsimd", f"qw{i % 2}", None)
            b.dma("gpsimd", scr[i].rearrange("p (c f) -> p c f", c=8),
                  qs[:, :, :], f"qw{i % 2}")
        b.barrier()
    if limit == "g2pre":
        return

    SS = A1[:, 0:4096].rearrange("p (c f) -> p c f", c=8)
    qTc = A1[:, 4096:5120].rearrange("p (c f) -> p c f", c=8)
    kTc = A1[:, 5120:6144].rearrange("p (c f) -> p c f", c=8)
    ktok = A1[:, 6144:7168].rearrange("p (c f) -> p c f", c=8)
    kdec = A1[:, 7168:8192].rearrange("p (c f) -> p c f", c=8)
    gb2 = A1[:, 8192:8200]
    sm = A1[:, 8200:8232]
    negG4 = sm[:, 0:4]
    eG4 = sm[:, 4:8]
    ebg4 = sm[:, 8:12]
    gcm = sm[:, 12:16]
    egd = sm[:, 16:20]
    eGC = sm[:, 20:24]
    obt = A1[:, 8232:8744]
    ob4 = A1[:, 8744:10792].rearrange("p (c f) -> p c f", c=4)
    vc = A1[:, 10792:12840]
    oscb = A1[:, 12840:14888]
    A2f = A2[:]
    Gd4 = A2f[:, 0:512].rearrange("p (c f) -> p c f", c=4)
    nGd4 = A2f[:, 512:1024].rearrange("p (c f) -> p c f", c=4)
    DqT4 = A2f[:, 1024:1536].rearrange("p (c f) -> p c f", c=4)
    Dk4 = A2f[:, 1536:2048].rearrange("p (c f) -> p c f", c=4)
    Aqk4 = A2f[:, 2048:2560].rearrange("p (c f) -> p c f", c=4)
    NT4 = A2f[:, 2560:3072].rearrange("p (c f) -> p c f", c=4)
    N4 = A2f[:, 3072:3584].rearrange("p (c f) -> p c f", c=4)
    X4 = A2f[:, 3584:4096].rearrange("p (c f) -> p c f", c=4)
    XT4 = A2f[:, 4096:4608].rearrange("p (c f) -> p c f", c=4)
    TT4 = A2f[:, 4608:5120].rearrange("p (c f) -> p c f", c=4)
    qg4 = A2f[:, 5120:6144].rearrange("p (c f) -> p c f", c=8)
    Ub4 = A2f[:, 6144:8192].rearrange("p (c f) -> p c f", c=4)
    bV4 = A2f[:, 8192:10240].rearrange("p (c f) -> p c f", c=4)
    kd24 = A2f[:, 10240:11264].rearrange("p (c f) -> p c f", c=8)
    Gd4f, nGd4f = A2f[:, 0:512], A2f[:, 512:1024]
    DqT4f, Dk4f = A2f[:, 1024:1536], A2f[:, 1536:2048]
    Aqk4f, NT4f, N4f = A2f[:, 2048:2560], A2f[:, 2560:3072], A2f[:, 3072:3584]
    X4f, XT4f, TT4f = A2f[:, 3584:4096], A2f[:, 4096:4608], A2f[:, 4608:5120]
    msk4 = {}
    msk4f = {}
    for nm, off in (("mask_ui", 11264), ("mask_us", 11776), ("mask_ls", 12288)):
        msk4[nm] = A2f[:, off:off + 512].rearrange("p (c f) -> p c f", c=4)
        msk4f[nm] = A2f[:, off:off + 512]

    b.memset(SS[:, :, :], 0.0)
    onb = WK[:, 0:512]
    sq2 = WK[:, 512:1024]
    b.dma("sync", onb, dr["bc_onorm"], "onw")
    for nm in ("mask_ui", "mask_us", "mask_ls"):
        for h in range(4):
            b.cp(msk4[nm][:, h, :], C[nm][:], drain=False)
    b.barrier()
    for c in range(NT):
        b.wait("sync", "pe")
        b.wait("sync", "ve")
        b.dma("sync", qTc[:, :, :], scr_qT[c].rearrange("p (a f) -> p a f", a=8), "qrd")
        b.dma("sync", kTc[:, :, :], scr_kT[c].rearrange("p (a f) -> p a f", a=8), "krd")
        b.dma("sync", vc, scr_vhat[c * 128:(c + 1) * 128, :], "vrd")
        b.dma("sync", oscb, scr_sgate[c * 128:(c + 1) * 128, :], "grd")
        b.wait("tensor", "ve")
        b.wait("tensor", "se")
        b.mm(ps[7][:, 0:4], C["ones_ui"][:], gb[:, c, 0:4], start=True, stop=True)
        b.wait("vector", "pe")
        b.cp(gb2[:, 0:4], ps[7][:, 0:4])
        b.cp(gb2[:, 4:8], gb[:, c, 4:8], drain=False)
        vgaps(b, C)
        b.ts(negG4, gb2[:, 0:4], -1.0, None, ALU.mult)
        b.wait("scalar", "ve")
        b.act(eG4, gb2[:, 0:4], AF.Exp)
        b.wait("vector", "se")
        b.tt(ebg4, eG4, gb2[:, 4:8], ALU.mult)
        b.wait_dma("tensor", "qrd")
        b.wait_dma("tensor", "krd")
        for h in range(4):
            bcast_col(b, C, ps, gb2[:, h:h + 1], Gd4[:, h, :], bank=7)
        b.wait("vector", "ve")
        for h in range(4):
            b.ts(Gd4[:, h, :], Gd4[:, h, :], negG4[:, h:h + 1], None, ALU.add, drain=False)
        for h in range(4):
            b.mm(ps[0][:, h * 128:(h + 1) * 128], kTc[:, 2 * h, :], qTc[:, 2 * h, :],
                 start=True, stop=False)
            b.mm(ps[0][:, h * 128:(h + 1) * 128], kTc[:, 2 * h + 1, :], qTc[:, 2 * h + 1, :],
                 start=False, stop=True)
        for h in range(4):
            b.mm(ps[1][:, h * 128:(h + 1) * 128], kTc[:, 2 * h, :], kTc[:, 2 * h, :],
                 start=True, stop=False)
            b.mm(ps[1][:, h * 128:(h + 1) * 128], kTc[:, 2 * h + 1, :], kTc[:, 2 * h + 1, :],
                 start=False, stop=True)
        b.wait("tensor", "ve")
        tp_batch(b, C, ps, [(kTc[:, a, :], ktok[:, a, :]) for a in range(8)],
                 banks=(2, 3, 4, 5))
        b.ts(nGd4f, Gd4f, -1.0, None, ALU.mult)
        b.tt(DqT4f, Gd4f, msk4f["mask_ui"], ALU.add)
        b.tt(Dk4f, Gd4f, msk4f["mask_us"], ALU.add)
        b.wait("scalar", "ve")
        b.act(DqT4f, DqT4f, AF.Exp)
        b.act(Dk4f, Dk4f, AF.Exp)
        b.wait("vector", "se")
        b.wait("vector", "pe")
        b.tt(Aqk4f, ps[0][:, 0:512], DqT4f, ALU.mult)
        b.tt(NT4f, ps[1][:, 0:512], Dk4f, ALU.mult)
        for h in range(4):
            bcast_col(b, C, ps, gb2[:, 4 + h:5 + h], Dk4[:, h, :], bank=7)
        b.wait("vector", "ve")
        b.ts(Dk4f, Dk4f, -1.0, None, ALU.mult, drain=False)
        b.tt(NT4f, NT4f, Dk4f, ALU.mult)
        b.tt(N4f, nGd4f, msk4f["mask_ls"], ALU.add)
        b.wait("scalar", "ve")
        b.act(N4f, N4f, AF.Exp)
        b.wait("vector", "se")
        b.tt(N4f, N4f, ps[1][:, 0:512], ALU.mult)
        for h in range(4):
            b.ts(N4[:, h, :], N4[:, h, :], gb2[:, 4 + h:5 + h], -1.0, ALU.mult, ALU.mult,
                 drain=(h == 0))
        for h in range(4):
            b.tt(TT4[:, h, :], NT4[:, h, :], ident, ALU.add, drain=False)
        b.cp(X4f, N4f)
        b.cp(XT4f, NT4f)
        for lev in range(6):
            b.wait("tensor", "ve")
            for h in range(4):
                b.mm(ps[2][:, h * 128:(h + 1) * 128], XT4[:, h, :], X4[:, h, :],
                     start=True, stop=True)
                b.mm(ps[3][:, h * 128:(h + 1) * 128], X4[:, h, :], XT4[:, h, :],
                     start=True, stop=True)
            b.wait("vector", "pe")
            b.cp(X4f, ps[2][:, 0:512])
            b.cp(XT4f, ps[3][:, 0:512])
            b.wait("tensor", "ve")
            for h in range(4):
                b.mm(ps[4][:, h * 128:(h + 1) * 128], X4[:, h, :], TT4[:, h, :],
                     start=True, stop=True)
            b.wait("vector", "pe")
            b.tt(TT4f, TT4f, ps[4][:, 0:512], ALU.add)
        b.wait_dma("vector", "vrd")
        for h in range(4):
            b.ts(bV4[:, h, :], vc[:, h * 512:(h + 1) * 512], gb2[:, 4 + h:5 + h],
                 None, ALU.mult, drain=(h == 0))
        b.wait("tensor", "ve")
        for h in range(4):
            b.mm(ps[4 + h % 2][:, 0:512], TT4[:, h, :], bV4[:, h, :],
                 start=True, stop=True)
            if h % 2 == 1:
                b.wait("vector", "pe")
                b.cp(Ub4[:, h - 1, :], ps[4][:, 0:512])
                b.cp(Ub4[:, h, :], ps[5][:, 0:512])
                if h == 1:
                    b.wait("tensor", "ve")
        for h in range(4):
            b.ts(kdec[:, 2 * h, :], ktok[:, 2 * h, :], ebg4[:, h:h + 1], None, ALU.mult,
                 drain=(h == 0))
            b.ts(kdec[:, 2 * h + 1, :], ktok[:, 2 * h + 1, :], ebg4[:, h:h + 1],
                 None, ALU.mult, drain=False)
        b.wait("tensor", "ve")
        for h in range(4):
            b.mm(ps[6][:, h * 128:(h + 1) * 128], kdec[:, 2 * h, :], TT4[:, h, :],
                 start=True, stop=True)
            b.mm(ps[7][:, h * 128:(h + 1) * 128], kdec[:, 2 * h + 1, :], TT4[:, h, :],
                 start=True, stop=True)
        b.wait("vector", "pe")
        b.cp(qg4[:, 0:4, :], ps[6][:, 0:512])
        b.cp(qg4[:, 4:8, :], ps[7][:, 0:512])
        if c > 0:
            b.wait("tensor", "ve")
            for h in range(4):
                b.mm(ps[4 + h % 2][:, 0:512], qg4[:, h, :], SS[:, 2 * h, :],
                     start=True, stop=False)
                b.mm(ps[4 + h % 2][:, 0:512], qg4[:, 4 + h, :], SS[:, 2 * h + 1, :],
                     start=False, stop=True)
                if h % 2 == 1:
                    b.wait("vector", "pe")
                    b.tt(Ub4[:, h - 1, :], Ub4[:, h - 1, :], ps[4][:, 0:512], ALU.subtract)
                    b.tt(Ub4[:, h, :], Ub4[:, h, :], ps[5][:, 0:512], ALU.subtract)
                    if h == 1:
                        b.wait("tensor", "ve")
        b.wait("tensor", "ve")
        for h in range(4):
            b.mm(ps[4 + h % 2][:, 0:512], Aqk4[:, h, :], Ub4[:, h, :],
                 start=True, stop=True)
            if c > 0:
                b.mm(ps[2 + h % 2][:, 0:512], qTc[:, 2 * h, :], SS[:, 2 * h, :],
                     start=True, stop=False)
                b.mm(ps[2 + h % 2][:, 0:512], qTc[:, 2 * h + 1, :], SS[:, 2 * h + 1, :],
                     start=False, stop=True)
            if h % 2 == 1:
                b.wait("vector", "pe")
                for hh in (h - 1, h):
                    if c > 0:
                        b.cp(obt, ps[4 + hh % 2][:, 0:512], drain=True)
                        b.stt(ob4[:, hh, :], ps[2 + hh % 2][:, 0:512], eG4[:, hh:hh + 1],
                              obt, ALU.mult, ALU.add)
                    else:
                        b.cp(ob4[:, hh, :], ps[4 + hh % 2][:, 0:512], drain=False)
                if h == 1:
                    b.wait("tensor", "ve")
        b.wait("vector", "pe")
        b.wait("vector", "se")
        for h in range(4):
            b.cp(gcm[:, h:h + 1], Gd4[:, h, 127:128], drain=(h == 0))
        b.wait("scalar", "ve")
        b.act(egd, gcm, AF.Exp)
        b.wait("vector", "se")
        b.tt(eGC, gb2[:, 0:4], gcm, ALU.add)
        vgaps(b, C)
        b.wait("scalar", "ve")
        b.act(eGC, eGC, AF.Exp)
        b.wait("vector", "se")
        for h in range(4):
            b.ts(kd24[:, 2 * h, :], ktok[:, 2 * h, :], egd[:, h:h + 1], None, ALU.mult,
                 drain=False)
            b.ts(kd24[:, 2 * h + 1, :], ktok[:, 2 * h + 1, :], egd[:, h:h + 1],
                 None, ALU.mult, drain=False)
        b.wait("tensor", "ve")
        for h in range(4):
            b.mm(ps[4 + h % 2][:, 0:512], kd24[:, 2 * h, :], Ub4[:, h, :],
                 start=True, stop=True)
            b.mm(ps[2 + h % 2][:, 0:512], kd24[:, 2 * h + 1, :], Ub4[:, h, :],
                 start=True, stop=True)
            if h % 2 == 1:
                b.wait("vector", "pe")
                for hh in (h - 1, h):
                    if c == 0:
                        b.cp(SS[:, 2 * hh, :], ps[4 + hh % 2][:, 0:512], drain=False)
                        b.cp(SS[:, 2 * hh + 1, :], ps[2 + hh % 2][:, 0:512], drain=False)
                    else:
                        b.stt(SS[:, 2 * hh, :], SS[:, 2 * hh, :], eGC[:, hh:hh + 1],
                              ps[4 + hh % 2][:, 0:512], ALU.mult, ALU.add)
                        b.stt(SS[:, 2 * hh + 1, :], SS[:, 2 * hh + 1, :], eGC[:, hh:hh + 1],
                              ps[2 + hh % 2][:, 0:512], ALU.mult, ALU.add)
                if h == 1:
                    b.wait("tensor", "ve")
        b.wait_dma("vector", "grd")
        for h in range(4):
            b.stt(sq2, ob4[:, h, :], 1.0, ob4[:, h, :], ALU.mult, ALU.mult,
                  accum=stats[:, h:h + 1])
        vgaps(b, C)
        rstd_chain(b, C, stats[:, 0:4], stats[:, 4:8], 512, 1e-6)
        b.wait("scalar", "ve")
        b.wait_dma("vector", "onw")
        for h in range(4):
            b.wait("scalar", "ve")
            b.act(sq2, ob4[:, h, :], AF.Identity, scale=stats[:, 4 + h:5 + h])
            b.wait("vector", "se")
            b.tt(sq2, sq2, onb, ALU.mult)
            b.tt(oscb[:, h * 512:(h + 1) * 512], oscb[:, h * 512:(h + 1) * 512],
                 sq2, ALU.mult)
        b.wait("gpsimd", "ve")
        if c >= 1:
            b.wait_dma("gpsimd", "ow", None)
        b.dma("gpsimd", scr_osc[c * 128:(c + 1) * 128, :], oscb, "ow")
        b.wait_dma("sync", "ow")
    b.barrier()
    if limit == "g2c":
        return

    oT = WK[:, 0:2048].rearrange("p (c f) -> p c f", c=16)
    xtk = WK[:, 2048:4096].rearrange("p (c f) -> p c f", c=2)
    osb = WK[:, 4096:6144]
    xa16 = A2_16
    for half in range(2):
        b.barrier()
        b.dma("sync", wv_[:], dr["l1_wo"][half * 1024:(half + 1) * 1024, :]
              .rearrange("(c p) f -> p c f", p=128), "wbig")
        b.wait_dma("tensor", "wbig")
        for i in range(NT):
            if i >= 1:
                b.wait("sync", "pe")
                b.wait("sync", "ve")
            b.dma("sync", osb, scr_osc[i * 128:(i + 1) * 128, :], "osr")
            b.dma("sync", xtk[:, i % 2, :], scr_x2[i * 128:(i + 1) * 128, :], f"x2r{i % 2}")
            b.wait_dma("tensor", "osr")
            b.wait("tensor", "ve")
            items = [(osb[:, m * 128:(m + 1) * 128], oT[:, m, :]) for m in range(16)]
            tp_batch(b, C, ps, items, banks=(2, 3, 4, 5))
            b.wait("tensor", "ve")
            for mh in range(2):
                pp = ps[mh]
                for cch in range(8):
                    b.mm(pp[:, 0:512], oT[:, half * 8 + cch, :],
                         wv_[:, cch, mh * 512:(mh + 1) * 512],
                         start=(cch == 0), stop=(cch == 7))
            b.wait("vector", "pe")
            b.wait_dma("vector", f"x2r{i % 2}")
            if half == 0:
                b.tt(xa16[:, i, 0:512], ps[0][:, 0:512], xtk[:, i % 2, 0:512], ALU.add)
                b.tt(xa16[:, i, 512:1024], ps[1][:, 0:512], xtk[:, i % 2, 512:1024], ALU.add)
            else:
                b.tt(xa16[:, i, 0:512], ps[0][:, 0:512], xa16[:, i, 0:512], ALU.add)
                b.tt(xa16[:, i, 512:1024], ps[1][:, 0:512], xa16[:, i, 512:1024], ALU.add)
    b.barrier()
    g2bc = WK[:, 0:1024]
    b2bc2 = WK[:, 1024:2048]
    bb2 = WK[:, 2048:3072]
    h2t = WK[:, 3072:4096]
    sqd3 = WK[:, 4096:5120]
    h2T = WK[:, 5120:6144].rearrange("p (c f) -> p c f", c=8)
    xtk2 = WK[:, 6144:8192].rearrange("p (c f) -> p c f", c=2)
    b.dma("sync", g2bc, dr["bc_ln2g"], "bc0")
    b.dma("sync", b2bc2, dr["bc_ln2b"], "bc1")
    b.dma("sync", bb2, dr["bc_b2"], "bc2")
    b.dma("sync", WA[:, 0:8192].rearrange("p (c f) -> p c f", c=8)[:],
          dr["l1_w2"][:].rearrange("(c p) f -> p c f", p=128), "wbig")
    w2_ = WA[:, 0:8192].rearrange("p (c f) -> p c f", c=8)
    for i in range(NT):
        b.red(zsum[:, i:i + 1], xa16[:, i, :], op="sum")
    vgaps(b, C)
    b.ts(nmean, zsum, -1.0 / 1024.0, None, ALU.mult)
    b.wait("scalar", "ve")
    for i in range(NT):
        b.wait("scalar", "ve")
        b.act(h2t, xa16[:, i, :], AF.Identity, bias=nmean[:, i:i + 1])
        b.wait("vector", "se")
        b.stt(sqd3, h2t, 1.0, h2t, ALU.mult, ALU.mult, accum=ssq[:, i:i + 1])
        b.cp(stream[:, i, :], h2t)
    vgaps(b, C)
    rstd_chain(b, C, ssq, rstd, 1024, 1e-5)
    b.wait("scalar", "ve")
    b.wait_dma("vector", "bc0")
    b.wait_dma("vector", "bc1")
    b.wait_dma("tensor", "wbig")
    for i in range(NT):
        b.wait("scalar", "ve")
        b.wait("scalar", "pe")
        b.act(h2t, stream[:, i, :], AF.Identity, scale=rstd[:, i:i + 1])
        b.wait("vector", "se")
        b.tt(h2t, h2t, g2bc, ALU.mult)
        b.tt(h2t, h2t, b2bc2, ALU.add)
        b.wait("tensor", "ve")
        items = [(h2t[:, cch * 128:(cch + 1) * 128], h2T[:, cch, :]) for cch in range(8)]
        tp_batch(b, C, ps, items, banks=(2, 3, 4, 5))
        b.wait("tensor", "ve")
        for mh in range(2):
            pp = ps[mh]
            for cch in range(8):
                b.mm(pp[:, 0:512], h2T[:, cch, :], w2_[:, cch, mh * 512:(mh + 1) * 512],
                     start=(cch == 0), stop=(cch == 7))
        if i >= 2:
            b.wait("sync", "ve")
        b.dma("sync", xtk2[:, i % 2, :], scr_x2[i * 128:(i + 1) * 128, :], f"x2s{i % 2}")
        b.wait("scalar", "pe")
        b.wait("vector", "pe")
        b.wait_dma("vector", "bc2")
        b.tt(h2t[:, 0:512], ps[0][:, 0:512], bb2[:, 0:512], ALU.add)
        b.tt(h2t[:, 512:1024], ps[1][:, 0:512], bb2[:, 512:1024], ALU.add)
        b.wait("scalar", "ve")
        b.act(h2t, h2t, AF.Gelu)
        b.wait("vector", "se")
        b.wait_dma("vector", f"x2s{i % 2}")
        b.tt(h2t, h2t, xa16[:, i, :], ALU.add)
        b.tt(stream[:, i, :], h2t, xtk2[:, i % 2, :], ALU.add)
        b.red(C["stats"][:, i:i + 1], stream[:, i, :], op="sum")
    postln_finish(b, C, stream, WK[:, 0:1024], WK[:, 1024:2048], eps=1e-5)


SPARSE_MOE = True


def emit_s4(nc, b, C, dr, ps, A1, A2, WA, WK, stream, A2_16, scr_x3):
    if SPARSE_MOE:
        return emit_s4_sparse(nc, b, C, dr, ps, A1, A2, WA, WK, stream, A2_16, scr_x3)
    return emit_s4_dense(nc, b, C, dr, ps, A1, A2, WA, WK, stream, A2_16, scr_x3)


def emit_s4_sparse(nc, b, C, dr, ps, A1, A2, WA, WK, stream, A2_16, scr_x3):
    ident = C["ident"][:]
    stats = C["stats"]
    ytok = A2_16
    comb = stream

    mrbc = WK[:, 0:1024]
    ytmp = WK[:, 1024:2048]
    sqd = WK[:, 2048:3072]
    lgb = WK[:, 3072:3080]
    lg2 = WK[:, 3080:3088]
    t1 = WK[:, 3088:3096]
    msk1 = WK[:, 3096:3104]
    msk2 = WK[:, 3104:3112]
    m8 = WK[:, 3112:3120]
    asb = WK[:, 3120:3128]
    offbc = WK[:, 3128:3136]
    ytT = WK[:, 4096:5120].rearrange("p (c f) -> p c f", c=8)
    b.dma("sync", mrbc, dr["bc_moerms"], "bc0")
    ssqm, rstdm = stats[:, 0:16], stats[:, 16:32]
    m1c, m2c, d12 = stats[:, 32:33], stats[:, 33:34], stats[:, 34:35]
    w1c, w2c = stats[:, 35:36], stats[:, 36:37]
    for i in range(NT):
        b.stt(sqd, stream[:, i, :], 1.0, stream[:, i, :], ALU.mult, ALU.mult,
              accum=ssqm[:, i:i + 1])
    vgaps(b, C)
    rstd_chain(b, C, ssqm, rstdm, 1024, 1e-6)
    b.wait("scalar", "ve")
    b.wait_dma("vector", "bc0")
    b.memset(C["offrow"][:], 0.0)
    for i in range(NT):
        b.wait("scalar", "ve")
        b.wait("scalar", "pe")
        b.act(ytmp, stream[:, i, :], AF.Identity, scale=rstdm[:, i:i + 1])
        b.wait("vector", "se")
        b.tt(ytmp, ytmp, mrbc, ALU.mult)
        b.cp(ytok[:, i, :], ytmp)
        b.wait("tensor", "ve")
        items = [(ytmp[:, c * 128:(c + 1) * 128], ytT[:, c, :]) for c in range(8)]
        tp_batch(b, C, ps, items, banks=(0, 1, 2, 3))
        b.wait("tensor", "ve")
        for c in range(8):
            b.mm(ps[6][:, 0:8], ytT[:, c, :], C["gwT"][:, c, :],
                 start=(c == 0), stop=(c == 7))
        b.wait("vector", "pe")
        b.cp(lgb, ps[6][:, 0:8])
        b.red(m1c, lgb, op="max")
        b.tt(msk1, lgb, m1c.broadcast_to([128, 8]), ALU.is_equal)
        b.stt(lg2, msk1, -1e30, lgb, ALU.mult, ALU.add)
        b.red(m2c, lg2, op="max")
        b.tt(msk2, lg2, m2c.broadcast_to([128, 8]), ALU.is_equal)
        b.tt(d12, m1c, m2c, ALU.subtract)
        b.wait("scalar", "ve")
        b.act(w1c, d12, AF.Sigmoid)
        b.act(w2c, d12, AF.Sigmoid, scale=-1.0)
        b.wait("vector", "se")
        b.ts(t1, msk2, w2c, None, ALU.mult)
        b.stt(C["wfull"][:, i, :], msk1, w1c, t1, ALU.mult, ALU.add)
        b.tt(m8, msk1, msk2, ALU.max)
        b.wait("tensor", "ve")
        b.mm(ps[6][:, 8:16], C["ones_ui"][:], m8, start=True, stop=True)
        b.mm(ps[7][:, 0:8], C["ones_r"][:], C["offrow"][:], start=True, stop=True)
        b.wait("vector", "pe")
        b.cp(offbc, ps[7][:, 0:8])
        b.tt(asb, ps[6][:, 8:16], offbc, ALU.add)
        b.tt(t1, asb, m8, ALU.mult)
        b.ts(C["rank"][:, i, :], t1, 1.0, -1.0, ALU.mult, ALU.add)
        b.wait("tensor", "ve")
        b.mm(ps[7][0:1, 8:16], C["e127"][:], asb, start=True, stop=True)
        b.wait("vector", "pe")
        b.cp(C["offrow"][:], ps[7][0:1, 8:16])
    b.barrier()

    for i in range(NT):
        b.dma("gpsimd", scr_x3[i * 128:(i + 1) * 128, :], stream[:, i, :], "x3w")
    b.wait_dma("vector", "x3w")
    for i in range(NT):
        b.memset(comb[:, i, :], 0.0)
    b.barrier()

    NBW = [384] * 7 + [128]
    w1s = WA[:, 0:3072].rearrange("p (c f) -> p c f", c=8)
    w2s = WA[:, 3072:6144].rearrange("p (c f) -> p c f", c=8)
    w3s = WA[:, 6144:9216].rearrange("p (c f) -> p c f", c=3)
    yeT = WK[:, 0:3072].rearrange("p (c f) -> p c f", c=8)
    oute = WK[:, 3072:6144].rearrange("p (c f) -> p c f", c=3)
    ygt = WK[:, 3072:4096]
    Pslab = WK[:, 6144:7168].rearrange("p (c f) -> p c f", c=8)
    tmps = WK[:, 7168:7552]
    hb = WK[:, 7552:7936]
    hTe = WK[:, 7936:8320].rearrange("p (c f) -> p c f", c=3)
    rk8 = WK[:, 8320:8328]
    rkbc = WK[:, 8328:8456]
    wbc = WK[:, 8456:8584]
    atile = Pslab

    for e in range(8):
        for half in range(2):
            for rcl in range(3):
                rc = 3 * half + rcl
                for g in range(2):
                    b.wait("vector", "pe")
                    for j in range(8):
                        tc = g * 8 + j
                        b.ts(rk8[:, j:j + 1], C["rank"][:, tc, e:e + 1], 1.0,
                             -128.0 * rc, ALU.mult, ALU.add)
                    vgaps(b, C)
                    for j in range(8):
                        b.ts(Pslab[:, j, :], C["iota_f"][:], rk8[:, j:j + 1], None,
                             ALU.is_equal)
                    b.wait("tensor", "ve")
                    for j in range(8):
                        tc = g * 8 + j
                        for mh in range(2):
                            b.mm(ps[mh][:, 0:512], Pslab[:, j, :],
                                 ytok[:, tc, mh * 512:(mh + 1) * 512],
                                 start=(g == 0 and j == 0), stop=(g == 1 and j == 7))
                b.wait("vector", "pe")
                b.cp(ygt[:, 0:512], ps[0][:, 0:512])
                b.cp(ygt[:, 512:1024], ps[1][:, 0:512])
                b.wait("tensor", "ve")
                items = [(ygt[:, c * 128:(c + 1) * 128], yeT[:, c, rcl * 128:(rcl + 1) * 128])
                         for c in range(8)]
                tp_batch(b, C, ps, items, banks=(2, 3, 4, 5))
            b.barrier()
            wslot = [WA[:, 0:3072].rearrange("p (c f) -> p c f", c=8),
                     WA[:, 3072:6144].rearrange("p (c f) -> p c f", c=8)]
            wslot3 = [WA[:, 0:3072].rearrange("p (c f) -> p c f", c=3),
                      WA[:, 3072:6144].rearrange("p (c f) -> p c f", c=3)]
            upb = WA[:, 6144:7296].rearrange("p (c f) -> p c f", c=3)
            hb3 = WA[:, 7296:8448].rearrange("p (c f) -> p c f", c=3)
            loads = []
            for nb in range(8):
                loads += [(1, nb), (2, nb), (3, nb)]
            pe_done = {}

            def issue_load(li):
                kind, nb = loads[li]
                slot = li % 2
                n0 = sum(NBW[:nb])
                NW = NBW[nb]
                nc3 = NW // 128
                q = "sync" if li % 2 == 0 else "gpsimd"
                if li >= 2:
                    b.wait(q, "pe", pe_done.get(li - 2, 0))
                src = dr[("moe_w1i", "moe_w2i", "moe_w3i")[kind - 1]]
                if kind < 3:
                    b.dma(q, wslot[slot][:, :, 0:NW],
                          src[e, nb, :, 0:8 * NW].rearrange("p (c f) -> p c f", c=8),
                          f"wl{slot}")
                else:
                    b.dma(q, wslot3[slot][:, 0:nc3, :],
                          src[e, nb, :, 0:nc3 * 1024].rearrange("p (c f) -> p c f", c=nc3),
                          f"wl{slot}")

            issue_load(0)
            issue_load(1)
            for nb in range(8):
                n0 = sum(NBW[:nb])
                NW = NBW[nb]
                nc3 = NW // 128
                li = nb * 3
                s1_, s2_, s3_ = li % 2, (li + 1) % 2, (li + 2) % 2
                b.wait_dma("tensor", f"wl{s1_}", None)
                for r in range(3):
                    pp = ps[(0, 1, 5)[r]]
                    if nb > 0:
                        b.wait("tensor", "ve")
                    for c in range(8):
                        b.mm(pp[:, 0:NW], yeT[:, c, r * 128:(r + 1) * 128],
                             wslot[s1_][:, c, 0:NW], start=(c == 0), stop=(c == 7))
                    b.wait("vector", "pe")
                    b.cp(upb[:, r, 0:NW], pp[:, 0:NW], drain=False)
                pe_done[li] = b.n["pe"]
                if li + 2 < len(loads):
                    issue_load(li + 2)
                b.wait_dma("tensor", f"wl{s2_}", None)
                for r in range(3):
                    pp = ps[(0, 1, 5)[r]]
                    b.wait("tensor", "ve")
                    for c in range(8):
                        b.mm(pp[:, 0:NW], yeT[:, c, r * 128:(r + 1) * 128],
                             wslot[s2_][:, c, 0:NW], start=(c == 0), stop=(c == 7))
                    b.wait("scalar", "pe")
                    b.wait("scalar", "ve")
                    b.act(tmps[:, 0:NW], upb[:, r, 0:NW], AF.Silu)
                    b.wait("vector", "se")
                    b.tt(hb3[:, r, 0:NW], tmps[:, 0:NW], pp[:, 0:NW], ALU.mult)
                pe_done[li + 1] = b.n["pe"]
                if li + 3 < len(loads):
                    issue_load(li + 3)
                b.wait_dma("tensor", f"wl{s3_}", None)
                b.wait("tensor", "ve")
                for r in range(3):
                    items = [(hb3[:, r, c * 128:(c + 1) * 128],
                              hTe[:, c, :]) for c in range(nc3)]
                    tp_batch(b, C, ps, items, banks=(2, 6))
                    b.wait("tensor", "ve")
                    for mh in range(2):
                        for c in range(nc3):
                            b.mm(ps[3 + mh][:, 0:512], hTe[:, c, :],
                                 wslot3[s3_][:, c, mh * 512:(mh + 1) * 512],
                                 start=(c == 0), stop=(c == nc3 - 1))
                    b.wait("vector", "pe")
                    if nb == 0:
                        b.cp(oute[:, r, 0:512], ps[3][:, 0:512], drain=False)
                        b.cp(oute[:, r, 512:1024], ps[4][:, 0:512], drain=False)
                    else:
                        b.tt(oute[:, r, 0:512], oute[:, r, 0:512], ps[3][:, 0:512], ALU.add)
                        b.tt(oute[:, r, 512:1024], oute[:, r, 512:1024], ps[4][:, 0:512],
                             ALU.add)
                pe_done[li + 2] = b.n["pe"]
            b.barrier()
            rkbc2 = (rkbc, wbc)
            bcast_col(b, C, ps, C["rank"][:, 0, e:e + 1], rkbc2[0], bank=7)
            for tc in range(NT):
                cur = rkbc2[tc % 2]
                b.wait("vector", "pe")
                for rcl in range(3):
                    rc = 3 * half + rcl
                    b.ts(atile[:, 0, :], cur, 1.0, -128.0 * rc, ALU.mult, ALU.add)
                    b.tt(Pslab[:, 4 + rcl, :], C["iota_p"][:], atile[:, 0, :], ALU.is_equal)
                if tc + 1 < NT:
                    bcast_col(b, C, ps, C["rank"][:, tc + 1, e:e + 1], rkbc2[(tc + 1) % 2],
                              bank=7)
                b.wait("tensor", "ve")
                for mh in range(2):
                    for rcl in range(3):
                        b.mm(ps[mh][:, 0:512], Pslab[:, 4 + rcl, :],
                             oute[:, rcl, mh * 512:(mh + 1) * 512],
                             start=(rcl == 0), stop=(rcl == 2))
                b.wait("vector", "pe")
                b.stt(comb[:, tc, 0:512], ps[0][:, 0:512], C["wfull"][:, tc, e:e + 1],
                      comb[:, tc, 0:512], ALU.mult, ALU.add)
                b.stt(comb[:, tc, 512:1024], ps[1][:, 0:512], C["wfull"][:, tc, e:e + 1],
                      comb[:, tc, 512:1024], ALU.mult, ALU.add)
            b.barrier()

    xrd = WK[:, 0:2048].rearrange("p (c f) -> p c f", c=2)
    for i in range(NT):
        if i >= 2:
            b.wait("sync", "ve")
        b.dma("sync", xrd[:, i % 2, :], scr_x3[i * 128:(i + 1) * 128, :], f"x3r{i % 2}")
        b.wait_dma("vector", f"x3r{i % 2}")
        b.tt(stream[:, i, :], stream[:, i, :], xrd[:, i % 2, :], ALU.add)
        b.red(C["stats"][:, i:i + 1], stream[:, i, :], op="sum")
    postln_finish(b, C, stream, WK[:, 2048:3072], WK[:, 3072:4096], eps=1e-5)


def emit_s4_dense(nc, b, C, dr, ps, A1, A2, WA, WK, stream, A2_16, scr_x3):
    ident = C["ident"][:]
    stats = C["stats"]
    yT = A2[:].rearrange("p (c t) -> p c t", c=8)

    mrbc = WK[:, 0:1024]
    ytmp = WK[:, 1024:2048]
    sqd = WK[:, 2048:3072]
    lgb = WK[:, 3072:3080]
    lg2 = WK[:, 3080:3088]
    t1 = WK[:, 3088:3096]
    msk1 = WK[:, 3096:3104]
    msk2 = WK[:, 3104:3112]
    b.dma("sync", mrbc, dr["bc_moerms"], "bc0")
    ssqm, rstdm = stats[:, 0:16], stats[:, 16:32]
    m1c, m2c, d12 = stats[:, 32:33], stats[:, 33:34], stats[:, 34:35]
    w1c, w2c = stats[:, 35:36], stats[:, 36:37]
    for i in range(NT):
        b.stt(sqd, stream[:, i, :], 1.0, stream[:, i, :], ALU.mult, ALU.mult,
              accum=ssqm[:, i:i + 1])
    vgaps(b, C)
    rstd_chain(b, C, ssqm, rstdm, 1024, 1e-6)
    b.wait("scalar", "ve")
    b.wait_dma("vector", "bc0")
    for i in range(NT):
        b.wait("scalar", "ve")
        b.wait("scalar", "pe")
        b.act(ytmp, stream[:, i, :], AF.Identity, scale=rstdm[:, i:i + 1])
        b.wait("vector", "se")
        b.tt(ytmp, ytmp, mrbc, ALU.mult)
        b.wait("tensor", "ve")
        items = [(ytmp[:, c * 128:(c + 1) * 128], yT[:, c, i * 128:(i + 1) * 128])
                 for c in range(8)]
        tp_batch(b, C, ps, items, banks=(0, 1, 2, 3))
        b.wait("tensor", "ve")
        for c in range(8):
            b.mm(ps[6][:, 0:8], yT[:, c, i * 128:(i + 1) * 128], C["gwT"][:, c, :],
                 start=(c == 0), stop=(c == 7))
        b.wait("vector", "pe")
        b.cp(lgb, ps[6][:, 0:8])
        b.red(m1c, lgb, op="max")
        b.tt(msk1, lgb, m1c.broadcast_to([128, 8]), ALU.is_equal)
        b.stt(lg2, msk1, -1e30, lgb, ALU.mult, ALU.add)
        b.red(m2c, lg2, op="max")
        b.tt(msk2, lg2, m2c.broadcast_to([128, 8]), ALU.is_equal)
        b.tt(d12, m1c, m2c, ALU.subtract)
        b.wait("scalar", "ve")
        b.act(w1c, d12, AF.Sigmoid)
        b.act(w2c, d12, AF.Sigmoid, scale=-1.0)
        b.wait("vector", "se")
        b.ts(t1, msk2, w2c, None, ALU.mult)
        b.stt(C["wfull"][:, i, :], msk1, w1c, t1, ALU.mult, ALU.add)
    b.barrier()

    for i in range(NT):
        b.dma("gpsimd", scr_x3[i * 128:(i + 1) * 128, :], stream[:, i, :], "x3w")
    b.wait_dma("vector", "x3w")
    comb = stream
    for i in range(NT):
        b.memset(comb[:, i, :], 0.0)
    b.barrier()

    NBW = [384] * 7 + [128]
    w1s = WA[:, 0:3072].rearrange("p (c f) -> p c f", c=8)
    w2s = WA[:, 3072:6144].rearrange("p (c f) -> p c f", c=8)
    w3s = WA[:, 6144:9216].rearrange("p (c f) -> p c f", c=3)
    tmps = WK[:, 0:384]
    hb = WK[:, 384:768]
    hTe = WK[:, 768:1152].rearrange("p (c f) -> p c f", c=3)
    for e in range(8):
        for nb in range(8):
            n0 = sum(NBW[:nb])
            NW = NBW[nb]
            nc3 = NW // 128
            b.wait("sync", "pe")
            b.dma("sync", w1s[:, :, 0:NW],
                  dr["moe_w1T"][e, :, n0:n0 + NW].rearrange("(c p) f -> p c f", p=128), "we1")
            b.dma("sync", w2s[:, :, 0:NW],
                  dr["moe_w2T"][e, :, n0:n0 + NW].rearrange("(c p) f -> p c f", p=128), "we2")
            b.dma("sync", w3s[:, 0:nc3, :],
                  dr["moe_w3T"][e, n0:n0 + NW, :].rearrange("(c p) f -> p c f", p=128), "we3")
            b.wait_dma("tensor", "we1")
            b.wait_dma("tensor", "we2")
            b.wait_dma("tensor", "we3")
            pe_mm, se_silu, ve_tt, ve_add = {}, {}, {}, {}

            def stage_a(i):
                gb_, ub_ = (0, 1) if i % 2 == 0 else (5, 6)
                if i >= 2:
                    b.wait("tensor", "se", se_silu[i - 2])
                    b.wait("tensor", "ve", ve_tt[i - 2])
                for c in range(8):
                    b.mm(ps[ub_][:, 0:NW], yT[:, c, i * 128:(i + 1) * 128],
                         w1s[:, c, 0:NW], start=(c == 0), stop=(c == 7))
                for c in range(8):
                    b.mm(ps[gb_][:, 0:NW], yT[:, c, i * 128:(i + 1) * 128],
                         w2s[:, c, 0:NW], start=(c == 0), stop=(c == 7))
                pe_mm[i] = b.n["pe"]

            def stage_b(i):
                gb_, ub_ = (0, 1) if i % 2 == 0 else (5, 6)
                b.wait("scalar", "pe", pe_mm[i])
                b.wait("scalar", "ve")
                b.act(tmps[:, 0:NW], ps[ub_][:, 0:NW], AF.Silu)
                se_silu[i] = b.n["se"]
                b.wait("vector", "se", se_silu[i])
                b.wait("vector", "pe")
                b.tt(hb[:, 0:NW], tmps[:, 0:NW], ps[gb_][:, 0:NW], ALU.mult)
                ve_tt[i] = b.n["ve"]

            def stage_c(i):
                b.wait("tensor", "ve", ve_tt[i])
                if i >= 1:
                    b.wait("tensor", "ve", ve_add[i - 1])
                for c in range(nc3):
                    b.tp(ps[2][:, c * 128:(c + 1) * 128], hb[:, c * 128:(c + 1) * 128], ident)
                b.wait("vector", "pe")
                for c in range(nc3):
                    b.cp(hTe[:, c, :], ps[2][:, c * 128:(c + 1) * 128], drain=False)
                b.wait("tensor", "ve")
                for mh in range(2):
                    for c in range(nc3):
                        b.mm(ps[3 + mh][:, 0:512], hTe[:, c, :],
                             w3s[:, c, mh * 512:(mh + 1) * 512],
                             start=(c == 0), stop=(c == nc3 - 1))
                b.wait("vector", "pe")
                b.stt(comb[:, i, 0:512], ps[3][:, 0:512], C["wfull"][:, i, e:e + 1],
                      comb[:, i, 0:512], ALU.mult, ALU.add)
                b.stt(comb[:, i, 512:1024], ps[4][:, 0:512], C["wfull"][:, i, e:e + 1],
                      comb[:, i, 512:1024], ALU.mult, ALU.add)
                ve_add[i] = b.n["ve"]

            stage_a(0)
            stage_b(0)
            for i in range(1, NT):
                stage_a(i)
                stage_c(i - 1)
                stage_b(i)
            stage_c(NT - 1)
            b.barrier()

    xrd = WK[:, 0:2048].rearrange("p (c f) -> p c f", c=2)
    for i in range(NT):
        if i >= 2:
            b.wait("sync", "ve")
        b.dma("sync", xrd[:, i % 2, :], scr_x3[i * 128:(i + 1) * 128, :], f"x3r{i % 2}")
        b.wait_dma("vector", f"x3r{i % 2}")
        b.tt(stream[:, i, :], stream[:, i, :], xrd[:, i % 2, :], ALU.add)
        b.red(C["stats"][:, i:i + 1], stream[:, i, :], op="sum")
    postln_finish(b, C, stream, WK[:, 2048:3072], WK[:, 3072:4096], eps=1e-5)


def _consts():
    r = np.arange(128)
    cc = {}
    cc["ident"] = np.eye(128, dtype=np.float32)
    cc["mask_causal"] = np.where(r[None, :] > r[:, None], np.float32(MASK_VAL), np.float32(0.0)).astype(np.float32)
    cc["mask_ls"] = np.where(r[None, :] < r[:, None], np.float32(0.0), np.float32(NEG_BIG)).astype(np.float32)
    cc["mask_us"] = np.where(r[None, :] > r[:, None], np.float32(0.0), np.float32(NEG_BIG)).astype(np.float32)
    cc["mask_ui"] = np.where(r[None, :] >= r[:, None], np.float32(0.0), np.float32(NEG_BIG)).astype(np.float32)
    cc["ones_ui"] = np.triu(np.ones((128, 128), dtype=np.float32))
    cc["ones_r"] = np.ones((1, 128), dtype=np.float32)
    cc["ones_col"] = np.ones((128, 1), dtype=np.float32)
    cc["iota_f"] = np.broadcast_to(np.arange(128, dtype=np.float32)[None, :], (128, 128)).copy()
    cc["iota_p"] = np.broadcast_to(np.arange(128, dtype=np.float32)[:, None], (128, 128)).copy()
    cc["e127"] = np.zeros((128, 1), dtype=np.float32); cc["e127"][127, 0] = 1.0
    cc["fp4mids"] = np.broadcast_to(-np.array([0.25, 0.75, 1.25, 1.75, 2.5, 3.5, 5.0, 0.0],
                                              dtype=np.float32), (128, 8)).copy()
    return cc


def _prep_inputs(inp):
    f = np.float32
    def t(a):
        return np.ascontiguousarray(np.asarray(a, dtype=f).T)
    def c(a):
        return np.ascontiguousarray(np.asarray(a, dtype=f))
    def bc(a, n=D):
        return np.ascontiguousarray(np.broadcast_to(np.asarray(a, dtype=f), (128, n)))

    m = {}
    m["l0_wqT"] = t(inp["L0_attn_q_proj_weight"])
    m["l0_wkT"] = t(inp["L0_attn_k_proj_weight"])
    m["l0_wvT"] = t(inp["L0_attn_v_proj_weight"])
    m["l0_woT"] = t(inp["L0_attn_o_proj_weight"])
    m["l0_wg"] = c(inp["L0_ffn_w_gate"])
    m["l0_wu"] = c(inp["L0_ffn_w_up"])
    m["l0_wd"] = c(inp["L0_ffn_w_down"])
    m["l1_wq"] = c(inp["L1_attn_Wq"]); m["l1_wk"] = c(inp["L1_attn_Wk"])
    m["l1_wv"] = c(inp["L1_attn_Wv"]); m["l1_wg"] = c(inp["L1_attn_Wg"])
    m["l1_wo"] = c(inp["L1_attn_Wo"]); m["l1_w2"] = c(inp["L1_attn_W2"])
    m["l1_wab"] = np.ascontiguousarray(
        np.concatenate([np.asarray(inp["L1_attn_Wa"], f), np.asarray(inp["L1_attn_Wb"], f)], axis=1))
    m["l1_convq"] = np.ascontiguousarray(np.asarray(inp["L1_attn_conv_q_w"], f).reshape(8, 128, 4).transpose(1, 0, 2))
    m["l1_convk"] = np.ascontiguousarray(np.asarray(inp["L1_attn_conv_k_w"], f).reshape(8, 128, 4).transpose(1, 0, 2))
    m["l1_convv"] = np.ascontiguousarray(np.asarray(inp["L1_attn_conv_v_w"], f).reshape(16, 128, 4).transpose(1, 0, 2))
    m["l1_dtb"] = bc(inp["L1_attn_dt_bias"], 4)
    m["l1_alog"] = bc(inp["L1_attn_A_log"], 4)
    m["bc_ln1g"] = bc(inp["L1_attn_ln1_g"]); m["bc_ln1b"] = bc(inp["L1_attn_ln1_b"])
    m["bc_ln2g"] = bc(inp["L1_attn_ln2_g"]); m["bc_ln2b"] = bc(inp["L1_attn_ln2_b"])
    m["bc_rmsin"] = bc(inp["L1_attn_rms_in_w"]); m["bc_b2"] = bc(inp["L1_attn_b2"])
    m["bc_moerms"] = bc(inp["L1_ffn_rms_w"]); m["bc_onorm"] = bc(inp["L1_attn_o_norm_w"], 512)
    gwT = np.asarray(inp["L1_ffn_gate_w"], f).T
    m["moe_gwT"] = np.ascontiguousarray(gwT.reshape(8, 128, 8).transpose(1, 0, 2))
    def img_ldT(wT, n0, nw):
        blk = wT[:, n0:n0 + nw]
        return blk.reshape(8, 128, nw).transpose(1, 0, 2).reshape(128, 8 * nw)
    NBW = [384] * 7 + [128]
    w1T = np.asarray(inp["L1_ffn_W1"], f).transpose(0, 2, 1)
    w2T = np.asarray(inp["L1_ffn_W2"], f).transpose(0, 2, 1)
    w3T = np.asarray(inp["L1_ffn_W3"], f).transpose(0, 2, 1)
    im1, im2, im3 = [], [], []
    for e in range(8):
        r1, r2, r3 = [], [], []
        for nb in range(8):
            n0, nw = sum(NBW[:nb]), NBW[nb]
            pad = 384 - nw
            a = img_ldT(w1T[e], n0, nw)
            bb = img_ldT(w2T[e], n0, nw)
            cc = w3T[e][n0:n0 + nw, :].reshape(nw // 128, 128, 1024).transpose(1, 0, 2).reshape(128, -1)
            if pad:
                a = np.concatenate([a, np.zeros((128, 8 * pad), f)], 1)
                bb = np.concatenate([bb, np.zeros((128, 8 * pad), f)], 1)
                cc = np.concatenate([cc, np.zeros((128, (pad // 128) * 1024), f)], 1)
            r1.append(a); r2.append(bb); r3.append(cc)
        im1.append(np.stack(r1)); im2.append(np.stack(r2)); im3.append(np.stack(r3))
    m["moe_w1i"] = np.ascontiguousarray(np.stack(im1))
    m["moe_w2i"] = np.ascontiguousarray(np.stack(im2))
    m["moe_w3i"] = np.ascontiguousarray(np.stack(im3))
    l0p = []
    for p in range(8):
        cols = slice(p * 128, (p + 1) * 128)
        l0p.append(np.concatenate([
            m["l0_wqT"][:, cols].reshape(8, 128, 128).transpose(1, 0, 2).reshape(128, 1024),
            m["l0_wkT"][:, cols].reshape(8, 128, 128).transpose(1, 0, 2).reshape(128, 1024),
            m["l0_wvT"][:, cols].reshape(8, 128, 128).transpose(1, 0, 2).reshape(128, 1024)], 1))
    m["l0_pairs"] = np.ascontiguousarray(np.stack(l0p))
    wd = m["l0_wd"]
    m["l0_wdi"] = np.ascontiguousarray(np.stack(
        [wd[:, nb * 256:(nb + 1) * 256].reshape(8, 128, 256).transpose(1, 0, 2).reshape(128, 2048)
         for nb in range(11)]))
    m.update(_consts())
    return m


def kernel(**inputs):
    return _run(inputs, upto="out")


def _run(inputs, upto="out", s1_limit=None, s2_limit=None, start="x0", s3_limit=None, xin=None):
    x0 = np.asarray(inputs["L0_attn_in_features"], dtype=np.float32)
    shared = _prep_inputs(inputs)
    in_maps = []
    for i in range(8):
        mi = dict(shared)
        mi["x0"] = np.ascontiguousarray(x0[i])
        in_maps.append(mi)
    nc, _ = _build(upto=upto, s1_limit=s1_limit, s2_limit=s2_limit, start=start, s3_limit=s3_limit)
    if xin is not None:
        for i in range(8):
            in_maps[i]["xin"] = np.ascontiguousarray(np.asarray(xin[i % len(xin)], np.float32))
    res = run_bass_kernel_spmd(nc, in_maps, core_ids=list(range(8)))
    return np.stack([np.asarray(res.results[i]["out"]) for i in range(8)], axis=0)
